# version 15
# baseline (speedup 1.0000x reference)
import contextlib
import math
import numpy as np
import ml_dtypes

import concourse.bass as bass
import concourse.mybir as mybir
from concourse.bass_utils import run_bass_kernel_spmd

F32 = mybir.dt.float32
BF16 = mybir.dt.bfloat16
AF = mybir.ActivationFunctionType
OP = mybir.AluOpType

NCORES = 8
CC = 16
TC = 1024
NU = 2
TU = 512
NE = 10
HJ = 8
H = 16
G = 4
CAPA, CAPB = 28, 60
A_SCALE = float(1.0 / (30.0 * math.sqrt(128.0)))
SLOPES = [float(2.0 ** (-(h + 1) * 0.5)) for h in range(H)]
BIG_NEG = -1.0e30

TAP = None


def core_tiles(i):
    return [4 * i + j for j in range(4)] + [60 - 4 * i + j for j in range(4)]


class Ctr:
    def __init__(self, h):
        self.h = h
        self.n = 0

    def inc(self, inst, k=1):
        inst.then_inc(self.h, k)
        self.n += k
        return self.n


def build_nc(tap=None):
    nc = bass.Bass("TRN2", target_bir_lowering=False, debug=False)
    ap = lambda t: t.ap()

    x0_in = ap(nc.dram_tensor("x0", [CC, 128, TC], F32, kind="ExternalInput"))
    wq_in = ap(nc.dram_tensor("wq", [H, 128, CC * 128], F32, kind="ExternalInput"))
    wk_in = ap(nc.dram_tensor("wk", [G, 128, CC * 128], F32, kind="ExternalInput"))
    wv_in = ap(nc.dram_tensor("wv", [CC, 128, 512], F32, kind="ExternalInput"))
    wo_in = ap(nc.dram_tensor("wo", [CC, 4, 128, 512], F32, kind="ExternalInput"))
    bo_in = ap(nc.dram_tensor("bo", [128, 16], F32, kind="ExternalInput"))
    gwse_in = ap(nc.dram_tensor("gwse", [128, CC, 9], F32, kind="ExternalInput"))
    wgu_in = ap(nc.dram_tensor("wgu", [NE, HJ, 128, 4096], F32, kind="ExternalInput"))
    wd_in = ap(nc.dram_tensor("wd", [4, NE, 2, 128, 2048], F32, kind="ExternalInput"))
    qpos_in = ap(nc.dram_tensor("qpos", [128, TC], F32, kind="ExternalInput"))
    gridA_in = ap(nc.dram_tensor("gridA", [128, 64], F32, kind="ExternalInput"))
    gridB_in = ap(nc.dram_tensor("gridB", [128, 64], F32, kind="ExternalInput"))
    gridOA_in = ap(nc.dram_tensor("gridOA", [128, 4], F32, kind="ExternalInput"))
    gridOB_in = ap(nc.dram_tensor("gridOB", [128, 4], F32, kind="ExternalInput"))
    cos_in = ap(nc.dram_tensor("cosl", [128, TC], F32, kind="ExternalInput"))
    sin_in = ap(nc.dram_tensor("sinl", [128, TC], F32, kind="ExternalInput"))
    masks_in = ap(nc.dram_tensor("masks", [4, 128, 512], BF16, kind="ExternalInput"))
    ident_in = ap(nc.dram_tensor("ident", [128, 128], F32, kind="ExternalInput"))
    pt_in = ap(nc.dram_tensor("ptmat", [128, 128], F32, kind="ExternalInput"))
    esel_in = ap(nc.dram_tensor("esel", [NE, NE * 128], BF16, kind="ExternalInput"))
    onesm_in = ap(nc.dram_tensor("onesm", [1, 128], F32, kind="ExternalInput"))
    onescb_in = ap(nc.dram_tensor("onescb", [128, 1], BF16, kind="ExternalInput"))
    neg30_in = ap(nc.dram_tensor("neg30", [128, 1], F32, kind="ExternalInput"))

    y_out = ap(nc.dram_tensor("y", [CC, 128, TC], F32, kind="ExternalOutput"))

    x1_d = ap(nc.dram_tensor("x1d", [CC, 128, TC], F32, kind="Internal"))
    x2_d = ap(nc.dram_tensor("x2d", [CC, 128, TC], F32, kind="Internal"))
    wgubf_d = ap(nc.dram_tensor("wgubf", [NE, HJ, 128, 4096], BF16, kind="Internal"))
    wdbf_d = ap(nc.dram_tensor("wdbf", [4, NE, 2, 128, 2048], BF16, kind="Internal"))
    ksend_d = ap(nc.dram_tensor("ksend", [G * 128, TC], BF16, kind="Internal"))
    vsend_d = ap(nc.dram_tensor("vsend", [TC, 512], BF16, kind="Internal"))
    krecv_d = ap(nc.dram_tensor("krecv", [NCORES, G * 128, TC], BF16, kind="Internal"))
    vrecv_d = ap(nc.dram_tensor("vrecv", [NCORES, TC, 512], BF16, kind="Internal"))

    ctx = contextlib.ExitStack()
    E = ctx.enter_context

    xhat = E(nc.sbuf_tensor("xhat", [128, CC * TC], BF16))
    arena = E(nc.sbuf_tensor("arena", [128, 40960], BF16))
    AQ, AKG, AVG, AKO, AVO = 0, 16384, 24576, 32768, 36864

    scr32 = E(nc.sbuf_tensor("scr32", [128, 5120], F32))
    QRAW, T1C, T2C, COSC, SINC = 0, 512, 1024, 2048, 3072
    SPQ, RBX, TPAIR = 2048, 3072, 4096
    scrbf = E(nc.sbuf_tensor("scrbf", [128, 4096], BF16))
    o32 = E(nc.sbuf_tensor("o32", [128, 4096], F32))

    wgusb = E(nc.sbuf_tensor("wgusb", [128, 8192], BF16))
    wdsb = E(nc.sbuf_tensor("wdsb", [128, 4096], BF16))
    wqsb = E(nc.sbuf_tensor("wqsb", [128, 4096], BF16))
    gwsesb = E(nc.sbuf_tensor("gwsesb", [128, 144], BF16))

    masks_sb = E(nc.sbuf_tensor("masks_sb", [128, 2048], BF16))
    qpos_sb = E(nc.sbuf_tensor("qpos_sb", [128, TC], F32))
    grids_sb = E(nc.sbuf_tensor("grids_sb", [128, 136], F32))
    kcols_sb = E(nc.sbuf_tensor("kcols_sb", [128, 136], F32))
    ident_sb = E(nc.sbuf_tensor("ident_sb", [128, 128], F32))
    pt_sb = E(nc.sbuf_tensor("pt_sb", [128, 128], F32))
    esel_sb = E(nc.sbuf_tensor("esel_sb", [NE, NE * 128], BF16))
    onesm_sb = E(nc.sbuf_tensor("onesm_sb", [1, 128], F32))
    onescb_sb = E(nc.sbuf_tensor("onescb_sb", [128, 1], BF16))
    neg30_sb = E(nc.sbuf_tensor("neg30_sb", [128, 1], F32))
    bo_sb = E(nc.sbuf_tensor("bo_sb", [128, 16], F32))
    combfm = E(nc.sbuf_tensor("combfm", [NE, TC], BF16))
    comb8 = E(nc.sbuf_tensor("comb8", [128, 16], F32))
    gsc = E(nc.sbuf_tensor("gsc", [128, 48], F32))
    cols = E(nc.sbuf_tensor("cols", [128, 16], F32))


    P = [E(nc.psum_tensor(f"P{i}", [128, 512], F32)) for i in range(8)]

    sW = Ctr(E(nc.semaphore("sW")))
    sX = Ctr(E(nc.semaphore("sX")))
    sT = Ctr(E(nc.semaphore("sT")))
    sV = Ctr(E(nc.semaphore("sV")))
    sS = Ctr(E(nc.semaphore("sS")))
    sVV = Ctr(E(nc.semaphore("sVV")))
    sCK = Ctr(E(nc.semaphore("sCK")))
    sCV = Ctr(E(nc.semaphore("sCV")))

    block = E(nc.Block())

    GLOB = {"wbB": 0}

    def B_(name, fn):
        getattr(block, name)(fn)

    def vsync(v):
        inst = v.engine_nop()
        inst.then_inc(sVV.h, 1)
        sVV.n += 1
        v.wait_ge(sVV.h, sVV.n)

    def _consts(s):
        for dst, src in (
            (masks_sb[:, :].rearrange("p (m c) -> p m c", m=4), masks_in.rearrange("m p c -> p m c")),
            (qpos_sb[:, :], qpos_in),
            (grids_sb[:, 0:64], gridA_in),
            (grids_sb[:, 64:128], gridB_in),
            (grids_sb[:, 128:132], gridOA_in),
            (grids_sb[:, 132:136], gridOB_in),
            (ident_sb[:, :], ident_in),
            (pt_sb[:, :], pt_in),
            (esel_sb[:, :], esel_in),
            (onesm_sb[:, :], onesm_in),
            (onescb_sb[:, :], onescb_in),
            (neg30_sb[:, :], neg30_in),
            (bo_sb[:, :], bo_in),
        ):
            s.dma_start(out=dst, in_=src).then_inc(sX.h, 16)
            sX.n += 16

    B_("sync", _consts)
    consts_done = sX.n

    def _gwse(g):
        g.dma_start(out=gwsesb[:, :].rearrange("p (c n) -> p c n", n=9),
                    in_=gwse_in).then_inc(sW.h, 16)
        sW.n += 16

    B_("gpsimd", _gwse)
    gwse_done = sW.n

    def emit_ln(src, prev=None):
        ln_sv, ln_ss, ln_st, ln_sx = sV.n, sS.n, sT.n, sX.n
        src_ready = prev.get("ready") if prev else None
        xhat_free = prev.get("xhat_free", 0) if prev else 0
        scrbf_free = prev.get("scrbf_free", 0) if prev else 0
        sS_base = sS.n
        p1_sc = [sS_base + cc + 1 for cc in range(CC)]
        sT_base = sT.n
        p1_t = [sT_base + cc + 1 for cc in range(CC)]
        dma1 = []

        def _p1_sync(s):
            s.wait_ge(sS.h, ln_ss)
            s.wait_ge(sV.h, ln_sv)
            if src_ready is None:
                s.wait_ge(sT.h, ln_st)
                s.wait_ge(sX.h, ln_sx)
            for cc in range(CC):
                if src_ready is not None:
                    s.wait_ge(sX.h, src_ready[cc])
                if cc >= 2:
                    s.wait_ge(sS.h, p1_sc[cc - 2])
                s.dma_start(out=scr32[:, (cc % 2) * 1024:(cc % 2 + 1) * 1024],
                            in_=src[cc]).then_inc(sX.h, 16)
                sX.n += 16
                dma1.append(sX.n)

        B_("sync", _p1_sync)

        def _p1_scalar(sc):
            if xhat_free:
                sc.wait_ge(sT.h, xhat_free)
            if scrbf_free:
                sc.wait_ge(sT.h, scrbf_free)
            for cc in range(CC):
                par = cc % 2
                sc.wait_ge(sX.h, dma1[cc])
                if cc >= 2:
                    sc.wait_ge(sT.h, p1_t[cc - 2])
                sc.activation(xhat[:, cc * TC:(cc + 1) * TC],
                              scr32[:, par * 1024:(par + 1) * 1024], AF.Copy)
                inst = sc.activation(scrbf[:, par * 1024:(par + 1) * 1024],
                                     scr32[:, par * 1024:(par + 1) * 1024],
                                     AF.Square)
                inst.then_inc(sS.h, 1)
                sS.n += 1

        B_("scalar", _p1_scalar)
        assert sS.n == p1_sc[-1]

        def _p1_tensor(t):
            t.wait_ge(sX.h, consts_done)
            t.wait_ge(sV.h, ln_sv)
            for cc in range(CC):
                par = cc % 2
                t.wait_ge(sS.h, p1_sc[cc])
                st, sp = cc == 0, cc == CC - 1
                inst = None
                for u in range(NU):
                    t.matmul(P[u][0:1, :], onescb_sb[:, 0:1],
                             xhat[:, cc * TC + u * TU: cc * TC + (u + 1) * TU],
                             start=st, stop=sp)
                    inst = t.matmul(P[2 + u][0:1, :], onescb_sb[:, 0:1],
                                    scrbf[:, par * 1024 + u * TU: par * 1024 + (u + 1) * TU],
                                    start=st, stop=sp)
                inst.then_inc(sT.h, 1)
                sT.n += 1

        B_("tensor", _p1_tensor)
        assert sT.n == p1_t[-1]
        stats_done = sT.n

        TMPA, TMPB = 3072 - 2048, 3584 - 2048
        bcast_t = 0
        for u in range(NU):
            def _r_v1(v, u=u):
                if u == 0:
                    v.wait_ge(sT.h, stats_done)
                inst = v.tensor_scalar_mul(scr32[0:1, 2048 + u * 512:2048 + (u + 1) * 512],
                                           P[u][0:1, :], 1.0 / 2048.0)
                inst.then_inc(sV.h, 1)
                sV.n += 1

            B_("vector", _r_v1)
            mean_v = sV.n

            def _r_s1(sc, u=u, bcast_t=bcast_t, mean_v=mean_v):
                sc.wait_ge(sV.h, mean_v)
                if bcast_t:
                    sc.wait_ge(sT.h, bcast_t)
                inst = sc.activation(scr32[0:1, 2048 + TMPA:2048 + TMPA + 512],
                                     scr32[0:1, 2048 + u * 512:2048 + (u + 1) * 512], AF.Square)
                inst.then_inc(sS.h, 1)
                sS.n += 1

            B_("scalar", _r_s1)
            msq_s = sS.n

            def _r_v2(v, u=u, msq_s=msq_s):
                v.wait_ge(sS.h, msq_s)
                v.scalar_tensor_tensor(scr32[0:1, 2048 + TMPB:2048 + TMPB + 512],
                                       P[2 + u][0:1, :], 1.0 / 2048.0,
                                       scr32[0:1, 2048 + TMPA:2048 + TMPA + 512],
                                       OP.mult, OP.subtract)
                vsync(v)
                inst = v.tensor_scalar_add(scr32[0:1, 2048 + TMPB:2048 + TMPB + 512],
                                           scr32[0:1, 2048 + TMPB:2048 + TMPB + 512], 1.0e-5)
                inst.then_inc(sV.h, 1)
                sV.n += 1

            B_("vector", _r_v2)
            var_v = sV.n

            def _r_s2(sc, var_v=var_v):
                sc.wait_ge(sV.h, var_v)
                inst = sc.activation(scr32[0:1, 2048 + TMPA:2048 + TMPA + 512],
                                     scr32[0:1, 2048 + TMPB:2048 + TMPB + 512], AF.Sqrt)
                inst.then_inc(sS.h, 1)
                sS.n += 1

            B_("scalar", _r_s2)
            sd_s = sS.n

            def _r_v3(v, sd_s=sd_s):
                v.wait_ge(sS.h, sd_s)
                inst = v.reciprocal(scr32[0:1, 2048 + TMPB:2048 + TMPB + 512],
                                    scr32[0:1, 2048 + TMPA:2048 + TMPA + 512])
                inst.then_inc(sV.h, 1)
                sV.n += 1

            B_("vector", _r_v3)
            rstd_v = sV.n

            def _bc_t(t, u=u, rstd_v=rstd_v):
                t.wait_ge(sV.h, rstd_v)
                t.matmul(P[4 + u][:, :], onesm_sb[0:1, :],
                         scr32[0:1, 2048 + u * 512:2048 + (u + 1) * 512], start=True, stop=True)
                inst = t.matmul(P[6 + u][:, :], onesm_sb[0:1, :],
                                scr32[0:1, 2048 + TMPB:2048 + TMPB + 512], start=True, stop=True)
                inst.then_inc(sT.h, 1)
                sT.n += 1

            B_("tensor", _bc_t)
            bcast_t = sT.n

        sV_base = sV.n
        p2_v = [sV_base + cc + 1 for cc in range(CC)]
        dma2 = []

        def _p2_sync(s):
            for cc in range(CC):
                if src_ready is not None:
                    s.wait_ge(sX.h, src_ready[cc])
                if cc >= 2:
                    s.wait_ge(sV.h, p2_v[cc - 2])
                s.dma_start(out=scr32[:, (cc % 2) * 1024:(cc % 2 + 1) * 1024],
                            in_=src[cc]).then_inc(sX.h, 16)
                sX.n += 16
                dma2.append(sX.n)

        B_("sync", _p2_sync)

        def _p2_v(v):
            v.wait_ge(sT.h, bcast_t)
            v.wait_ge(sS.h, ln_ss)
            for cc in range(CC):
                par = cc % 2
                v.wait_ge(sX.h, dma2[cc])
                inst = None
                for u in range(NU):
                    v.tensor_tensor(scr32[:, 4096 + u * 512:4096 + (u + 1) * 512],
                                    scr32[:, par * 1024 + u * TU: par * 1024 + (u + 1) * TU],
                                    P[4 + u][:, :], OP.subtract)
                    inst = v.tensor_tensor(
                        xhat[:, cc * TC + u * TU: cc * TC + (u + 1) * TU],
                        scr32[:, 4096 + u * 512:4096 + (u + 1) * 512],
                        P[6 + u][:, :], OP.mult)
                inst.then_inc(sV.h, 1)
                sV.n += 1

        B_("vector", _p2_v)
        assert sV.n == p2_v[-1]
        return sV.n

    def emit_ffn(src, dst, first_call, prev=None):
        ffn_t0 = sT.n
        wb_gate = 0 if first_call else GLOB["wbB"]
        ln_done = emit_ln(src, prev)
        ready = [0] * CC

        prev_gate_v = 0
        for ts in range(8):
            def _g_t1(t, ts=ts, prev_gate_v=prev_gate_v):
                if ts == 0:
                    t.wait_ge(sV.h, ln_done)
                    t.wait_ge(sW.h, gwse_done)
                else:
                    t.wait_ge(sV.h, prev_gate_v)
                inst = None
                for cc in range(CC):
                    inst = t.matmul(
                        P[0][:, 0:9],
                        xhat[:, cc * TC + ts * 128: cc * TC + (ts + 1) * 128],
                        gwsesb[:, cc * 9:(cc + 1) * 9],
                        start=(cc == 0), stop=(cc == CC - 1))
                inst.then_inc(sT.h, 1)
                sT.n += 1

            B_("tensor", _g_t1)
            g_t = sT.n

            def _g_v1(v, g_t=g_t):
                v.wait_ge(sT.h, g_t)
                v.reduce_max(cols[:, 0:1], P[0][:, 0:8], axis=mybir.AxisListType.X)
                vsync(v)
                v.tensor_scalar_mul(cols[:, 1:2], cols[:, 0:1], -1.0)
                vsync(v)
                v.tensor_single_scalar(gsc[:, 0:8], P[0][:, 0:8], cols[:, 0:1],
                                       OP.is_equal)
                vsync(v)
                v.scalar_tensor_tensor(gsc[:, 8:16], gsc[:, 0:8], BIG_NEG,
                                       P[0][:, 0:8], OP.mult, OP.add)
                vsync(v)
                v.reduce_max(cols[:, 2:3], gsc[:, 8:16], axis=mybir.AxisListType.X)
                vsync(v)
                inst = v.tensor_tensor(cols[:, 3:4], cols[:, 2:3], cols[:, 0:1],
                                       OP.subtract)
                inst.then_inc(sV.h, 1)
                sV.n += 1

            B_("vector", _g_v1)
            g_v1 = sV.n

            def _g_s1(sc, g_v1=g_v1):
                sc.wait_ge(sV.h, g_v1)
                sc.activation(cols[:, 4:5], cols[:, 3:4], AF.Exp)
                sc.activation(gsc[:, 16:24], P[0][:, 0:8], AF.Exp, bias=cols[:, 1:2])
                inst = sc.activation(cols[:, 5:6], P[0][:, 8:9], AF.Exp, scale=-1.0)
                inst.then_inc(sS.h, 1)
                sS.n += 1

            B_("scalar", _g_s1)
            g_s1 = sS.n

            def _g_v2(v, g_s1=g_s1):
                v.wait_ge(sS.h, g_s1)
                v.tensor_scalar_add(cols[:, 6:7], cols[:, 4:5], 1.0)
                v.tensor_single_scalar(gsc[:, 24:32], P[0][:, 0:8], cols[:, 2:3],
                                       OP.is_ge)
                vsync(v)
                v.reciprocal(cols[:, 7:8], cols[:, 6:7])
                v.tensor_tensor(gsc[:, 32:40], gsc[:, 16:24], gsc[:, 24:32], OP.mult)
                vsync(v)
                v.tensor_scalar(comb8[:, 0:8], gsc[:, 32:40], cols[:, 7:8], 0.5,
                                OP.mult, OP.mult)
                v.tensor_scalar_add(cols[:, 8:9], cols[:, 5:6], 1.0)
                vsync(v)
                v.reciprocal(cols[:, 9:10], cols[:, 8:9])
                vsync(v)
                v.tensor_scalar_mul(comb8[:, 8:9], cols[:, 9:10], 0.5)
                inst = v.tensor_scalar_mul(comb8[:, 9:10], cols[:, 9:10], 0.5)
                inst.then_inc(sV.h, 1)
                sV.n += 1

            B_("vector", _g_v2)
            g_v2 = sV.n

            def _g_t2(t, g_v2=g_v2):
                t.wait_ge(sV.h, g_v2)
                inst = t.transpose(P[1][0:NE, 0:128], comb8[:, 0:NE], ident_sb[:, :])
                inst.then_inc(sT.h, 1)
                sT.n += 1

            B_("tensor", _g_t2)
            g_t2 = sT.n

            def _g_v3(v, g_t2=g_t2, ts=ts):
                v.wait_ge(sT.h, g_t2)
                inst = v.tensor_copy(combfm[0:NE, ts * 128:(ts + 1) * 128],
                                     P[1][0:NE, 0:128])
                inst.then_inc(sV.h, 1)
                sV.n += 1

            B_("vector", _g_v3)
            prev_gate_v = sV.n
        comb_done = sV.n
        if GLOB.get("gates_only") and (first_call != bool(GLOB.get("go2"))):
            def _dbgc(g):
                g.wait_ge(sV.h, comb_done)
                g.dma_start(out=y_out[0][0:NE, :],
                            in_=combfm[0:NE, :]).then_inc(sX.h, 16)
                sX.n += 16
                for cx in range(8):
                    g.dma_start(out=y_out[1 + cx],
                                in_=xhat[:, cx * TC:(cx + 1) * TC]).then_inc(sX.h, 16)
                    sX.n += 16
            B_("gpsimd", _dbgc)
            return sX.n

        wbA = 0
        wbB = 0
        wg_free = {0: ffn_t0, 1: ffn_t0}
        wd_free = {0: ffn_t0, 1: ffn_t0}
        xo_done = {0: 0, 1: 0}
        out_done = {0: 0, 1: 0}
        for u in range(NU):
            a_entry_sv = sV.n
            bank_free = {0: None, 1: None, 2: None, 3: None, 4: None, 5: None}
            p_prev2 = {}
            combbc_done = [0, 0]
            wg_done = {}
            s1_done = {}
            wb_done = {}
            for pe in range(NE):
                def _a_tbc(t, pe=pe, u=u, a_entry_sv=a_entry_sv):
                    if pe == 0:
                        t.wait_ge(sV.h, a_entry_sv)
                    if combbc_done[pe % 2]:
                        t.wait_ge(sV.h, combbc_done[pe % 2])
                    inst = t.matmul(P[6 + pe % 2][:, :],
                                    esel_sb[0:NE, pe * 128:(pe + 1) * 128],
                                    combfm[0:NE, u * TU:(u + 1) * TU],
                                    start=True, stop=True)
                    inst.then_inc(sT.h, 1)
                    sT.n += 1

                B_("tensor", _a_tbc)
                bc_t = sT.n

                def _a_vbc(v, bc_t=bc_t, pe=pe):
                    v.wait_ge(sT.h, bc_t)
                    inst = v.tensor_copy(
                        scr32[:, 4096 + (pe % 2) * 512: 4096 + (pe % 2 + 1) * 512],
                        P[6 + pe % 2][:, :])
                    inst.then_inc(sV.h, 1)
                    sV.n += 1

                B_("vector", _a_vbc)
                combbc_done[pe % 2] = sV.n

                for j in range(HJ):
                    wpar = j % 2

                    def _a_gw(g, pe=pe, j=j, wpar=wpar, u=u):
                        if not first_call and u == 0 and pe == 0 and j == 0:
                            g.wait_ge(sX.h, wb_gate)
                        if first_call and u == 0:
                            prev = wb_done.get((pe * HJ + j) - 2)
                            if prev:
                                g.wait_ge(sX.h, prev)
                        if wg_free[wpar]:
                            g.wait_ge(sT.h, wg_free[wpar])
                        if first_call and u == 0:
                            g.dma_start(out=wgusb[:, wpar * 4096:(wpar + 1) * 4096],
                                        in_=wgu_in[pe, j]).then_inc(sW.h, 16)
                        else:
                            if first_call and u == 1 and pe == 0 and j == 0:
                                g.wait_ge(sX.h, wbA)
                            g.dma_start(out=wgusb[:, wpar * 4096:(wpar + 1) * 4096],
                                        in_=wgubf_d[pe, j]).then_inc(sW.h, 16)
                        sW.n += 16
                        wg_done[(pe, j)] = sW.n

                    B_("gpsimd", _a_gw)

                    if first_call and u == 0:
                        def _a_wb(s, pe=pe, j=j, wpar=wpar):
                            s.wait_ge(sW.h, wg_done[(pe, j)])
                            s.dma_start(out=wgubf_d[pe, j],
                                        in_=wgusb[:, wpar * 4096:(wpar + 1) * 4096]
                                        ).then_inc(sX.h, 16)
                            sX.n += 16
                            wb_done[pe * HJ + j] = sX.n

                        B_("sync", _a_wb)

                    def _a_t(t, pe=pe, j=j, wpar=wpar, u=u):
                        jp = (pe * HJ + j) % 3
                        t.wait_ge(sW.h, wg_done[(pe, j)])
                        if bank_free[jp * 2] is not None:
                            sem, cnt = bank_free[jp * 2]
                            t.wait_ge(sem.h, cnt)
                        inst = None
                        for cc in range(CC):
                            inst = t.matmul(
                                P[jp * 2][:, :],
                                wgusb[:, wpar * 4096 + cc * 128: wpar * 4096 + (cc + 1) * 128],
                                xhat[:, cc * TC + u * TU: cc * TC + (u + 1) * TU],
                                start=(cc == 0), stop=(cc == CC - 1))
                        inst.then_inc(sT.h, 1)
                        sT.n += 1
                        if bank_free[jp * 2 + 1] is not None:
                            sem, cnt = bank_free[jp * 2 + 1]
                            t.wait_ge(sem.h, cnt)
                        for cc in range(CC):
                            inst = t.matmul(
                                P[jp * 2 + 1][:, :],
                                wgusb[:, wpar * 4096 + 2048 + cc * 128: wpar * 4096 + 2048 + (cc + 1) * 128],
                                xhat[:, cc * TC + u * TU: cc * TC + (u + 1) * TU],
                                start=(cc == 0), stop=(cc == CC - 1))
                        inst.then_inc(sT.h, 1)
                        sT.n += 1
                        wg_free[wpar] = sT.n

                    B_("tensor", _a_t)
                    h1_t = sT.n - 1
                    h3_t = sT.n

                    def _a_s(sc, h1_t=h1_t, pe=pe, j=j):
                        jp = (pe * HJ + j) % 3
                        sc.wait_ge(sT.h, h1_t)
                        pv = p_prev2.get((pe * HJ + j) % 2)
                        if pv:
                            sc.wait_ge(sV.h, pv)
                        inst = sc.activation(
                            scr32[:, 2048 + (jp % 2) * 512: 2048 + (jp % 2 + 1) * 512],
                            P[jp * 2][:, :], AF.Silu)
                        inst.then_inc(sS.h, 1)
                        sS.n += 1

                    B_("scalar", _a_s)
                    s1_done[(pe, j)] = sS.n

                    def _a_v(v, pe=pe, j=j, h3_t=h3_t):
                        jp = (pe * HJ + j) % 3
                        v.wait_ge(sT.h, h3_t)
                        v.wait_ge(sV.h, combbc_done[pe % 2])
                        v.tensor_tensor(
                            scr32[:, 3072 + (jp % 2) * 512: 3072 + (jp % 2 + 1) * 512],
                            P[jp * 2 + 1][:, :],
                            scr32[:, 4096 + (pe % 2) * 512: 4096 + (pe % 2 + 1) * 512],
                            OP.mult)
                        v.wait_ge(sS.h, s1_done[(pe, j)])
                        inst = v.tensor_tensor(
                            arena[:, pe * 4096 + j * 512: pe * 4096 + (j + 1) * 512],
                            scr32[:, 2048 + (jp % 2) * 512: 2048 + (jp % 2 + 1) * 512],
                            scr32[:, 3072 + (jp % 2) * 512: 3072 + (jp % 2 + 1) * 512],
                            OP.mult)
                        inst.then_inc(sV.h, 1)
                        sV.n += 1

                    B_("vector", _a_v)
                    bank_free[((pe * HJ + j) % 3) * 2] = (sS, s1_done[(pe, j)])
                    bank_free[((pe * HJ + j) % 3) * 2 + 1] = (sV, sV.n)
                    p_prev2[(pe * HJ + j) % 2] = sV.n
            a_done = sV.n
            if u == NU - 1:
                xhat_free_out = sT.n
            if first_call and u == 0:
                wbA = sX.n

            wd_done = {}
            wdwb_done = {}
            evac_done = {0: 0, 1: 0}
            for cg in range(4):
                cp = cg % 2

                def _b_xo(s, cg=cg, cp=cp, u=u):
                    if out_done[cp]:
                        s.wait_ge(sX.h, out_done[cp])
                    else:
                        s.wait_ge(sV.h, ln_done)
                    s.dma_start(
                        out=o32[:, cp * 2048:(cp + 1) * 2048].rearrange(
                            "p (i t) -> p i t", i=4),
                        in_=src.rearrange("c p t -> p c t")[
                            :, cg * 4:(cg + 1) * 4, u * TU:(u + 1) * TU]
                    ).then_inc(sX.h, 16)
                    sX.n += 16
                    xo_done[cp] = sX.n

                B_("sync", _b_xo)

                for pe in range(NE):
                    for j in range(HJ):
                        wpar = (pe * HJ + j) % 2

                        def _b_gw(g, cg=cg, pe=pe, j=j, u=u):
                            if j % 4 != 0:
                                return
                            jg = j // 4
                            wpar = (cg * NE * 2 + pe * 2 + jg) % 2
                            if not first_call and u == 0 and cg == 0 and pe == 0 and j == 0:
                                g.wait_ge(sX.h, wb_gate)
                            if first_call and u == 0:
                                prev = wdwb_done.get((cg * NE + pe) * 2 + jg - 2)
                                if prev:
                                    g.wait_ge(sX.h, prev)
                            if wd_free[wpar]:
                                g.wait_ge(sT.h, wd_free[wpar])
                            if first_call and u == 0:
                                g.dma_start(out=wdsb[:, wpar * 2048:(wpar + 1) * 2048],
                                            in_=wd_in[cg, pe, jg]).then_inc(sW.h, 16)
                            else:
                                if first_call and u == 1 and cg == 0 and pe == 0 and j == 0:
                                    g.wait_ge(sX.h, wbB)
                                g.dma_start(out=wdsb[:, wpar * 2048:(wpar + 1) * 2048],
                                            in_=wdbf_d[cg, pe, jg]).then_inc(sW.h, 16)
                            sW.n += 16
                            wd_done[(cg, pe, jg)] = sW.n

                        B_("gpsimd", _b_gw)

                        if first_call and u == 0:
                            def _b_wb(s, cg=cg, pe=pe, j=j):
                                if j % 4 != 0:
                                    return
                                jg = j // 4
                                wpar = (cg * NE * 2 + pe * 2 + jg) % 2
                                s.wait_ge(sW.h, wd_done[(cg, pe, jg)])
                                s.dma_start(out=wdbf_d[cg, pe, jg],
                                            in_=wdsb[:, wpar * 2048:(wpar + 1) * 2048]
                                            ).then_inc(sX.h, 16)
                                sX.n += 16
                                wdwb_done[(cg * NE + pe) * 2 + jg] = sX.n

                            B_("sync", _b_wb)

                        def _b_t(t, cg=cg, pe=pe, j=j, cp=cp):
                            jg = j // 4
                            wpar = (cg * NE * 2 + pe * 2 + jg) % 2
                            t.wait_ge(sW.h, wd_done[(cg, pe, jg)])
                            if pe == 0 and j == 0:
                                t.wait_ge(sV.h, a_done)
                                if evac_done[cp]:
                                    t.wait_ge(sV.h, evac_done[cp])
                            first = pe == 0 and j == 0
                            last = pe == NE - 1 and j == HJ - 1
                            inst = None
                            for i in range(4):
                                inst = t.matmul(
                                    P[cp * 4 + i][:, :],
                                    wdsb[:, wpar * 2048 + (j % 4) * 512 + i * 128:
                                         wpar * 2048 + (j % 4) * 512 + (i + 1) * 128],
                                    arena[:, pe * 4096 + j * 512: pe * 4096 + (j + 1) * 512],
                                    start=first, stop=last)
                            inst.then_inc(sT.h, 1)
                            sT.n += 1
                            if j % 4 == 3:
                                wd_free[wpar] = sT.n

                        B_("tensor", _b_t)
                b_t = sT.n

                def _b_v(v, cp=cp, b_t=b_t):
                    v.wait_ge(sT.h, b_t)
                    v.wait_ge(sX.h, xo_done[cp])
                    inst = None
                    for i in range(4):
                        inst = v.tensor_tensor(
                            o32[:, cp * 2048 + i * 512: cp * 2048 + (i + 1) * 512],
                            P[cp * 4 + i][:, :],
                            o32[:, cp * 2048 + i * 512: cp * 2048 + (i + 1) * 512],
                            OP.add)
                    inst.then_inc(sV.h, 1)
                    sV.n += 1
                    evac_done[cp] = sV.n

                B_("vector", _b_v)

                def _b_out(s, cg=cg, cp=cp, u=u):
                    s.wait_ge(sV.h, evac_done[cp])
                    s.dma_start(
                        out=dst.rearrange("c p t -> p c t")[
                            :, cg * 4:(cg + 1) * 4, u * TU:(u + 1) * TU],
                        in_=o32[:, cp * 2048:(cp + 1) * 2048].rearrange(
                            "p (i t) -> p i t", i=4)).then_inc(sX.h, 16)
                    sX.n += 16
                    out_done[cp] = sX.n

                B_("sync", _b_out)
                if u == NU - 1:
                    for i in range(4):
                        ready[cg * 4 + i] = sX.n
            if first_call and u == 0:
                wbB = sX.n
                GLOB["wbB"] = sX.n
        return {"last": sX.n, "ready": ready, "xhat_free": xhat_free_out,
                "scrbf_free": 0}

    def emit_attn(src, dst, prev=None):
        ln_done = emit_ln(src, prev)
        ready = [0] * CC

        cs_sv, cs_ss = sV.n, sS.n

        def _cs(s):
            s.wait_ge(sV.h, cs_sv)
            s.wait_ge(sS.h, cs_ss)
            s.dma_start(out=scr32[:, COSC:COSC + 1024], in_=cos_in).then_inc(sX.h, 16)
            sX.n += 16
            s.dma_start(out=scr32[:, SINC:SINC + 1024], in_=sin_in).then_inc(sX.h, 16)
            sX.n += 16

        B_("sync", _cs)
        cs_done = sX.n

        st = {"pq_free": {0: 0, 1: 0}, "qraw_free": 0, "pr_free": 0}

        def proj_rope(n_heads, w_in, dst_col):
            wq_done = {}
            wfree = {0: sT.n, 1: sT.n}
            for h in range(n_heads):
                wpar = h % 2

                def _pw(g, h=h, wpar=wpar):
                    if wfree[wpar]:
                        g.wait_ge(sT.h, wfree[wpar])
                    g.dma_start(out=wqsb[:, wpar * 2048:(wpar + 1) * 2048],
                                in_=w_in[h]).then_inc(sW.h, 16)
                    sW.n += 16
                    wq_done[h] = sW.n

                B_("gpsimd", _pw)

                for u in range(NU):
                    def _pt(t, h=h, u=u, wpar=wpar):
                        if h == 0 and u == 0:
                            t.wait_ge(sV.h, ln_done)
                        t.wait_ge(sW.h, wq_done[h])
                        if st["pq_free"][u]:
                            t.wait_ge(sS.h, st["pq_free"][u])
                        inst = None
                        for cc in range(CC):
                            inst = t.matmul(
                                P[u][:, :],
                                wqsb[:, wpar * 2048 + cc * 128: wpar * 2048 + (cc + 1) * 128],
                                xhat[:, cc * TC + u * TU: cc * TC + (u + 1) * TU],
                                start=(cc == 0), stop=(cc == CC - 1))
                        inst.then_inc(sT.h, 1)
                        sT.n += 1

                    B_("tensor", _pt)
                    mm_t = sT.n
                    if u == NU - 1:
                        wfree[wpar] = sT.n

                    def _ps(sc, mm_t=mm_t, u=u):
                        sc.wait_ge(sT.h, mm_t)
                        if st["qraw_free"]:
                            sc.wait_ge(sT.h, st["qraw_free"])
                        if st["pr_free"]:
                            sc.wait_ge(sV.h, st["pr_free"])
                        inst = sc.activation(scr32[:, QRAW:QRAW + 512], P[u][:, :], AF.Copy)
                        inst.then_inc(sS.h, 1)
                        sS.n += 1
                        st["pq_free"][u] = sS.n

                    B_("scalar", _ps)
                    cp_s = sS.n

                    def _pt2(t, cp_s=cp_s):
                        t.wait_ge(sS.h, cp_s)
                        if st["pr_free"]:
                            t.wait_ge(sV.h, st["pr_free"])
                        inst = t.matmul(P[2][:, :], pt_sb[:, :], scr32[:, QRAW:QRAW + 512],
                                        start=True, stop=True)
                        inst.then_inc(sT.h, 1)
                        sT.n += 1
                        st["qraw_free"] = sT.n

                    B_("tensor", _pt2)
                    rot_t = sT.n

                    def _pv(v, h=h, u=u, rot_t=rot_t):
                        v.wait_ge(sX.h, cs_done)
                        v.wait_ge(sT.h, rot_t)
                        v.tensor_tensor(scr32[:, T1C:T1C + 512], scr32[:, QRAW:QRAW + 512],
                                        scr32[:, COSC + u * TU:COSC + (u + 1) * TU], OP.mult)
                        v.tensor_tensor(scr32[:, T2C:T2C + 512], P[2][:, :],
                                        scr32[:, SINC + u * TU:SINC + (u + 1) * TU],
                                        OP.mult)
                        vsync(v)
                        inst = v.tensor_tensor(
                            arena[:, dst_col + h * TC + u * TU:
                                  dst_col + h * TC + (u + 1) * TU],
                            scr32[:, T1C:T1C + 512], scr32[:, T2C:T2C + 512], OP.add)
                        inst.then_inc(sV.h, 1)
                        sV.n += 1
                        st["pr_free"] = sV.n

                    B_("vector", _pv)
            return sV.n

        q_done = proj_rope(H, wq_in, AQ)
        k_done = proj_rope(G, wk_in, AKO)
        if GLOB.get("tap_qrope"):
            def _dbgq(g):
                g.wait_ge(sV.h, q_done)
                for hx in range(16):
                    g.dma_start(out=y_out[hx],
                                in_=arena[:, AQ + hx * TC:AQ + (hx + 1) * TC]
                                ).then_inc(sX.h, 16)
                    sX.n += 16
            B_("gpsimd", _dbgq)
            return sX.n

        wv_done = {}
        wv_free = {0: sT.n, 1: sT.n, 2: sT.n, 3: sT.n}
        pv_free = {0: 0, 1: 0}
        for ts in range(8):
            for q2 in range(8):
                qp = q2 % 4

                def _vw(g, ts=ts, q2=q2, qp=qp):
                    if wv_free[qp]:
                        g.wait_ge(sT.h, wv_free[qp])
                    g.dma_start(
                        out=wqsb[:, qp * 1024:(qp + 1) * 1024].rearrange(
                            "p (c d) -> p c d", c=2),
                        in_=wv_in[q2 * 2:(q2 + 1) * 2].rearrange("c p d -> p c d")).then_inc(sW.h, 16)
                    sW.n += 16
                    wv_done[(ts, q2)] = sW.n

                B_("gpsimd", _vw)

                def _vt(t, ts=ts, q2=q2, qp=qp):
                    if ts == 0 and q2 == 0:
                        t.wait_ge(sV.h, ln_done)
                    t.wait_ge(sW.h, wv_done[(ts, q2)])
                    if q2 == 0 and pv_free[ts % 2]:
                        t.wait_ge(sV.h, pv_free[ts % 2])
                    inst = None
                    for c2 in range(2):
                        cc = q2 * 2 + c2
                        inst = t.matmul(
                            P[4 + ts % 2][:, :],
                            xhat[:, cc * TC + ts * 128: cc * TC + (ts + 1) * 128],
                            wqsb[:, qp * 1024 + c2 * 512: qp * 1024 + (c2 + 1) * 512],
                            start=(cc == 0), stop=(cc == CC - 1))
                    inst.then_inc(sT.h, 1)
                    sT.n += 1
                    wv_free[qp] = sT.n

                B_("tensor", _vt)
            mm_t = sT.n

            def _vv(v, ts=ts, mm_t=mm_t):
                v.wait_ge(sT.h, mm_t)
                inst = v.tensor_copy(arena[:, AVO + ts * 512:AVO + (ts + 1) * 512],
                                     P[4 + ts % 2][:, :])
                inst.then_inc(sV.h, 1)
                sV.n += 1
                pv_free[ts % 2] = sV.n

            B_("vector", _vv)
        v_done = sV.n
        attn_xhat_free = sT.n

        def _kvsend(s):
            s.wait_ge(sV.h, k_done)
            s.dma_start(out=ksend_d.rearrange("(g p) t -> p g t", g=G),
                        in_=arena[:, AKO:AKO + 4096].rearrange("p (g t) -> p g t", g=G)
                        ).then_inc(sX.h, 16)
            sX.n += 16
            s.wait_ge(sV.h, v_done)
            s.dma_start(out=vsend_d.rearrange("(ts p) d -> p ts d", ts=8),
                        in_=arena[:, AVO:AVO + 4096].rearrange("p (ts d) -> p ts d", ts=8)
                        ).then_inc(sX.h, 16)
            sX.n += 16

        if GLOB.get("tap_kown"):
            def _dbgko(g):
                g.wait_ge(sV.h, k_done)
                for cx in range(4):
                    g.dma_start(out=y_out[cx],
                                in_=arena[:, AKO + cx * 1024:AKO + (cx + 1) * 1024]
                                ).then_inc(sX.h, 16)
                    sX.n += 16
                g.wait_ge(sV.h, v_done)
                for cx in range(4):
                    g.dma_start(out=y_out[4 + cx],
                                in_=arena[:, AVO + cx * 1024:AVO + (cx + 1) * 1024]
                                ).then_inc(sX.h, 16)
                    sX.n += 16
                for cx in range(8):
                    g.dma_start(out=y_out[8 + cx],
                                in_=xhat[:, cx * TC:(cx + 1) * TC]
                                ).then_inc(sX.h, 16)
                    sX.n += 16
            B_("gpsimd", _dbgko)
            return sX.n

        B_("sync", _kvsend)
        kv_sent = sX.n

        def _cc(g):
            g.wait_ge(sX.h, kv_sent)
            g.collective_compute("AllGather", OP.bypass,
                                 replica_groups=[list(range(NCORES))],
                                 ins=[ksend_d.opt()], outs=[krecv_d.opt()]
                                 ).then_inc(sCK.h, 1)
            sCK.n += 1
            g.collective_compute("AllGather", OP.bypass,
                                 replica_groups=[list(range(NCORES))],
                                 ins=[vsend_d.opt()], outs=[vrecv_d.opt()]
                                 ).then_inc(sCV.h, 1)
            sCV.n += 1

        B_("gpsimd", _cc)
        ck_wait = sCK.n
        cv_wait = sCV.n

        inner_sv, inner_ss = sV.n, sS.n

        def _inner_entry(t):
            t.wait_ge(sV.h, inner_sv)
            t.wait_ge(sS.h, inner_ss)

        B_("tensor", _inner_entry)

        kg_free = 0
        sp_free = {0: 0, 1: 0}
        pt_free = {0: 0, 1: 0}
        ps_free = {0: 0, 1: 0, 2: 0, 3: 0}
        ctx_free = {0: 0, 1: 0}
        sums_free = {0: 0, 1: 0}
        rb_free = 0

        for g in range(G):
            def _kvload(s, g=g, kg_free=kg_free):
                s.wait_ge(sCK.h, ck_wait)
                s.wait_ge(sCV.h, cv_wait)
                if kg_free:
                    s.wait_ge(sT.h, kg_free)
                for r in range(NCORES):
                    for half in range(2):
                        t0 = 4 * r if half == 0 else 60 - 4 * r
                        s.dma_start(
                            out=arena[:, AKG + t0 * 128: AKG + (t0 + 4) * 128],
                            in_=krecv_d[r, g * 128:(g + 1) * 128,
                                        half * 512:(half + 1) * 512]
                        ).then_inc(sX.h, 16)
                        sX.n += 16
                        s.dma_start(
                            out=arena[:, AVG:AVG + 8192].rearrange(
                                "p (t d) -> p t d", d=128)[:, t0:t0 + 4, :],
                            in_=vrecv_d[r].rearrange(
                                "(lt p) (gg d) -> p lt gg d", lt=8, gg=4)[
                                :, half * 4:(half + 1) * 4, g, :]
                        ).then_inc(sX.h, 16)
                        sX.n += 16

            B_("sync", _kvload)
            kv_loaded = sX.n
            if GLOB.get("tap_kg") and g == 0:
                def _dbgkg(gg):
                    gg.wait_ge(sX.h, kv_loaded)
                    for cx in range(8):
                        gg.dma_start(out=y_out[cx],
                                     in_=arena[:, AKG + cx * 1024:AKG + (cx + 1) * 1024]
                                     ).then_inc(sX.h, 16)
                        sX.n += 16
                    for cx in range(8):
                        gg.dma_start(out=y_out[8 + cx],
                                     in_=arena[:, AVG + cx * 1024:AVG + (cx + 1) * 1024]
                                     ).then_inc(sX.h, 16)
                        sX.n += 16
                B_("gpsimd", _dbgkg)
                return sX.n

            def _kv_t(t, kv_loaded=kv_loaded):
                t.wait_ge(sX.h, kv_loaded)

            B_("tensor", _kv_t)

            for hh in range(4):
                h = 4 * g + hh

                def _hprep(v, h=h, kv_loaded=kv_loaded):
                    v.wait_ge(sX.h, kv_loaded)
                    v.tensor_scalar_mul(scr32[:, SPQ:SPQ + 1024], qpos_sb[:, :], SLOPES[h])
                    v.tensor_scalar_mul(kcols_sb[:, 0:136], grids_sb[:, 0:136],
                                        SLOPES[h])
                    vsync(v)

                B_("vector", _hprep)

                for span in range(2):
                    cap = CAPA if span == 0 else CAPB
                    units = [("m", kt) for kt in range(cap)] + \
                            [("d", jj) for jj in range(4)]
                    n_units = len(units)
                    n_groups = n_units // 4
                    ctx_bank = span
                    sums_bank = 2 + span
                    sumsc = scr32[:, 4096:4864].bitcast(BF16)

                    pend = []

                    def emit_sumc(st_):
                        grp, gb, exp_s = st_["grp"], st_["gb"], st_["exp_s"]

                        def _sumc(v, grp=grp, gb=gb, exp_s=exp_s):
                            v.wait_ge(sS.h, exp_s)
                            if grp[0][0] == "d":
                                for ui, (_, jj) in enumerate(grp):
                                    v.tensor_tensor(
                                        scrbf[:, gb * 2048 + ui * 512:
                                              gb * 2048 + (ui + 1) * 512],
                                        scrbf[:, gb * 2048 + ui * 512:
                                              gb * 2048 + (ui + 1) * 512],
                                        masks_sb[:, jj * 512:(jj + 1) * 512],
                                        OP.mult)
                            v.tensor_tensor(sumsc[:, 0:512],
                                            scrbf[:, gb * 2048:gb * 2048 + 512],
                                            scrbf[:, gb * 2048 + 512:gb * 2048 + 1024],
                                            OP.add)
                            v.tensor_tensor(sumsc[:, 512:1024],
                                            scrbf[:, gb * 2048 + 1024:gb * 2048 + 1536],
                                            scrbf[:, gb * 2048 + 1536:gb * 2048 + 2048],
                                            OP.add)
                            inst = v.tensor_tensor(sumsc[:, 1024:1536],
                                                   sumsc[:, 0:512],
                                                   sumsc[:, 512:1024], OP.add)
                            inst.then_inc(sV.h, 1)
                            sV.n += 1

                        B_("vector", _sumc)
                        st_["sumc_v"] = sV.n

                    def emit_pvt(st_):
                        grp, gb, gi = st_["grp"], st_["gb"], st_["gi"]
                        exp_s, sumc_v = st_["exp_s"], st_["sumc_v"]

                        def _pvt(t, grp=grp, gb=gb, gi=gi, g=g, span=span,
                                 exp_s=exp_s, sumc_v=sumc_v):
                            t.wait_ge(sS.h, exp_s)
                            if grp[0][0] == "d":
                                t.wait_ge(sV.h, sumc_v)
                            if gi == 0:
                                if ctx_free[ctx_bank]:
                                    t.wait_ge(sV.h, ctx_free[ctx_bank])
                                if sums_free[span]:
                                    t.wait_ge(sV.h, sums_free[span])
                            for ui, (kind, idx) in enumerate(grp):
                                if kind == "m":
                                    vt = arena[:, AVG + idx * 128:
                                               AVG + (idx + 1) * 128]
                                else:
                                    base = AVO + (span * 4 + idx) * 512 + g * 128
                                    vt = arena[:, base:base + 128]
                                t.matmul(P[ctx_bank][:, :], vt,
                                         scrbf[:, gb * 2048 + ui * 512:
                                               gb * 2048 + (ui + 1) * 512],
                                         start=(gi == 0 and ui == 0),
                                         stop=(gi == n_groups - 1 and ui == 3))
                            t.wait_ge(sV.h, sumc_v)
                            inst = t.matmul(
                                P[sums_bank][0:1, :], onescb_sb[:, 0:1],
                                sumsc[:, 1024:1536],
                                start=(gi == 0), stop=(gi == n_groups - 1))
                            inst.then_inc(sT.h, 1)
                            sT.n += 1

                        B_("tensor", _pvt)
                        pt_free[gb] = sT.n

                    for gi in range(n_groups):
                        grp = units[4 * gi: 4 * gi + 4]
                        gb = gi % 2
                        s_mm = []

                        def _st_mm(t, grp=grp, h=h, span=span, g=g, gi=gi):
                            for ui, (kind, idx) in enumerate(grp):
                                if ps_free[ui]:
                                    t.wait_ge(sV.h, ps_free[ui])
                                if kind == "m":
                                    lhs = arena[:, AKG + idx * 128:
                                                AKG + (idx + 1) * 128]
                                else:
                                    base = AKO + g * 1024 + (span * 4 + idx) * 128
                                    lhs = arena[:, base:base + 128]
                                inst = t.matmul(
                                    P[4 + ui][:, :], lhs,
                                    arena[:, AQ + h * TC + span * TU:
                                          AQ + h * TC + (span + 1) * TU],
                                    start=True, stop=True)
                                inst.then_inc(sT.h, 1)
                                sT.n += 1
                                s_mm.append(sT.n)

                        B_("tensor", _st_mm)
                        if len(pend) >= 2:
                            st0 = pend.pop(0)
                            emit_sumc(st0)
                            emit_pvt(st0)

                        def _st_ev(v, grp=grp, span=span, gb=gb, s_mm=s_mm):
                            for ui, (kind, idx) in enumerate(grp):
                                v.wait_ge(sT.h, s_mm[ui])
                                if ui == 0 and sp_free[gb]:
                                    v.wait_ge(sS.h, sp_free[gb])
                                if kind == "m":
                                    kcol = kcols_sb[:, span * 64 + idx:
                                                    span * 64 + idx + 1]
                                else:
                                    kcol = kcols_sb[:, 128 + span * 4 + idx:
                                                    128 + span * 4 + idx + 1]
                                inst = v.scalar_tensor_tensor(
                                    o32[:, gb * 2048 + ui * 512:
                                        gb * 2048 + (ui + 1) * 512],
                                    P[4 + ui][:, :], kcol,
                                    scr32[:, SPQ + span * TU:SPQ + (span + 1) * TU],
                                    OP.add, OP.subtract)
                                inst.then_inc(sV.h, 1)
                                sV.n += 1
                                ps_free[ui] = sV.n

                        B_("vector", _st_ev)
                        ev_v = sV.n

                        def _sc_pair(sc, ev_v=ev_v):
                            sc.wait_ge(sV.h, ev_v)
                            sc.activation(o32[:, 0:2048], o32[:, 0:2048],
                                          AF.Tanh, scale=A_SCALE)
                            sc.activation(o32[:, 2048:4096], o32[:, 2048:4096],
                                          AF.Tanh, scale=A_SCALE)
                            if pt_free[0]:
                                sc.wait_ge(sT.h, pt_free[0])
                            if pt_free[1]:
                                sc.wait_ge(sT.h, pt_free[1])
                            sc.activation(scrbf[:, 0:2048], o32[:, 0:2048],
                                          AF.Exp, bias=neg30_sb[:, 0:1], scale=30.0)
                            inst = sc.activation(scrbf[:, 2048:4096],
                                                 o32[:, 2048:4096], AF.Exp,
                                                 bias=neg30_sb[:, 0:1], scale=30.0)
                            inst.then_inc(sS.h, 1)
                            sS.n += 1
                            sp_free[0] = sS.n
                            sp_free[1] = sS.n

                        if gi % 2 == 1:
                            B_("scalar", _sc_pair)
                            pend[-1]["exp_s"] = sS.n
                            pend.append({"grp": grp, "gb": gb, "gi": gi,
                                         "exp_s": sS.n})
                        else:
                            pend.append({"grp": grp, "gb": gb, "gi": gi,
                                         "exp_s": None})
                    while pend:
                        st0 = pend.pop(0)
                        emit_sumc(st0)
                        emit_pvt(st0)
                    span_t = sT.n

                    def _ctx_v1(v, span_t=span_t, sums_bank=sums_bank):
                        v.wait_ge(sT.h, span_t)
                        inst = v.reciprocal(scr32[0:1, 3584:4096], P[sums_bank][0:1, :])
                        inst.then_inc(sV.h, 1)
                        sV.n += 1

                    B_("vector", _ctx_v1)
                    rec_v = sV.n

                    def _ctx_t(t, rec_v=rec_v, sums_bank=sums_bank):
                        t.wait_ge(sV.h, rec_v)
                        inst = t.matmul(P[sums_bank][:, :], onesm_sb[0:1, :],
                                        scr32[0:1, 3584:4096], start=True, stop=True)
                        inst.then_inc(sT.h, 1)
                        sT.n += 1

                    B_("tensor", _ctx_t)
                    rb_t = sT.n

                    def _ctx_v2(v, h=h, span=span, rb_t=rb_t, ctx_bank=ctx_bank,
                                sums_bank=sums_bank):
                        v.wait_ge(sT.h, rb_t)
                        v.tensor_copy(scr32[:, RBX:RBX + 512], P[sums_bank][:, :])
                        vsync(v)
                        inst = v.tensor_tensor(
                            arena[:, AQ + h * TC + span * TU:
                                  AQ + h * TC + (span + 1) * TU],
                            P[ctx_bank][:, :], scr32[:, RBX:RBX + 512], OP.mult)
                        inst.then_inc(sV.h, 1)
                        sV.n += 1

                    B_("vector", _ctx_v2)
                    ctx_free[ctx_bank] = sV.n
                    sums_free[span] = sV.n
                    rb_free = sV.n
            kg_free = sT.n
        attn_done = sV.n
        attn_scrbf_free = sT.n
        if GLOB.get("tap_ctx"):
            def _dbgc2(g):
                g.wait_ge(sV.h, attn_done)
                for hx in range(16):
                    g.dma_start(out=y_out[hx],
                                in_=arena[:, AQ + hx * TC:AQ + (hx + 1) * TC]
                                ).then_inc(sX.h, 16)
                    sX.n += 16
            B_("gpsimd", _dbgc2)
            return sX.n

        wo_done = {}
        wo_free = {0: sT.n, 1: sT.n}
        wo_evac = [sV.n]
        oout_done = [0]
        for ocg in range(4):
            def _w_xo(s, ocg=ocg):
                if oout_done[0]:
                    s.wait_ge(sX.h, oout_done[0])
                else:
                    s.wait_ge(sV.h, attn_done)
                s.dma_start(
                    out=o32[:, :].rearrange("p (i t) -> p i t", i=4),
                    in_=src.rearrange("c p t -> p c t")[:, ocg * 4:(ocg + 1) * 4, :]
                ).then_inc(sX.h, 16)
                sX.n += 16

            B_("sync", _w_xo)
            oxo_done = sX.n

            wo_guard = wo_evac[0]
            for hc in range(CC):
                wpar = hc % 2

                def _w_gw(g, ocg=ocg, hc=hc, wpar=wpar):
                    if wo_free[wpar]:
                        g.wait_ge(sT.h, wo_free[wpar])
                    g.dma_start(out=wqsb[:, wpar * 2048:wpar * 2048 + 512],
                                in_=wo_in[hc, ocg]).then_inc(sW.h, 16)
                    sW.n += 16
                    wo_done[(ocg, hc)] = sW.n

                B_("gpsimd", _w_gw)

                def _w_t(t, ocg=ocg, hc=hc, wpar=wpar, wo_guard=wo_guard):
                    t.wait_ge(sW.h, wo_done[(ocg, hc)])
                    if hc == 0:
                        t.wait_ge(sV.h, wo_guard)
                    inst = None
                    for oc4 in range(4):
                        for u in range(NU):
                            inst = t.matmul(
                                P[oc4 * 2 + u][:, :],
                                wqsb[:, wpar * 2048 + oc4 * 128:
                                     wpar * 2048 + (oc4 + 1) * 128],
                                arena[:, AQ + hc * TC + u * TU:
                                      AQ + hc * TC + (u + 1) * TU],
                                start=(hc == 0), stop=(hc == CC - 1))
                    inst.then_inc(sT.h, 1)
                    sT.n += 1
                    wo_free[wpar] = sT.n

                B_("tensor", _w_t)
            w_t = sT.n

            def _w_v(v, ocg=ocg, w_t=w_t, oxo_done=oxo_done):
                v.wait_ge(sT.h, w_t)
                v.wait_ge(sX.h, oxo_done)
                inst = None
                for oc4 in range(4):
                    for u in range(NU):
                        inst = v.scalar_tensor_tensor(
                            o32[:, oc4 * 1024 + u * 512: oc4 * 1024 + (u + 1) * 512],
                            P[oc4 * 2 + u][:, :],
                            bo_sb[:, ocg * 4 + oc4: ocg * 4 + oc4 + 1],
                            o32[:, oc4 * 1024 + u * 512: oc4 * 1024 + (u + 1) * 512],
                            OP.add, OP.add)
                inst.then_inc(sV.h, 1)
                sV.n += 1
                wo_evac[0] = sV.n

            B_("vector", _w_v)

            def _w_out(s, ocg=ocg):
                s.wait_ge(sV.h, wo_evac[0])
                s.dma_start(
                    out=dst.rearrange("c p t -> p c t")[:, ocg * 4:(ocg + 1) * 4, :],
                    in_=o32[:, :].rearrange("p (i t) -> p i t", i=4)
                ).then_inc(sX.h, 16)
                sX.n += 16
                oout_done[0] = sX.n

                for i in range(4):
                    ready[ocg * 4 + i] = sX.n

            B_("sync", _w_out)
        return {"last": sX.n, "ready": ready, "xhat_free": attn_xhat_free,
                "scrbf_free": attn_scrbf_free}

    if tap == "comb":
        GLOB["gates_only"] = True
        emit_ffn(x0_in, y_out, first_call=True)
    elif tap == "comb2":
        GLOB["gates_only"] = True
        GLOB["go2"] = True
        r1 = emit_ffn(x0_in, x1_d, first_call=True)
        r2 = emit_attn(x1_d, x2_d, prev=r1)
        emit_ffn(x2_d, y_out, first_call=False, prev=r2)
    elif tap in ("qrope", "ctx", "kg", "kown"):
        GLOB["tap_qrope"] = tap == "qrope"
        GLOB["tap_ctx"] = tap == "ctx"
        GLOB["tap_kg"] = tap == "kg"
        GLOB["tap_kown"] = tap == "kown"
        r1 = emit_ffn(x0_in, x1_d, first_call=True)
        emit_attn(x1_d, y_out, prev=r1)
    elif tap == "ln1":
        ln_done = emit_ln(x0_in)
        def _dbg(g):
            g.wait_ge(sV.h, sV.n)
            for cc in range(CC):
                g.dma_start(out=y_out[cc],
                            in_=xhat[:, cc * TC:(cc + 1) * TC]).then_inc(sX.h, 16)
                sX.n += 16
        B_("gpsimd", _dbg)
    elif tap == "x1":
        emit_ffn(x0_in, y_out, first_call=True)
    elif tap == "x2":
        r1 = emit_ffn(x0_in, x1_d, first_call=True)
        emit_attn(x1_d, y_out, prev=r1)
    else:
        r1 = emit_ffn(x0_in, x1_d, first_call=True)
        r2 = emit_attn(x1_d, x2_d, prev=r1)
        emit_ffn(x2_d, y_out, first_call=False, prev=r2)

    final = sX.n

    def _fin(s):
        s.wait_ge(sX.h, final)

    B_("sync", _fin)
    ctx.close()
    return nc



def _prep_shared(I):
    f32 = np.float32
    Wq = np.asarray(I["L0_attn_Wq"], f32)
    Wk = np.asarray(I["L0_attn_Wk"], f32)
    Wv = np.asarray(I["L0_attn_Wv"], f32)
    Wo = np.asarray(I["L0_attn_Wo"], f32)
    bo = np.asarray(I["L0_attn_bo"], f32)
    gate_w = np.asarray(I["L0_ffn_gate_w"], f32)
    se_w = np.asarray(I["L0_ffn_se_gate_w"], f32)
    w_gate = np.asarray(I["L0_ffn_w_gate"], f32)
    w_up = np.asarray(I["L0_ffn_w_up"], f32)
    w_down = np.asarray(I["L0_ffn_w_down"], f32)
    sh_gate = np.asarray(I["L0_ffn_sh_gate"], f32)
    sh_up = np.asarray(I["L0_ffn_sh_up"], f32)
    sh_down = np.asarray(I["L0_ffn_sh_down"], f32)

    out = {}
    out["wq"] = np.ascontiguousarray(
        Wq.reshape(16, 128, 16, 128).transpose(2, 1, 0, 3)).reshape(16, 128, 2048)
    out["wk"] = np.ascontiguousarray(
        Wk.reshape(16, 128, 4, 128).transpose(2, 1, 0, 3)).reshape(4, 128, 2048)
    out["wv"] = np.ascontiguousarray(Wv.reshape(16, 128, 512))
    out["wo"] = np.ascontiguousarray(Wo.reshape(16, 128, 4, 512).transpose(0, 2, 1, 3))
    out["bo"] = np.ascontiguousarray(bo.reshape(16, 128).T)
    gws = np.concatenate([gate_w, se_w], 0)
    out["gwse"] = np.ascontiguousarray(gws.T.reshape(16, 128, 9).transpose(1, 0, 2))
    wg_all = np.concatenate([w_gate, sh_gate.reshape(2, 1024, 2048)], 0)
    wu_all = np.concatenate([w_up, sh_up.reshape(2, 1024, 2048)], 0)
    wg_p = wg_all.reshape(10, 8, 128, 16, 128).transpose(0, 1, 4, 3, 2).reshape(
        10, 8, 128, 2048)
    wu_p = wu_all.reshape(10, 8, 128, 16, 128).transpose(0, 1, 4, 3, 2).reshape(
        10, 8, 128, 2048)
    out["wgu"] = np.ascontiguousarray(np.concatenate([wg_p, wu_p], axis=3))
    wd_all = np.concatenate(
        [w_down, np.stack([sh_down[:, :1024], sh_down[:, 1024:]], 0)], 0)
    wd_p = wd_all.reshape(10, 4, 512, 8, 128).transpose(1, 0, 3, 4, 2)
    out["wd"] = np.ascontiguousarray(
        wd_p.reshape(4, 10, 2, 4, 128, 512).transpose(0, 1, 2, 4, 3, 5).reshape(
            4, 10, 2, 128, 2048))
    masks = []
    for jj in range(4):
        parts = []
        for m in range(4):
            if m > jj:
                parts.append(np.ones((128, 128), f32))
            elif m == jj:
                parts.append(np.triu(np.ones((128, 128), f32)))
            else:
                parts.append(np.zeros((128, 128), f32))
        masks.append(np.concatenate(parts, axis=1))
    out["masks"] = np.stack(masks, 0).astype(ml_dtypes.bfloat16)
    out["ident"] = np.eye(128, dtype=f32)
    pt = np.zeros((128, 128), f32)
    for i in range(64):
        pt[2 * i + 1, 2 * i] = -1.0
        pt[2 * i, 2 * i + 1] = 1.0
    out["ptmat"] = pt
    esel = np.zeros((10, 1280), f32)
    for pe in range(10):
        esel[pe, pe * 128:(pe + 1) * 128] = 1.0
    out["esel"] = esel.astype(ml_dtypes.bfloat16)
    out["onesm"] = np.ones((1, 128), f32)
    out["onescb"] = np.ones((128, 1), f32).astype(ml_dtypes.bfloat16)
    out["neg30"] = np.full((128, 1), -30.0, f32)
    return out


def _prep_core(I, i):
    f32 = np.float32
    tiles = core_tiles(i)
    hsT = np.asarray(I["L0_attn_hidden_states"], f32)[0].T
    x0 = np.concatenate([hsT[:, 128 * g:128 * (g + 1)] for g in tiles], axis=1)
    out = {"x0": np.ascontiguousarray(x0).reshape(16, 128, 1024)}
    pos = np.concatenate([np.arange(128 * g, 128 * (g + 1), dtype=f32) for g in tiles])
    out["qpos"] = np.ascontiguousarray(np.broadcast_to(pos[None, :], (128, 1024)))
    p = np.arange(128, dtype=f32)[:, None]
    kt = np.arange(64, dtype=f32)[None, :]
    gA = np.where(kt < 4 * i, kt * 128 + p, BIG_NEG)
    out["gridA"] = np.ascontiguousarray(gA.astype(f32))
    gB = np.where(kt < 60 - 4 * i, kt * 128 + p, BIG_NEG)
    out["gridB"] = np.ascontiguousarray(gB.astype(f32))
    out["gridOA"] = np.ascontiguousarray(
        (np.array(tiles[:4], f32)[None, :] * 128 + p).astype(f32))
    out["gridOB"] = np.ascontiguousarray(
        (np.array(tiles[4:], f32)[None, :] * 128 + p).astype(f32))
    inv = 1.0 / 10000.0 ** (np.arange(0, 128, 2, dtype=np.float64) / 128)
    invr = np.repeat(inv, 2)
    fr = invr[:, None] * pos[None, :].astype(np.float64)
    out["cosl"] = np.ascontiguousarray(np.cos(fr).astype(f32))
    out["sinl"] = np.ascontiguousarray(np.sin(fr).astype(f32))
    return out


def kernel(**inputs):
    shared = _prep_shared(inputs)
    in_maps = []
    for i in range(NCORES):
        m = dict(shared)
        m.update(_prep_core(inputs, i))
        in_maps.append(m)
    nc = build_nc(TAP)
    res = run_bass_kernel_spmd(nc, in_maps, core_ids=list(range(NCORES)))
    out = np.zeros((8192, 2048), np.float32)
    for i in range(NCORES):
        y = np.asarray(res.results[i]["y"], np.float32).reshape(2048, 1024)
        for j, g in enumerate(core_tiles(i)):
            out[128 * g:128 * (g + 1), :] = y[:, 128 * j:128 * (j + 1)].T
    return out[None, :, :]




# version 16
# speedup vs baseline: 1.1399x; 1.1399x over previous
import contextlib
import math
import numpy as np
import ml_dtypes

import concourse.bass as bass
import concourse.mybir as mybir
from concourse.bass_utils import run_bass_kernel_spmd

F32 = mybir.dt.float32
BF16 = mybir.dt.bfloat16
AF = mybir.ActivationFunctionType
OP = mybir.AluOpType

NCORES = 8
CC = 16
TC = 1024
NU = 2
TU = 512
NE = 10
HJ = 8
H = 16
G = 4
CAPA, CAPB = 28, 60
A_SCALE = float(1.0 / (30.0 * math.sqrt(128.0)))
SLOPES = [float(2.0 ** (-(h + 1) * 0.5)) for h in range(H)]
BIG_NEG = -1.0e30

TAP = None


def core_tiles(i):
    return [4 * i + j for j in range(4)] + [60 - 4 * i + j for j in range(4)]


class Ctr:
    def __init__(self, h):
        self.h = h
        self.n = 0

    def inc(self, inst, k=1):
        inst.then_inc(self.h, k)
        self.n += k
        return self.n


def build_nc(tap=None):
    nc = bass.Bass("TRN2", target_bir_lowering=False, debug=False)
    ap = lambda t: t.ap()

    x0_in = ap(nc.dram_tensor("x0", [CC, 128, TC], F32, kind="ExternalInput"))
    wq_in = ap(nc.dram_tensor("wq", [H, 128, CC * 128], F32, kind="ExternalInput"))
    wk_in = ap(nc.dram_tensor("wk", [G, 128, CC * 128], F32, kind="ExternalInput"))
    wv_in = ap(nc.dram_tensor("wv", [CC, 128, 512], F32, kind="ExternalInput"))
    wo_in = ap(nc.dram_tensor("wo", [CC, 4, 128, 512], F32, kind="ExternalInput"))
    bo_in = ap(nc.dram_tensor("bo", [128, 16], F32, kind="ExternalInput"))
    gwse_in = ap(nc.dram_tensor("gwse", [128, CC, 9], F32, kind="ExternalInput"))
    wgu_in = ap(nc.dram_tensor("wgu", [NE, HJ, 128, 4096], F32, kind="ExternalInput"))
    wd_in = ap(nc.dram_tensor("wd", [4, NE, 2, 128, 2048], F32, kind="ExternalInput"))
    qpos_in = ap(nc.dram_tensor("qpos", [128, TC], F32, kind="ExternalInput"))
    gridA_in = ap(nc.dram_tensor("gridA", [128, 64], F32, kind="ExternalInput"))
    gridB_in = ap(nc.dram_tensor("gridB", [128, 64], F32, kind="ExternalInput"))
    gridOA_in = ap(nc.dram_tensor("gridOA", [128, 4], F32, kind="ExternalInput"))
    gridOB_in = ap(nc.dram_tensor("gridOB", [128, 4], F32, kind="ExternalInput"))
    cos_in = ap(nc.dram_tensor("cosl", [128, TC], F32, kind="ExternalInput"))
    sin_in = ap(nc.dram_tensor("sinl", [128, TC], F32, kind="ExternalInput"))
    masks_in = ap(nc.dram_tensor("masks", [4, 128, 512], BF16, kind="ExternalInput"))
    ident_in = ap(nc.dram_tensor("ident", [128, 128], F32, kind="ExternalInput"))
    pt_in = ap(nc.dram_tensor("ptmat", [128, 128], F32, kind="ExternalInput"))
    esel_in = ap(nc.dram_tensor("esel", [NE, NE * 128], BF16, kind="ExternalInput"))
    onesm_in = ap(nc.dram_tensor("onesm", [1, 128], F32, kind="ExternalInput"))
    onescb_in = ap(nc.dram_tensor("onescb", [128, 1], BF16, kind="ExternalInput"))
    neg30_in = ap(nc.dram_tensor("neg30", [128, 1], F32, kind="ExternalInput"))

    y_out = ap(nc.dram_tensor("y", [CC, 128, TC], F32, kind="ExternalOutput"))

    x1_d = ap(nc.dram_tensor("x1d", [CC, 128, TC], F32, kind="Internal"))
    x2_d = ap(nc.dram_tensor("x2d", [CC, 128, TC], F32, kind="Internal"))
    wgubf_d = ap(nc.dram_tensor("wgubf", [NE, HJ, 128, 4096], BF16, kind="Internal"))
    wdbf_d = ap(nc.dram_tensor("wdbf", [4, NE, 2, 128, 2048], BF16, kind="Internal"))
    ksend_d = ap(nc.dram_tensor("ksend", [G * 128, TC], BF16, kind="Internal"))
    vsend_d = ap(nc.dram_tensor("vsend", [TC, 512], BF16, kind="Internal"))
    krecv_d = ap(nc.dram_tensor("krecv", [NCORES, G * 128, TC], BF16, kind="Internal"))
    vrecv_d = ap(nc.dram_tensor("vrecv", [NCORES, TC, 512], BF16, kind="Internal"))

    ctx = contextlib.ExitStack()
    E = ctx.enter_context

    xhat = E(nc.sbuf_tensor("xhat", [128, CC * TC], BF16))
    arena = E(nc.sbuf_tensor("arena", [128, 40960], BF16))
    AQ, AKG, AVG, AKO, AVO = 0, 16384, 24576, 32768, 36864

    scr32 = E(nc.sbuf_tensor("scr32", [128, 5120], F32))
    QRAW, T1C, T2C, COSC, SINC = 0, 512, 1024, 2048, 3072
    SPQ, RBX, TPAIR = 2048, 3072, 4096
    scrbf = E(nc.sbuf_tensor("scrbf", [128, 4096], BF16))
    o32 = E(nc.sbuf_tensor("o32", [128, 4096], F32))

    wgusb = E(nc.sbuf_tensor("wgusb", [128, 8192], BF16))
    wdsb = E(nc.sbuf_tensor("wdsb", [128, 4096], BF16))
    wqsb = E(nc.sbuf_tensor("wqsb", [128, 4096], BF16))
    gwsesb = E(nc.sbuf_tensor("gwsesb", [128, 144], BF16))

    masks_sb = E(nc.sbuf_tensor("masks_sb", [128, 2048], BF16))
    qpos_sb = E(nc.sbuf_tensor("qpos_sb", [128, TC], F32))
    grids_sb = E(nc.sbuf_tensor("grids_sb", [128, 136], F32))
    kcols_sb = E(nc.sbuf_tensor("kcols_sb", [128, 136], F32))
    ident_sb = E(nc.sbuf_tensor("ident_sb", [128, 128], F32))
    pt_sb = E(nc.sbuf_tensor("pt_sb", [128, 128], F32))
    esel_sb = E(nc.sbuf_tensor("esel_sb", [NE, NE * 128], BF16))
    onesm_sb = E(nc.sbuf_tensor("onesm_sb", [1, 128], F32))
    onescb_sb = E(nc.sbuf_tensor("onescb_sb", [128, 1], BF16))
    neg30_sb = E(nc.sbuf_tensor("neg30_sb", [128, 1], F32))
    bo_sb = E(nc.sbuf_tensor("bo_sb", [128, 16], F32))
    combfm = E(nc.sbuf_tensor("combfm", [NE, TC], BF16))
    comb8 = E(nc.sbuf_tensor("comb8", [128, 16], F32))
    gsc = E(nc.sbuf_tensor("gsc", [128, 48], F32))
    cols = E(nc.sbuf_tensor("cols", [128, 16], F32))


    P = [E(nc.psum_tensor(f"P{i}", [128, 512], F32)) for i in range(8)]

    sW = Ctr(E(nc.semaphore("sW")))
    sX = Ctr(E(nc.semaphore("sX")))
    sT = Ctr(E(nc.semaphore("sT")))
    sV = Ctr(E(nc.semaphore("sV")))
    sS = Ctr(E(nc.semaphore("sS")))
    sVV = Ctr(E(nc.semaphore("sVV")))
    sCK = Ctr(E(nc.semaphore("sCK")))
    sCV = Ctr(E(nc.semaphore("sCV")))

    block = E(nc.Block())

    GLOB = {"wbB": 0}

    def B_(name, fn):
        getattr(block, name)(fn)

    def vsync(v):
        inst = v.engine_nop()
        inst.then_inc(sVV.h, 1)
        sVV.n += 1
        v.wait_ge(sVV.h, sVV.n)

    def _consts(s):
        for dst, src in (
            (masks_sb[:, :].rearrange("p (m c) -> p m c", m=4), masks_in.rearrange("m p c -> p m c")),
            (qpos_sb[:, :], qpos_in),
            (grids_sb[:, 0:64], gridA_in),
            (grids_sb[:, 64:128], gridB_in),
            (grids_sb[:, 128:132], gridOA_in),
            (grids_sb[:, 132:136], gridOB_in),
            (ident_sb[:, :], ident_in),
            (pt_sb[:, :], pt_in),
            (esel_sb[:, :], esel_in),
            (onesm_sb[:, :], onesm_in),
            (onescb_sb[:, :], onescb_in),
            (neg30_sb[:, :], neg30_in),
            (bo_sb[:, :], bo_in),
        ):
            s.dma_start(out=dst, in_=src).then_inc(sX.h, 16)
            sX.n += 16

    B_("sync", _consts)
    consts_done = sX.n

    def _gwse(g):
        g.dma_start(out=gwsesb[:, :].rearrange("p (c n) -> p c n", n=9),
                    in_=gwse_in).then_inc(sW.h, 16)
        sW.n += 16

    B_("gpsimd", _gwse)
    gwse_done = sW.n

    def emit_ln(src, prev=None):
        ln_sv, ln_ss, ln_st, ln_sx = sV.n, sS.n, sT.n, sX.n
        src_ready = prev.get("ready") if prev else None
        xhat_free = prev.get("xhat_free", 0) if prev else 0
        scrbf_free = prev.get("scrbf_free", 0) if prev else 0
        sS_base = sS.n
        p1_sc = [sS_base + cc + 1 for cc in range(CC)]
        sT_base = sT.n
        p1_t = [sT_base + cc + 1 for cc in range(CC)]
        dma1 = []

        def _p1_sync(s):
            s.wait_ge(sS.h, ln_ss)
            s.wait_ge(sV.h, ln_sv)
            if src_ready is None:
                s.wait_ge(sT.h, ln_st)
                s.wait_ge(sX.h, ln_sx)
            for cc in range(CC):
                if src_ready is not None:
                    s.wait_ge(sX.h, src_ready[cc])
                if cc >= 2:
                    s.wait_ge(sS.h, p1_sc[cc - 2])
                s.dma_start(out=scr32[:, (cc % 2) * 1024:(cc % 2 + 1) * 1024],
                            in_=src[cc]).then_inc(sX.h, 16)
                sX.n += 16
                dma1.append(sX.n)

        B_("sync", _p1_sync)

        def _p1_scalar(sc):
            if xhat_free:
                sc.wait_ge(sT.h, xhat_free)
            if scrbf_free:
                sc.wait_ge(sT.h, scrbf_free)
            for cc in range(CC):
                par = cc % 2
                sc.wait_ge(sX.h, dma1[cc])
                if cc >= 2:
                    sc.wait_ge(sT.h, p1_t[cc - 2])
                sc.activation(xhat[:, cc * TC:(cc + 1) * TC],
                              scr32[:, par * 1024:(par + 1) * 1024], AF.Copy)
                inst = sc.activation(scrbf[:, par * 1024:(par + 1) * 1024],
                                     scr32[:, par * 1024:(par + 1) * 1024],
                                     AF.Square)
                inst.then_inc(sS.h, 1)
                sS.n += 1

        B_("scalar", _p1_scalar)
        assert sS.n == p1_sc[-1]

        def _p1_tensor(t):
            t.wait_ge(sX.h, consts_done)
            t.wait_ge(sV.h, ln_sv)
            for cc in range(CC):
                par = cc % 2
                t.wait_ge(sS.h, p1_sc[cc])
                st, sp = cc == 0, cc == CC - 1
                inst = None
                for u in range(NU):
                    t.matmul(P[u][0:1, :], onescb_sb[:, 0:1],
                             xhat[:, cc * TC + u * TU: cc * TC + (u + 1) * TU],
                             start=st, stop=sp)
                    inst = t.matmul(P[2 + u][0:1, :], onescb_sb[:, 0:1],
                                    scrbf[:, par * 1024 + u * TU: par * 1024 + (u + 1) * TU],
                                    start=st, stop=sp)
                inst.then_inc(sT.h, 1)
                sT.n += 1

        B_("tensor", _p1_tensor)
        assert sT.n == p1_t[-1]
        stats_done = sT.n

        TMPA, TMPB = 3072 - 2048, 3584 - 2048
        bcast_t = 0
        for u in range(NU):
            def _r_v1(v, u=u):
                if u == 0:
                    v.wait_ge(sT.h, stats_done)
                inst = v.tensor_scalar_mul(scr32[0:1, 2048 + u * 512:2048 + (u + 1) * 512],
                                           P[u][0:1, :], 1.0 / 2048.0)
                inst.then_inc(sV.h, 1)
                sV.n += 1

            B_("vector", _r_v1)
            mean_v = sV.n

            def _r_s1(sc, u=u, bcast_t=bcast_t, mean_v=mean_v):
                sc.wait_ge(sV.h, mean_v)
                if bcast_t:
                    sc.wait_ge(sT.h, bcast_t)
                inst = sc.activation(scr32[0:1, 2048 + TMPA:2048 + TMPA + 512],
                                     scr32[0:1, 2048 + u * 512:2048 + (u + 1) * 512], AF.Square)
                inst.then_inc(sS.h, 1)
                sS.n += 1

            B_("scalar", _r_s1)
            msq_s = sS.n

            def _r_v2(v, u=u, msq_s=msq_s):
                v.wait_ge(sS.h, msq_s)
                v.scalar_tensor_tensor(scr32[0:1, 2048 + TMPB:2048 + TMPB + 512],
                                       P[2 + u][0:1, :], 1.0 / 2048.0,
                                       scr32[0:1, 2048 + TMPA:2048 + TMPA + 512],
                                       OP.mult, OP.subtract)
                vsync(v)
                inst = v.tensor_scalar_add(scr32[0:1, 2048 + TMPB:2048 + TMPB + 512],
                                           scr32[0:1, 2048 + TMPB:2048 + TMPB + 512], 1.0e-5)
                inst.then_inc(sV.h, 1)
                sV.n += 1

            B_("vector", _r_v2)
            var_v = sV.n

            def _r_s2(sc, var_v=var_v):
                sc.wait_ge(sV.h, var_v)
                inst = sc.activation(scr32[0:1, 2048 + TMPA:2048 + TMPA + 512],
                                     scr32[0:1, 2048 + TMPB:2048 + TMPB + 512], AF.Sqrt)
                inst.then_inc(sS.h, 1)
                sS.n += 1

            B_("scalar", _r_s2)
            sd_s = sS.n

            def _r_v3(v, sd_s=sd_s):
                v.wait_ge(sS.h, sd_s)
                inst = v.reciprocal(scr32[0:1, 2048 + TMPB:2048 + TMPB + 512],
                                    scr32[0:1, 2048 + TMPA:2048 + TMPA + 512])
                inst.then_inc(sV.h, 1)
                sV.n += 1

            B_("vector", _r_v3)
            rstd_v = sV.n

            def _bc_t(t, u=u, rstd_v=rstd_v):
                t.wait_ge(sV.h, rstd_v)
                t.matmul(P[4 + u][:, :], onesm_sb[0:1, :],
                         scr32[0:1, 2048 + u * 512:2048 + (u + 1) * 512], start=True, stop=True)
                inst = t.matmul(P[6 + u][:, :], onesm_sb[0:1, :],
                                scr32[0:1, 2048 + TMPB:2048 + TMPB + 512], start=True, stop=True)
                inst.then_inc(sT.h, 1)
                sT.n += 1

            B_("tensor", _bc_t)
            bcast_t = sT.n

        sV_base = sV.n
        p2_v = [sV_base + cc + 1 for cc in range(CC)]
        dma2 = []

        def _p2_sync(s):
            for cc in range(CC):
                if src_ready is not None:
                    s.wait_ge(sX.h, src_ready[cc])
                if cc >= 2:
                    s.wait_ge(sV.h, p2_v[cc - 2])
                s.dma_start(out=scr32[:, (cc % 2) * 1024:(cc % 2 + 1) * 1024],
                            in_=src[cc]).then_inc(sX.h, 16)
                sX.n += 16
                dma2.append(sX.n)

        B_("sync", _p2_sync)

        def _p2_v(v):
            v.wait_ge(sT.h, bcast_t)
            v.wait_ge(sS.h, ln_ss)
            for cc in range(CC):
                par = cc % 2
                v.wait_ge(sX.h, dma2[cc])
                inst = None
                for u in range(NU):
                    v.tensor_tensor(scr32[:, 4096 + u * 512:4096 + (u + 1) * 512],
                                    scr32[:, par * 1024 + u * TU: par * 1024 + (u + 1) * TU],
                                    P[4 + u][:, :], OP.subtract)
                    inst = v.tensor_tensor(
                        xhat[:, cc * TC + u * TU: cc * TC + (u + 1) * TU],
                        scr32[:, 4096 + u * 512:4096 + (u + 1) * 512],
                        P[6 + u][:, :], OP.mult)
                inst.then_inc(sV.h, 1)
                sV.n += 1

        B_("vector", _p2_v)
        assert sV.n == p2_v[-1]
        return sV.n

    def emit_ffn(src, dst, first_call, prev=None):
        ffn_t0 = sT.n
        wb_gate = 0 if first_call else GLOB["wbB"]
        ln_done = emit_ln(src, prev)
        ready = [0] * CC

        prev_gate_v = 0
        for ts in range(8):
            def _g_t1(t, ts=ts, prev_gate_v=prev_gate_v):
                if ts == 0:
                    t.wait_ge(sV.h, ln_done)
                    t.wait_ge(sW.h, gwse_done)
                else:
                    t.wait_ge(sV.h, prev_gate_v)
                inst = None
                for cc in range(CC):
                    inst = t.matmul(
                        P[0][:, 0:9],
                        xhat[:, cc * TC + ts * 128: cc * TC + (ts + 1) * 128],
                        gwsesb[:, cc * 9:(cc + 1) * 9],
                        start=(cc == 0), stop=(cc == CC - 1))
                inst.then_inc(sT.h, 1)
                sT.n += 1

            B_("tensor", _g_t1)
            g_t = sT.n

            def _g_v1(v, g_t=g_t):
                v.wait_ge(sT.h, g_t)
                v.reduce_max(cols[:, 0:1], P[0][:, 0:8], axis=mybir.AxisListType.X)
                vsync(v)
                v.tensor_scalar_mul(cols[:, 1:2], cols[:, 0:1], -1.0)
                vsync(v)
                v.tensor_single_scalar(gsc[:, 0:8], P[0][:, 0:8], cols[:, 0:1],
                                       OP.is_equal)
                vsync(v)
                v.scalar_tensor_tensor(gsc[:, 8:16], gsc[:, 0:8], BIG_NEG,
                                       P[0][:, 0:8], OP.mult, OP.add)
                vsync(v)
                v.reduce_max(cols[:, 2:3], gsc[:, 8:16], axis=mybir.AxisListType.X)
                vsync(v)
                inst = v.tensor_tensor(cols[:, 3:4], cols[:, 2:3], cols[:, 0:1],
                                       OP.subtract)
                inst.then_inc(sV.h, 1)
                sV.n += 1

            B_("vector", _g_v1)
            g_v1 = sV.n

            def _g_s1(sc, g_v1=g_v1):
                sc.wait_ge(sV.h, g_v1)
                sc.activation(cols[:, 4:5], cols[:, 3:4], AF.Exp)
                sc.activation(gsc[:, 16:24], P[0][:, 0:8], AF.Exp, bias=cols[:, 1:2])
                inst = sc.activation(cols[:, 5:6], P[0][:, 8:9], AF.Exp, scale=-1.0)
                inst.then_inc(sS.h, 1)
                sS.n += 1

            B_("scalar", _g_s1)
            g_s1 = sS.n

            def _g_v2(v, g_s1=g_s1):
                v.wait_ge(sS.h, g_s1)
                v.tensor_scalar_add(cols[:, 6:7], cols[:, 4:5], 1.0)
                v.tensor_single_scalar(gsc[:, 24:32], P[0][:, 0:8], cols[:, 2:3],
                                       OP.is_ge)
                vsync(v)
                v.reciprocal(cols[:, 7:8], cols[:, 6:7])
                v.tensor_tensor(gsc[:, 32:40], gsc[:, 16:24], gsc[:, 24:32], OP.mult)
                vsync(v)
                v.tensor_scalar(comb8[:, 0:8], gsc[:, 32:40], cols[:, 7:8], 0.5,
                                OP.mult, OP.mult)
                v.tensor_scalar_add(cols[:, 8:9], cols[:, 5:6], 1.0)
                vsync(v)
                v.reciprocal(cols[:, 9:10], cols[:, 8:9])
                vsync(v)
                v.tensor_scalar_mul(comb8[:, 8:9], cols[:, 9:10], 0.5)
                inst = v.tensor_scalar_mul(comb8[:, 9:10], cols[:, 9:10], 0.5)
                inst.then_inc(sV.h, 1)
                sV.n += 1

            B_("vector", _g_v2)
            g_v2 = sV.n

            def _g_t2(t, g_v2=g_v2):
                t.wait_ge(sV.h, g_v2)
                inst = t.transpose(P[1][0:NE, 0:128], comb8[:, 0:NE], ident_sb[:, :])
                inst.then_inc(sT.h, 1)
                sT.n += 1

            B_("tensor", _g_t2)
            g_t2 = sT.n

            def _g_v3(v, g_t2=g_t2, ts=ts):
                v.wait_ge(sT.h, g_t2)
                inst = v.tensor_copy(combfm[0:NE, ts * 128:(ts + 1) * 128],
                                     P[1][0:NE, 0:128])
                inst.then_inc(sV.h, 1)
                sV.n += 1

            B_("vector", _g_v3)
            prev_gate_v = sV.n
        comb_done = sV.n
        if GLOB.get("gates_only") and (first_call != bool(GLOB.get("go2"))):
            def _dbgc(g):
                g.wait_ge(sV.h, comb_done)
                g.dma_start(out=y_out[0][0:NE, :],
                            in_=combfm[0:NE, :]).then_inc(sX.h, 16)
                sX.n += 16
                for cx in range(8):
                    g.dma_start(out=y_out[1 + cx],
                                in_=xhat[:, cx * TC:(cx + 1) * TC]).then_inc(sX.h, 16)
                    sX.n += 16
            B_("gpsimd", _dbgc)
            return sX.n

        wbA = 0
        wbB = 0
        wg_free = {0: ffn_t0, 1: ffn_t0}
        wd_free = {0: ffn_t0, 1: ffn_t0}
        xo_done = {0: 0, 1: 0}
        out_done = {0: 0, 1: 0}
        for u in range(NU):
            a_entry_sv = sV.n
            bank_free = {0: None, 1: None, 2: None, 3: None, 4: None, 5: None}
            p_prev2 = {}
            combbc_done = [0, 0]
            wg_done = {}
            s1_done = {}
            wb_done = {}
            for pe in range(NE):
                def _a_tbc(t, pe=pe, u=u, a_entry_sv=a_entry_sv):
                    if pe == 0:
                        t.wait_ge(sV.h, a_entry_sv)
                    if combbc_done[pe % 2]:
                        t.wait_ge(sV.h, combbc_done[pe % 2])
                    inst = t.matmul(P[6 + pe % 2][:, :],
                                    esel_sb[0:NE, pe * 128:(pe + 1) * 128],
                                    combfm[0:NE, u * TU:(u + 1) * TU],
                                    start=True, stop=True)
                    inst.then_inc(sT.h, 1)
                    sT.n += 1

                B_("tensor", _a_tbc)
                bc_t = sT.n

                def _a_vbc(v, bc_t=bc_t, pe=pe):
                    v.wait_ge(sT.h, bc_t)
                    inst = v.tensor_copy(
                        scr32[:, 4096 + (pe % 2) * 512: 4096 + (pe % 2 + 1) * 512],
                        P[6 + pe % 2][:, :])
                    inst.then_inc(sV.h, 1)
                    sV.n += 1

                B_("vector", _a_vbc)
                combbc_done[pe % 2] = sV.n

                for j in range(HJ):
                    wpar = j % 2

                    def _a_gw(g, pe=pe, j=j, wpar=wpar, u=u):
                        if not first_call and u == 0 and pe == 0 and j == 0:
                            g.wait_ge(sX.h, wb_gate)
                        if first_call and u == 0:
                            prev = wb_done.get((pe * HJ + j) - 2)
                            if prev:
                                g.wait_ge(sX.h, prev)
                        if wg_free[wpar]:
                            g.wait_ge(sT.h, wg_free[wpar])
                        if first_call and u == 0:
                            g.dma_start(out=wgusb[:, wpar * 4096:(wpar + 1) * 4096],
                                        in_=wgu_in[pe, j]).then_inc(sW.h, 16)
                        else:
                            if first_call and u == 1 and pe == 0 and j == 0:
                                g.wait_ge(sX.h, wbA)
                            g.dma_start(out=wgusb[:, wpar * 4096:(wpar + 1) * 4096],
                                        in_=wgubf_d[pe, j]).then_inc(sW.h, 16)
                        sW.n += 16
                        wg_done[(pe, j)] = sW.n

                    B_("gpsimd", _a_gw)

                    if first_call and u == 0:
                        def _a_wb(s, pe=pe, j=j, wpar=wpar):
                            s.wait_ge(sW.h, wg_done[(pe, j)])
                            s.dma_start(out=wgubf_d[pe, j],
                                        in_=wgusb[:, wpar * 4096:(wpar + 1) * 4096]
                                        ).then_inc(sX.h, 16)
                            sX.n += 16
                            wb_done[pe * HJ + j] = sX.n

                        B_("sync", _a_wb)

                    def _a_t(t, pe=pe, j=j, wpar=wpar, u=u):
                        jp = (pe * HJ + j) % 3
                        t.wait_ge(sW.h, wg_done[(pe, j)])
                        if bank_free[jp * 2] is not None:
                            sem, cnt = bank_free[jp * 2]
                            t.wait_ge(sem.h, cnt)
                        inst = None
                        for cc in range(CC):
                            inst = t.matmul(
                                P[jp * 2][:, :],
                                wgusb[:, wpar * 4096 + cc * 128: wpar * 4096 + (cc + 1) * 128],
                                xhat[:, cc * TC + u * TU: cc * TC + (u + 1) * TU],
                                start=(cc == 0), stop=(cc == CC - 1))
                        inst.then_inc(sT.h, 1)
                        sT.n += 1
                        if bank_free[jp * 2 + 1] is not None:
                            sem, cnt = bank_free[jp * 2 + 1]
                            t.wait_ge(sem.h, cnt)
                        for cc in range(CC):
                            inst = t.matmul(
                                P[jp * 2 + 1][:, :],
                                wgusb[:, wpar * 4096 + 2048 + cc * 128: wpar * 4096 + 2048 + (cc + 1) * 128],
                                xhat[:, cc * TC + u * TU: cc * TC + (u + 1) * TU],
                                start=(cc == 0), stop=(cc == CC - 1))
                        inst.then_inc(sT.h, 1)
                        sT.n += 1
                        wg_free[wpar] = sT.n

                    B_("tensor", _a_t)
                    h1_t = sT.n - 1
                    h3_t = sT.n

                    def _a_s(sc, h1_t=h1_t, pe=pe, j=j):
                        jp = (pe * HJ + j) % 3
                        sc.wait_ge(sT.h, h1_t)
                        pv = p_prev2.get((pe * HJ + j) % 2)
                        if pv:
                            sc.wait_ge(sV.h, pv)
                        inst = sc.activation(
                            scr32[:, 2048 + (jp % 2) * 512: 2048 + (jp % 2 + 1) * 512],
                            P[jp * 2][:, :], AF.Silu)
                        inst.then_inc(sS.h, 1)
                        sS.n += 1

                    B_("scalar", _a_s)
                    s1_done[(pe, j)] = sS.n

                    def _a_v(v, pe=pe, j=j, h3_t=h3_t):
                        jp = (pe * HJ + j) % 3
                        v.wait_ge(sT.h, h3_t)
                        v.wait_ge(sV.h, combbc_done[pe % 2])
                        v.tensor_tensor(
                            scr32[:, 3072 + (jp % 2) * 512: 3072 + (jp % 2 + 1) * 512],
                            P[jp * 2 + 1][:, :],
                            scr32[:, 4096 + (pe % 2) * 512: 4096 + (pe % 2 + 1) * 512],
                            OP.mult)
                        v.wait_ge(sS.h, s1_done[(pe, j)])
                        inst = v.tensor_tensor(
                            arena[:, pe * 4096 + j * 512: pe * 4096 + (j + 1) * 512],
                            scr32[:, 2048 + (jp % 2) * 512: 2048 + (jp % 2 + 1) * 512],
                            scr32[:, 3072 + (jp % 2) * 512: 3072 + (jp % 2 + 1) * 512],
                            OP.mult)
                        inst.then_inc(sV.h, 1)
                        sV.n += 1

                    B_("vector", _a_v)
                    bank_free[((pe * HJ + j) % 3) * 2] = (sS, s1_done[(pe, j)])
                    bank_free[((pe * HJ + j) % 3) * 2 + 1] = (sV, sV.n)
                    p_prev2[(pe * HJ + j) % 2] = sV.n
            a_done = sV.n
            if u == NU - 1:
                xhat_free_out = sT.n
            if first_call and u == 0:
                wbA = sX.n

            wd_done = {}
            wdwb_done = {}
            evac_done = {0: 0, 1: 0}
            for cg in range(4):
                cp = cg % 2

                def _b_xo(s, cg=cg, cp=cp, u=u):
                    if out_done[cp]:
                        s.wait_ge(sX.h, out_done[cp])
                    else:
                        s.wait_ge(sV.h, ln_done)
                    s.dma_start(
                        out=o32[:, cp * 2048:(cp + 1) * 2048].rearrange(
                            "p (i t) -> p i t", i=4),
                        in_=src.rearrange("c p t -> p c t")[
                            :, cg * 4:(cg + 1) * 4, u * TU:(u + 1) * TU]
                    ).then_inc(sX.h, 16)
                    sX.n += 16
                    xo_done[cp] = sX.n

                B_("sync", _b_xo)

                for pe in range(NE):
                    for j in range(HJ):
                        wpar = (pe * HJ + j) % 2

                        def _b_gw(g, cg=cg, pe=pe, j=j, u=u):
                            if j % 4 != 0:
                                return
                            jg = j // 4
                            wpar = (cg * NE * 2 + pe * 2 + jg) % 2
                            if not first_call and u == 0 and cg == 0 and pe == 0 and j == 0:
                                g.wait_ge(sX.h, wb_gate)
                            if first_call and u == 0:
                                prev = wdwb_done.get((cg * NE + pe) * 2 + jg - 2)
                                if prev:
                                    g.wait_ge(sX.h, prev)
                            if wd_free[wpar]:
                                g.wait_ge(sT.h, wd_free[wpar])
                            if first_call and u == 0:
                                g.dma_start(out=wdsb[:, wpar * 2048:(wpar + 1) * 2048],
                                            in_=wd_in[cg, pe, jg]).then_inc(sW.h, 16)
                            else:
                                if first_call and u == 1 and cg == 0 and pe == 0 and j == 0:
                                    g.wait_ge(sX.h, wbB)
                                g.dma_start(out=wdsb[:, wpar * 2048:(wpar + 1) * 2048],
                                            in_=wdbf_d[cg, pe, jg]).then_inc(sW.h, 16)
                            sW.n += 16
                            wd_done[(cg, pe, jg)] = sW.n

                        B_("gpsimd", _b_gw)

                        if first_call and u == 0:
                            def _b_wb(s, cg=cg, pe=pe, j=j):
                                if j % 4 != 0:
                                    return
                                jg = j // 4
                                wpar = (cg * NE * 2 + pe * 2 + jg) % 2
                                s.wait_ge(sW.h, wd_done[(cg, pe, jg)])
                                s.dma_start(out=wdbf_d[cg, pe, jg],
                                            in_=wdsb[:, wpar * 2048:(wpar + 1) * 2048]
                                            ).then_inc(sX.h, 16)
                                sX.n += 16
                                wdwb_done[(cg * NE + pe) * 2 + jg] = sX.n

                            B_("sync", _b_wb)

                        def _b_t(t, cg=cg, pe=pe, j=j, cp=cp):
                            jg = j // 4
                            wpar = (cg * NE * 2 + pe * 2 + jg) % 2
                            t.wait_ge(sW.h, wd_done[(cg, pe, jg)])
                            if pe == 0 and j == 0:
                                t.wait_ge(sV.h, a_done)
                                if evac_done[cp]:
                                    t.wait_ge(sV.h, evac_done[cp])
                            first = pe == 0 and j == 0
                            last = pe == NE - 1 and j == HJ - 1
                            inst = None
                            for i in range(4):
                                inst = t.matmul(
                                    P[cp * 4 + i][:, :],
                                    wdsb[:, wpar * 2048 + (j % 4) * 512 + i * 128:
                                         wpar * 2048 + (j % 4) * 512 + (i + 1) * 128],
                                    arena[:, pe * 4096 + j * 512: pe * 4096 + (j + 1) * 512],
                                    start=first, stop=last)
                            inst.then_inc(sT.h, 1)
                            sT.n += 1
                            if j % 4 == 3:
                                wd_free[wpar] = sT.n

                        B_("tensor", _b_t)
                b_t = sT.n

                def _b_v(v, cp=cp, b_t=b_t):
                    v.wait_ge(sT.h, b_t)
                    v.wait_ge(sX.h, xo_done[cp])
                    inst = None
                    for i in range(4):
                        inst = v.tensor_tensor(
                            o32[:, cp * 2048 + i * 512: cp * 2048 + (i + 1) * 512],
                            P[cp * 4 + i][:, :],
                            o32[:, cp * 2048 + i * 512: cp * 2048 + (i + 1) * 512],
                            OP.add)
                    inst.then_inc(sV.h, 1)
                    sV.n += 1
                    evac_done[cp] = sV.n

                B_("vector", _b_v)

                def _b_out(s, cg=cg, cp=cp, u=u):
                    s.wait_ge(sV.h, evac_done[cp])
                    s.dma_start(
                        out=dst.rearrange("c p t -> p c t")[
                            :, cg * 4:(cg + 1) * 4, u * TU:(u + 1) * TU],
                        in_=o32[:, cp * 2048:(cp + 1) * 2048].rearrange(
                            "p (i t) -> p i t", i=4)).then_inc(sX.h, 16)
                    sX.n += 16
                    out_done[cp] = sX.n

                B_("sync", _b_out)
                if u == NU - 1:
                    for i in range(4):
                        ready[cg * 4 + i] = sX.n
            if first_call and u == 0:
                wbB = sX.n
                GLOB["wbB"] = sX.n
        return {"last": sX.n, "ready": ready, "xhat_free": xhat_free_out,
                "scrbf_free": 0}

    def emit_attn(src, dst, prev=None):
        ln_done = emit_ln(src, prev)
        ready = [0] * CC

        cs_sv, cs_ss = sV.n, sS.n

        def _cs(s):
            s.wait_ge(sV.h, cs_sv)
            s.wait_ge(sS.h, cs_ss)
            s.dma_start(out=scr32[:, COSC:COSC + 1024], in_=cos_in).then_inc(sX.h, 16)
            sX.n += 16
            s.dma_start(out=scr32[:, SINC:SINC + 1024], in_=sin_in).then_inc(sX.h, 16)
            sX.n += 16

        B_("sync", _cs)
        cs_done = sX.n

        st = {"pq_free": {0: 0, 1: 0}, "qraw_free": 0, "pr_free": 0}

        def proj_rope(n_heads, w_in, dst_col):
            wq_done = {}
            wfree = {0: sT.n, 1: sT.n}
            for h in range(n_heads):
                wpar = h % 2

                def _pw(g, h=h, wpar=wpar):
                    if wfree[wpar]:
                        g.wait_ge(sT.h, wfree[wpar])
                    g.dma_start(out=wqsb[:, wpar * 2048:(wpar + 1) * 2048],
                                in_=w_in[h]).then_inc(sW.h, 16)
                    sW.n += 16
                    wq_done[h] = sW.n

                B_("gpsimd", _pw)

                for u in range(NU):
                    def _pt(t, h=h, u=u, wpar=wpar):
                        if h == 0 and u == 0:
                            t.wait_ge(sV.h, ln_done)
                        t.wait_ge(sW.h, wq_done[h])
                        if st["pq_free"][u]:
                            t.wait_ge(sS.h, st["pq_free"][u])
                        inst = None
                        for cc in range(CC):
                            inst = t.matmul(
                                P[u][:, :],
                                wqsb[:, wpar * 2048 + cc * 128: wpar * 2048 + (cc + 1) * 128],
                                xhat[:, cc * TC + u * TU: cc * TC + (u + 1) * TU],
                                start=(cc == 0), stop=(cc == CC - 1))
                        inst.then_inc(sT.h, 1)
                        sT.n += 1

                    B_("tensor", _pt)
                    mm_t = sT.n
                    if u == NU - 1:
                        wfree[wpar] = sT.n

                    def _ps(sc, mm_t=mm_t, u=u):
                        sc.wait_ge(sT.h, mm_t)
                        if st["qraw_free"]:
                            sc.wait_ge(sT.h, st["qraw_free"])
                        if st["pr_free"]:
                            sc.wait_ge(sV.h, st["pr_free"])
                        inst = sc.activation(scr32[:, QRAW:QRAW + 512], P[u][:, :], AF.Copy)
                        inst.then_inc(sS.h, 1)
                        sS.n += 1
                        st["pq_free"][u] = sS.n

                    B_("scalar", _ps)
                    cp_s = sS.n

                    def _pt2(t, cp_s=cp_s):
                        t.wait_ge(sS.h, cp_s)
                        if st["pr_free"]:
                            t.wait_ge(sV.h, st["pr_free"])
                        inst = t.matmul(P[2][:, :], pt_sb[:, :], scr32[:, QRAW:QRAW + 512],
                                        start=True, stop=True)
                        inst.then_inc(sT.h, 1)
                        sT.n += 1
                        st["qraw_free"] = sT.n

                    B_("tensor", _pt2)
                    rot_t = sT.n

                    def _pv(v, h=h, u=u, rot_t=rot_t):
                        v.wait_ge(sX.h, cs_done)
                        v.wait_ge(sT.h, rot_t)
                        v.tensor_tensor(scr32[:, T1C:T1C + 512], scr32[:, QRAW:QRAW + 512],
                                        scr32[:, COSC + u * TU:COSC + (u + 1) * TU], OP.mult)
                        v.tensor_tensor(scr32[:, T2C:T2C + 512], P[2][:, :],
                                        scr32[:, SINC + u * TU:SINC + (u + 1) * TU],
                                        OP.mult)
                        vsync(v)
                        inst = v.tensor_tensor(
                            arena[:, dst_col + h * TC + u * TU:
                                  dst_col + h * TC + (u + 1) * TU],
                            scr32[:, T1C:T1C + 512], scr32[:, T2C:T2C + 512], OP.add)
                        inst.then_inc(sV.h, 1)
                        sV.n += 1
                        st["pr_free"] = sV.n

                    B_("vector", _pv)
            return sV.n

        q_done = proj_rope(H, wq_in, AQ)
        k_done = proj_rope(G, wk_in, AKO)
        if GLOB.get("tap_qrope"):
            def _dbgq(g):
                g.wait_ge(sV.h, q_done)
                for hx in range(16):
                    g.dma_start(out=y_out[hx],
                                in_=arena[:, AQ + hx * TC:AQ + (hx + 1) * TC]
                                ).then_inc(sX.h, 16)
                    sX.n += 16
            B_("gpsimd", _dbgq)
            return sX.n

        wv_done = {}
        wv_free = {0: sT.n, 1: sT.n, 2: sT.n, 3: sT.n}
        pv_free = {0: 0, 1: 0}
        for ts in range(8):
            for q2 in range(8):
                qp = q2 % 4

                def _vw(g, ts=ts, q2=q2, qp=qp):
                    if wv_free[qp]:
                        g.wait_ge(sT.h, wv_free[qp])
                    g.dma_start(
                        out=wqsb[:, qp * 1024:(qp + 1) * 1024].rearrange(
                            "p (c d) -> p c d", c=2),
                        in_=wv_in[q2 * 2:(q2 + 1) * 2].rearrange("c p d -> p c d")).then_inc(sW.h, 16)
                    sW.n += 16
                    wv_done[(ts, q2)] = sW.n

                B_("gpsimd", _vw)

                def _vt(t, ts=ts, q2=q2, qp=qp):
                    if ts == 0 and q2 == 0:
                        t.wait_ge(sV.h, ln_done)
                    t.wait_ge(sW.h, wv_done[(ts, q2)])
                    if q2 == 0 and pv_free[ts % 2]:
                        t.wait_ge(sV.h, pv_free[ts % 2])
                    inst = None
                    for c2 in range(2):
                        cc = q2 * 2 + c2
                        inst = t.matmul(
                            P[4 + ts % 2][:, :],
                            xhat[:, cc * TC + ts * 128: cc * TC + (ts + 1) * 128],
                            wqsb[:, qp * 1024 + c2 * 512: qp * 1024 + (c2 + 1) * 512],
                            start=(cc == 0), stop=(cc == CC - 1))
                    inst.then_inc(sT.h, 1)
                    sT.n += 1
                    wv_free[qp] = sT.n

                B_("tensor", _vt)
            mm_t = sT.n

            def _vv(v, ts=ts, mm_t=mm_t):
                v.wait_ge(sT.h, mm_t)
                inst = v.tensor_copy(arena[:, AVO + ts * 512:AVO + (ts + 1) * 512],
                                     P[4 + ts % 2][:, :])
                inst.then_inc(sV.h, 1)
                sV.n += 1
                pv_free[ts % 2] = sV.n

            B_("vector", _vv)
        v_done = sV.n
        attn_xhat_free = sT.n

        def _kvsend(s):
            s.wait_ge(sV.h, k_done)
            s.dma_start(out=ksend_d.rearrange("(g p) t -> p g t", g=G),
                        in_=arena[:, AKO:AKO + 4096].rearrange("p (g t) -> p g t", g=G)
                        ).then_inc(sX.h, 16)
            sX.n += 16
            s.wait_ge(sV.h, v_done)
            s.dma_start(out=vsend_d.rearrange("(ts p) d -> p ts d", ts=8),
                        in_=arena[:, AVO:AVO + 4096].rearrange("p (ts d) -> p ts d", ts=8)
                        ).then_inc(sX.h, 16)
            sX.n += 16

        if GLOB.get("tap_kown"):
            def _dbgko(g):
                g.wait_ge(sV.h, k_done)
                for cx in range(4):
                    g.dma_start(out=y_out[cx],
                                in_=arena[:, AKO + cx * 1024:AKO + (cx + 1) * 1024]
                                ).then_inc(sX.h, 16)
                    sX.n += 16
                g.wait_ge(sV.h, v_done)
                for cx in range(4):
                    g.dma_start(out=y_out[4 + cx],
                                in_=arena[:, AVO + cx * 1024:AVO + (cx + 1) * 1024]
                                ).then_inc(sX.h, 16)
                    sX.n += 16
                for cx in range(8):
                    g.dma_start(out=y_out[8 + cx],
                                in_=xhat[:, cx * TC:(cx + 1) * TC]
                                ).then_inc(sX.h, 16)
                    sX.n += 16
            B_("gpsimd", _dbgko)
            return sX.n

        B_("sync", _kvsend)
        kv_sent = sX.n

        def _cc(g):
            g.wait_ge(sX.h, kv_sent)
            g.collective_compute("AllGather", OP.bypass,
                                 replica_groups=[list(range(NCORES))],
                                 ins=[ksend_d.opt()], outs=[krecv_d.opt()]
                                 ).then_inc(sCK.h, 1)
            sCK.n += 1
            g.collective_compute("AllGather", OP.bypass,
                                 replica_groups=[list(range(NCORES))],
                                 ins=[vsend_d.opt()], outs=[vrecv_d.opt()]
                                 ).then_inc(sCV.h, 1)
            sCV.n += 1

        B_("gpsimd", _cc)
        ck_wait = sCK.n
        cv_wait = sCV.n

        inner_sv, inner_ss = sV.n, sS.n

        def _inner_entry(t):
            t.wait_ge(sV.h, inner_sv)
            t.wait_ge(sS.h, inner_ss)

        B_("tensor", _inner_entry)

        kg_free = 0
        sp_free = {0: 0, 1: 0}
        pt_free = {0: 0, 1: 0}
        ps_free = {0: 0, 1: 0, 2: 0, 3: 0}
        ctx_free = {0: 0, 1: 0}
        sums_free = {0: 0, 1: 0}
        rb_free = 0

        for g in range(G):
            def _kvload(s, g=g, kg_free=kg_free):
                s.wait_ge(sCK.h, ck_wait)
                s.wait_ge(sCV.h, cv_wait)
                if kg_free:
                    s.wait_ge(sT.h, kg_free)
                for r in range(NCORES):
                    for half in range(2):
                        t0 = 4 * r if half == 0 else 60 - 4 * r
                        s.dma_start(
                            out=arena[:, AKG + t0 * 128: AKG + (t0 + 4) * 128],
                            in_=krecv_d[r, g * 128:(g + 1) * 128,
                                        half * 512:(half + 1) * 512]
                        ).then_inc(sX.h, 16)
                        sX.n += 16
                        s.dma_start(
                            out=arena[:, AVG:AVG + 8192].rearrange(
                                "p (t d) -> p t d", d=128)[:, t0:t0 + 4, :],
                            in_=vrecv_d[r].rearrange(
                                "(lt p) (gg d) -> p lt gg d", lt=8, gg=4)[
                                :, half * 4:(half + 1) * 4, g, :]
                        ).then_inc(sX.h, 16)
                        sX.n += 16

            B_("sync", _kvload)
            kv_loaded = sX.n
            if GLOB.get("tap_kg") and g == 0:
                def _dbgkg(gg):
                    gg.wait_ge(sX.h, kv_loaded)
                    for cx in range(8):
                        gg.dma_start(out=y_out[cx],
                                     in_=arena[:, AKG + cx * 1024:AKG + (cx + 1) * 1024]
                                     ).then_inc(sX.h, 16)
                        sX.n += 16
                    for cx in range(8):
                        gg.dma_start(out=y_out[8 + cx],
                                     in_=arena[:, AVG + cx * 1024:AVG + (cx + 1) * 1024]
                                     ).then_inc(sX.h, 16)
                        sX.n += 16
                B_("gpsimd", _dbgkg)
                return sX.n

            def _kv_t(t, kv_loaded=kv_loaded):
                t.wait_ge(sX.h, kv_loaded)

            B_("tensor", _kv_t)

            for hh in range(4):
                h = 4 * g + hh

                def _hprep(v, h=h, kv_loaded=kv_loaded):
                    v.wait_ge(sX.h, kv_loaded)
                    v.tensor_scalar_mul(scr32[:, SPQ:SPQ + 1024], qpos_sb[:, :], SLOPES[h])
                    v.tensor_scalar_mul(kcols_sb[:, 0:136], grids_sb[:, 0:136],
                                        SLOPES[h])
                    vsync(v)

                B_("vector", _hprep)

                for span in range(2):
                    cap = CAPA if span == 0 else CAPB
                    units = [("m", kt) for kt in range(cap)] + \
                            [("d", jj) for jj in range(4)]
                    n_units = len(units)
                    n_groups = n_units // 4
                    ctx_bank = span
                    sums_bank = 2 + span
                    sumsc = scr32[:, 4096:4864].bitcast(BF16)

                    pend = []

                    def emit_sumc(st_):
                        grp, gb, exp_s = st_["grp"], st_["gb"], st_["exp_s"]

                        def _sumc(v, grp=grp, gb=gb, exp_s=exp_s):
                            v.wait_ge(sS.h, exp_s)
                            if grp[0][0] == "d":
                                for ui, (_, jj) in enumerate(grp):
                                    v.tensor_tensor(
                                        scrbf[:, gb * 2048 + ui * 512:
                                              gb * 2048 + (ui + 1) * 512],
                                        scrbf[:, gb * 2048 + ui * 512:
                                              gb * 2048 + (ui + 1) * 512],
                                        masks_sb[:, jj * 512:(jj + 1) * 512],
                                        OP.mult)
                            v.tensor_tensor(sumsc[:, 0:512],
                                            scrbf[:, gb * 2048:gb * 2048 + 512],
                                            scrbf[:, gb * 2048 + 512:gb * 2048 + 1024],
                                            OP.add)
                            v.tensor_tensor(sumsc[:, 512:1024],
                                            scrbf[:, gb * 2048 + 1024:gb * 2048 + 1536],
                                            scrbf[:, gb * 2048 + 1536:gb * 2048 + 2048],
                                            OP.add)
                            inst = v.tensor_tensor(sumsc[:, 1024:1536],
                                                   sumsc[:, 0:512],
                                                   sumsc[:, 512:1024], OP.add)
                            inst.then_inc(sV.h, 1)
                            sV.n += 1

                        B_("vector", _sumc)
                        st_["sumc_v"] = sV.n

                    def emit_pvt(st_):
                        grp, gb, gi = st_["grp"], st_["gb"], st_["gi"]
                        exp_s, sumc_v = st_["exp_s"], st_["sumc_v"]

                        def _pvt(t, grp=grp, gb=gb, gi=gi, g=g, span=span,
                                 exp_s=exp_s, sumc_v=sumc_v):
                            t.wait_ge(sS.h, exp_s)
                            if grp[0][0] == "d":
                                t.wait_ge(sV.h, sumc_v)
                            if gi == 0:
                                if ctx_free[ctx_bank]:
                                    t.wait_ge(sV.h, ctx_free[ctx_bank])
                                if sums_free[span]:
                                    t.wait_ge(sV.h, sums_free[span])
                            for ui, (kind, idx) in enumerate(grp):
                                if kind == "m":
                                    vt = arena[:, AVG + idx * 128:
                                               AVG + (idx + 1) * 128]
                                else:
                                    base = AVO + (span * 4 + idx) * 512 + g * 128
                                    vt = arena[:, base:base + 128]
                                t.matmul(P[ctx_bank][:, :], vt,
                                         scrbf[:, gb * 2048 + ui * 512:
                                               gb * 2048 + (ui + 1) * 512],
                                         start=(gi == 0 and ui == 0),
                                         stop=(gi == n_groups - 1 and ui == 3))
                            t.wait_ge(sV.h, sumc_v)
                            inst = t.matmul(
                                P[sums_bank][0:1, :], onescb_sb[:, 0:1],
                                sumsc[:, 1024:1536],
                                start=(gi == 0), stop=(gi == n_groups - 1))
                            inst.then_inc(sT.h, 1)
                            sT.n += 1

                        B_("tensor", _pvt)
                        pt_free[gb] = sT.n

                    for gi in range(n_groups):
                        grp = units[4 * gi: 4 * gi + 4]
                        gb = gi % 2
                        s_mm = []

                        def _st_mm(t, grp=grp, h=h, span=span, g=g, gi=gi):
                            for ui, (kind, idx) in enumerate(grp):
                                if ps_free[ui]:
                                    t.wait_ge(sV.h, ps_free[ui])
                                if kind == "m":
                                    lhs = arena[:, AKG + idx * 128:
                                                AKG + (idx + 1) * 128]
                                else:
                                    base = AKO + g * 1024 + (span * 4 + idx) * 128
                                    lhs = arena[:, base:base + 128]
                                inst = t.matmul(
                                    P[4 + ui][:, :], lhs,
                                    arena[:, AQ + h * TC + span * TU:
                                          AQ + h * TC + (span + 1) * TU],
                                    start=True, stop=True)
                                inst.then_inc(sT.h, 1)
                                sT.n += 1
                                s_mm.append(sT.n)

                        B_("tensor", _st_mm)
                        if len(pend) >= 2:
                            st0 = pend.pop(0)
                            emit_sumc(st0)
                            emit_pvt(st0)

                        def _st_ev(v, grp=grp, span=span, gb=gb, s_mm=s_mm):
                            for ui, (kind, idx) in enumerate(grp):
                                v.wait_ge(sT.h, s_mm[ui])
                                if ui == 0 and sp_free[gb]:
                                    v.wait_ge(sS.h, sp_free[gb])
                                if kind == "m":
                                    kcol = kcols_sb[:, span * 64 + idx:
                                                    span * 64 + idx + 1]
                                else:
                                    kcol = kcols_sb[:, 128 + span * 4 + idx:
                                                    128 + span * 4 + idx + 1]
                                inst = v.scalar_tensor_tensor(
                                    o32[:, gb * 2048 + ui * 512:
                                        gb * 2048 + (ui + 1) * 512],
                                    P[4 + ui][:, :], kcol,
                                    scr32[:, SPQ + span * TU:SPQ + (span + 1) * TU],
                                    OP.add, OP.subtract)
                                inst.then_inc(sV.h, 1)
                                sV.n += 1
                                ps_free[ui] = sV.n

                        B_("vector", _st_ev)
                        ev_v = sV.n

                        def _sc(sc, gb=gb, ev_v=ev_v):
                            sc.wait_ge(sV.h, ev_v)
                            if pt_free[gb]:
                                sc.wait_ge(sT.h, pt_free[gb])
                            sc.activation(o32[:, gb * 2048:(gb + 1) * 2048],
                                          o32[:, gb * 2048:(gb + 1) * 2048],
                                          AF.Tanh, scale=A_SCALE)
                            inst = sc.activation(
                                scrbf[:, gb * 2048:(gb + 1) * 2048],
                                o32[:, gb * 2048:(gb + 1) * 2048], AF.Exp,
                                bias=neg30_sb[:, 0:1], scale=30.0)
                            inst.then_inc(sS.h, 1)
                            sS.n += 1
                            sp_free[gb] = sS.n

                        B_("scalar", _sc)
                        pend.append({"grp": grp, "gb": gb, "gi": gi,
                                     "exp_s": sS.n})
                    while pend:
                        st0 = pend.pop(0)
                        emit_sumc(st0)
                        emit_pvt(st0)
                    span_t = sT.n

                    def _ctx_v1(v, span_t=span_t, sums_bank=sums_bank):
                        v.wait_ge(sT.h, span_t)
                        inst = v.reciprocal(scr32[0:1, 3584:4096], P[sums_bank][0:1, :])
                        inst.then_inc(sV.h, 1)
                        sV.n += 1

                    B_("vector", _ctx_v1)
                    rec_v = sV.n

                    def _ctx_t(t, rec_v=rec_v, sums_bank=sums_bank):
                        t.wait_ge(sV.h, rec_v)
                        inst = t.matmul(P[sums_bank][:, :], onesm_sb[0:1, :],
                                        scr32[0:1, 3584:4096], start=True, stop=True)
                        inst.then_inc(sT.h, 1)
                        sT.n += 1

                    B_("tensor", _ctx_t)
                    rb_t = sT.n

                    def _ctx_v2(v, h=h, span=span, rb_t=rb_t, ctx_bank=ctx_bank,
                                sums_bank=sums_bank):
                        v.wait_ge(sT.h, rb_t)
                        v.tensor_copy(scr32[:, RBX:RBX + 512], P[sums_bank][:, :])
                        vsync(v)
                        inst = v.tensor_tensor(
                            arena[:, AQ + h * TC + span * TU:
                                  AQ + h * TC + (span + 1) * TU],
                            P[ctx_bank][:, :], scr32[:, RBX:RBX + 512], OP.mult)
                        inst.then_inc(sV.h, 1)
                        sV.n += 1

                    B_("vector", _ctx_v2)
                    ctx_free[ctx_bank] = sV.n
                    sums_free[span] = sV.n
                    rb_free = sV.n
            kg_free = sT.n
        attn_done = sV.n
        attn_scrbf_free = sT.n
        if GLOB.get("tap_ctx"):
            def _dbgc2(g):
                g.wait_ge(sV.h, attn_done)
                for hx in range(16):
                    g.dma_start(out=y_out[hx],
                                in_=arena[:, AQ + hx * TC:AQ + (hx + 1) * TC]
                                ).then_inc(sX.h, 16)
                    sX.n += 16
            B_("gpsimd", _dbgc2)
            return sX.n

        wo_done = {}
        wo_free = {0: sT.n, 1: sT.n}
        wo_evac = [sV.n]
        oout_done = [0]
        for ocg in range(4):
            def _w_xo(s, ocg=ocg):
                if oout_done[0]:
                    s.wait_ge(sX.h, oout_done[0])
                else:
                    s.wait_ge(sV.h, attn_done)
                s.dma_start(
                    out=o32[:, :].rearrange("p (i t) -> p i t", i=4),
                    in_=src.rearrange("c p t -> p c t")[:, ocg * 4:(ocg + 1) * 4, :]
                ).then_inc(sX.h, 16)
                sX.n += 16

            B_("sync", _w_xo)
            oxo_done = sX.n

            wo_guard = wo_evac[0]
            for hc in range(CC):
                wpar = hc % 2

                def _w_gw(g, ocg=ocg, hc=hc, wpar=wpar):
                    if wo_free[wpar]:
                        g.wait_ge(sT.h, wo_free[wpar])
                    g.dma_start(out=wqsb[:, wpar * 2048:wpar * 2048 + 512],
                                in_=wo_in[hc, ocg]).then_inc(sW.h, 16)
                    sW.n += 16
                    wo_done[(ocg, hc)] = sW.n

                B_("gpsimd", _w_gw)

                def _w_t(t, ocg=ocg, hc=hc, wpar=wpar, wo_guard=wo_guard):
                    t.wait_ge(sW.h, wo_done[(ocg, hc)])
                    if hc == 0:
                        t.wait_ge(sV.h, wo_guard)
                    inst = None
                    for oc4 in range(4):
                        for u in range(NU):
                            inst = t.matmul(
                                P[oc4 * 2 + u][:, :],
                                wqsb[:, wpar * 2048 + oc4 * 128:
                                     wpar * 2048 + (oc4 + 1) * 128],
                                arena[:, AQ + hc * TC + u * TU:
                                      AQ + hc * TC + (u + 1) * TU],
                                start=(hc == 0), stop=(hc == CC - 1))
                    inst.then_inc(sT.h, 1)
                    sT.n += 1
                    wo_free[wpar] = sT.n

                B_("tensor", _w_t)
            w_t = sT.n

            def _w_v(v, ocg=ocg, w_t=w_t, oxo_done=oxo_done):
                v.wait_ge(sT.h, w_t)
                v.wait_ge(sX.h, oxo_done)
                inst = None
                for oc4 in range(4):
                    for u in range(NU):
                        inst = v.scalar_tensor_tensor(
                            o32[:, oc4 * 1024 + u * 512: oc4 * 1024 + (u + 1) * 512],
                            P[oc4 * 2 + u][:, :],
                            bo_sb[:, ocg * 4 + oc4: ocg * 4 + oc4 + 1],
                            o32[:, oc4 * 1024 + u * 512: oc4 * 1024 + (u + 1) * 512],
                            OP.add, OP.add)
                inst.then_inc(sV.h, 1)
                sV.n += 1
                wo_evac[0] = sV.n

            B_("vector", _w_v)

            def _w_out(s, ocg=ocg):
                s.wait_ge(sV.h, wo_evac[0])
                s.dma_start(
                    out=dst.rearrange("c p t -> p c t")[:, ocg * 4:(ocg + 1) * 4, :],
                    in_=o32[:, :].rearrange("p (i t) -> p i t", i=4)
                ).then_inc(sX.h, 16)
                sX.n += 16
                oout_done[0] = sX.n

                for i in range(4):
                    ready[ocg * 4 + i] = sX.n

            B_("sync", _w_out)
        return {"last": sX.n, "ready": ready, "xhat_free": attn_xhat_free,
                "scrbf_free": attn_scrbf_free}

    if tap == "comb":
        GLOB["gates_only"] = True
        emit_ffn(x0_in, y_out, first_call=True)
    elif tap == "comb2":
        GLOB["gates_only"] = True
        GLOB["go2"] = True
        r1 = emit_ffn(x0_in, x1_d, first_call=True)
        r2 = emit_attn(x1_d, x2_d, prev=r1)
        emit_ffn(x2_d, y_out, first_call=False, prev=r2)
    elif tap in ("qrope", "ctx", "kg", "kown"):
        GLOB["tap_qrope"] = tap == "qrope"
        GLOB["tap_ctx"] = tap == "ctx"
        GLOB["tap_kg"] = tap == "kg"
        GLOB["tap_kown"] = tap == "kown"
        r1 = emit_ffn(x0_in, x1_d, first_call=True)
        emit_attn(x1_d, y_out, prev=r1)
    elif tap == "ln1":
        ln_done = emit_ln(x0_in)
        def _dbg(g):
            g.wait_ge(sV.h, sV.n)
            for cc in range(CC):
                g.dma_start(out=y_out[cc],
                            in_=xhat[:, cc * TC:(cc + 1) * TC]).then_inc(sX.h, 16)
                sX.n += 16
        B_("gpsimd", _dbg)
    elif tap == "x1":
        emit_ffn(x0_in, y_out, first_call=True)
    elif tap == "x2":
        r1 = emit_ffn(x0_in, x1_d, first_call=True)
        emit_attn(x1_d, y_out, prev=r1)
    else:
        r1 = emit_ffn(x0_in, x1_d, first_call=True)
        r2 = emit_attn(x1_d, x2_d, prev=r1)
        emit_ffn(x2_d, y_out, first_call=False, prev=r2)

    final = sX.n

    def _fin(s):
        s.wait_ge(sX.h, final)

    B_("sync", _fin)
    ctx.close()
    return nc



def _prep_shared(I):
    f32 = np.float32
    Wq = np.asarray(I["L0_attn_Wq"], f32)
    Wk = np.asarray(I["L0_attn_Wk"], f32)
    Wv = np.asarray(I["L0_attn_Wv"], f32)
    Wo = np.asarray(I["L0_attn_Wo"], f32)
    bo = np.asarray(I["L0_attn_bo"], f32)
    gate_w = np.asarray(I["L0_ffn_gate_w"], f32)
    se_w = np.asarray(I["L0_ffn_se_gate_w"], f32)
    w_gate = np.asarray(I["L0_ffn_w_gate"], f32)
    w_up = np.asarray(I["L0_ffn_w_up"], f32)
    w_down = np.asarray(I["L0_ffn_w_down"], f32)
    sh_gate = np.asarray(I["L0_ffn_sh_gate"], f32)
    sh_up = np.asarray(I["L0_ffn_sh_up"], f32)
    sh_down = np.asarray(I["L0_ffn_sh_down"], f32)

    out = {}
    out["wq"] = np.ascontiguousarray(
        Wq.reshape(16, 128, 16, 128).transpose(2, 1, 0, 3)).reshape(16, 128, 2048)
    out["wk"] = np.ascontiguousarray(
        Wk.reshape(16, 128, 4, 128).transpose(2, 1, 0, 3)).reshape(4, 128, 2048)
    out["wv"] = np.ascontiguousarray(Wv.reshape(16, 128, 512))
    out["wo"] = np.ascontiguousarray(Wo.reshape(16, 128, 4, 512).transpose(0, 2, 1, 3))
    out["bo"] = np.ascontiguousarray(bo.reshape(16, 128).T)
    gws = np.concatenate([gate_w, se_w], 0)
    out["gwse"] = np.ascontiguousarray(gws.T.reshape(16, 128, 9).transpose(1, 0, 2))
    wg_all = np.concatenate([w_gate, sh_gate.reshape(2, 1024, 2048)], 0)
    wu_all = np.concatenate([w_up, sh_up.reshape(2, 1024, 2048)], 0)
    wg_p = wg_all.reshape(10, 8, 128, 16, 128).transpose(0, 1, 4, 3, 2).reshape(
        10, 8, 128, 2048)
    wu_p = wu_all.reshape(10, 8, 128, 16, 128).transpose(0, 1, 4, 3, 2).reshape(
        10, 8, 128, 2048)
    out["wgu"] = np.ascontiguousarray(np.concatenate([wg_p, wu_p], axis=3))
    wd_all = np.concatenate(
        [w_down, np.stack([sh_down[:, :1024], sh_down[:, 1024:]], 0)], 0)
    wd_p = wd_all.reshape(10, 4, 512, 8, 128).transpose(1, 0, 3, 4, 2)
    out["wd"] = np.ascontiguousarray(
        wd_p.reshape(4, 10, 2, 4, 128, 512).transpose(0, 1, 2, 4, 3, 5).reshape(
            4, 10, 2, 128, 2048))
    masks = []
    for jj in range(4):
        parts = []
        for m in range(4):
            if m > jj:
                parts.append(np.ones((128, 128), f32))
            elif m == jj:
                parts.append(np.triu(np.ones((128, 128), f32)))
            else:
                parts.append(np.zeros((128, 128), f32))
        masks.append(np.concatenate(parts, axis=1))
    out["masks"] = np.stack(masks, 0).astype(ml_dtypes.bfloat16)
    out["ident"] = np.eye(128, dtype=f32)
    pt = np.zeros((128, 128), f32)
    for i in range(64):
        pt[2 * i + 1, 2 * i] = -1.0
        pt[2 * i, 2 * i + 1] = 1.0
    out["ptmat"] = pt
    esel = np.zeros((10, 1280), f32)
    for pe in range(10):
        esel[pe, pe * 128:(pe + 1) * 128] = 1.0
    out["esel"] = esel.astype(ml_dtypes.bfloat16)
    out["onesm"] = np.ones((1, 128), f32)
    out["onescb"] = np.ones((128, 1), f32).astype(ml_dtypes.bfloat16)
    out["neg30"] = np.full((128, 1), -30.0, f32)
    return out


def _prep_core(I, i):
    f32 = np.float32
    tiles = core_tiles(i)
    hsT = np.asarray(I["L0_attn_hidden_states"], f32)[0].T
    x0 = np.concatenate([hsT[:, 128 * g:128 * (g + 1)] for g in tiles], axis=1)
    out = {"x0": np.ascontiguousarray(x0).reshape(16, 128, 1024)}
    pos = np.concatenate([np.arange(128 * g, 128 * (g + 1), dtype=f32) for g in tiles])
    out["qpos"] = np.ascontiguousarray(np.broadcast_to(pos[None, :], (128, 1024)))
    p = np.arange(128, dtype=f32)[:, None]
    kt = np.arange(64, dtype=f32)[None, :]
    gA = np.where(kt < 4 * i, kt * 128 + p, BIG_NEG)
    out["gridA"] = np.ascontiguousarray(gA.astype(f32))
    gB = np.where(kt < 60 - 4 * i, kt * 128 + p, BIG_NEG)
    out["gridB"] = np.ascontiguousarray(gB.astype(f32))
    out["gridOA"] = np.ascontiguousarray(
        (np.array(tiles[:4], f32)[None, :] * 128 + p).astype(f32))
    out["gridOB"] = np.ascontiguousarray(
        (np.array(tiles[4:], f32)[None, :] * 128 + p).astype(f32))
    inv = 1.0 / 10000.0 ** (np.arange(0, 128, 2, dtype=np.float64) / 128)
    invr = np.repeat(inv, 2)
    fr = invr[:, None] * pos[None, :].astype(np.float64)
    out["cosl"] = np.ascontiguousarray(np.cos(fr).astype(f32))
    out["sinl"] = np.ascontiguousarray(np.sin(fr).astype(f32))
    return out


def kernel(**inputs):
    shared = _prep_shared(inputs)
    in_maps = []
    for i in range(NCORES):
        m = dict(shared)
        m.update(_prep_core(inputs, i))
        in_maps.append(m)
    nc = build_nc(TAP)
    res = run_bass_kernel_spmd(nc, in_maps, core_ids=list(range(NCORES)))
    out = np.zeros((8192, 2048), np.float32)
    for i in range(NCORES):
        y = np.asarray(res.results[i]["y"], np.float32).reshape(2048, 1024)
        for j, g in enumerate(core_tiles(i)):
            out[128 * g:128 * (g + 1), :] = y[:, 128 * j:128 * (j + 1)].T
    return out[None, :, :]




# version 17
# speedup vs baseline: 1.2388x; 1.0867x over previous
import contextlib
import math
import numpy as np
import ml_dtypes

import concourse.bass as bass
import concourse.mybir as mybir
from concourse.bass_utils import run_bass_kernel_spmd

F32 = mybir.dt.float32
BF16 = mybir.dt.bfloat16
AF = mybir.ActivationFunctionType
OP = mybir.AluOpType

NCORES = 8
CC = 16
TC = 1024
NU = 2
TU = 512
NE = 10
HJ = 8
H = 16
G = 4
CAPA, CAPB = 28, 60
A_SCALE = float(1.0 / (30.0 * math.sqrt(128.0)))
SLOPES = [float(2.0 ** (-(h + 1) * 0.5)) for h in range(H)]
BIG_NEG = -1.0e30

TAP = None


def core_tiles(i):
    return [4 * i + j for j in range(4)] + [60 - 4 * i + j for j in range(4)]


class Ctr:
    def __init__(self, h):
        self.h = h
        self.n = 0

    def inc(self, inst, k=1):
        inst.then_inc(self.h, k)
        self.n += k
        return self.n


def build_nc(tap=None):
    nc = bass.Bass("TRN2", target_bir_lowering=False, debug=False)
    ap = lambda t: t.ap()

    x0_in = ap(nc.dram_tensor("x0", [CC, 128, TC], F32, kind="ExternalInput"))
    wq_in = ap(nc.dram_tensor("wq", [H, 128, CC * 128], F32, kind="ExternalInput"))
    wk_in = ap(nc.dram_tensor("wk", [G, 128, CC * 128], F32, kind="ExternalInput"))
    wv_in = ap(nc.dram_tensor("wv", [CC, 128, 512], F32, kind="ExternalInput"))
    wo_in = ap(nc.dram_tensor("wo", [CC, 4, 128, 512], F32, kind="ExternalInput"))
    bo_in = ap(nc.dram_tensor("bo", [128, 16], F32, kind="ExternalInput"))
    gwse_in = ap(nc.dram_tensor("gwse", [128, CC, 9], F32, kind="ExternalInput"))
    wgu_in = ap(nc.dram_tensor("wgu", [NE, HJ, 128, 4096], F32, kind="ExternalInput"))
    wd_in = ap(nc.dram_tensor("wd", [4, NE, 2, 128, 2048], F32, kind="ExternalInput"))
    qpos_in = ap(nc.dram_tensor("qpos", [128, TC], F32, kind="ExternalInput"))
    gridA_in = ap(nc.dram_tensor("gridA", [128, 64], F32, kind="ExternalInput"))
    gridB_in = ap(nc.dram_tensor("gridB", [128, 64], F32, kind="ExternalInput"))
    gridOA_in = ap(nc.dram_tensor("gridOA", [128, 4], F32, kind="ExternalInput"))
    gridOB_in = ap(nc.dram_tensor("gridOB", [128, 4], F32, kind="ExternalInput"))
    cos_in = ap(nc.dram_tensor("cosl", [128, TC], F32, kind="ExternalInput"))
    sin_in = ap(nc.dram_tensor("sinl", [128, TC], F32, kind="ExternalInput"))
    masks_in = ap(nc.dram_tensor("masks", [4, 128, 512], BF16, kind="ExternalInput"))
    ident_in = ap(nc.dram_tensor("ident", [128, 128], F32, kind="ExternalInput"))
    pt_in = ap(nc.dram_tensor("ptmat", [128, 128], F32, kind="ExternalInput"))
    esel_in = ap(nc.dram_tensor("esel", [NE, NE * 128], BF16, kind="ExternalInput"))
    onesm_in = ap(nc.dram_tensor("onesm", [1, 128], F32, kind="ExternalInput"))
    onescb_in = ap(nc.dram_tensor("onescb", [128, 1], BF16, kind="ExternalInput"))
    neg30_in = ap(nc.dram_tensor("neg30", [128, 1], F32, kind="ExternalInput"))

    y_out = ap(nc.dram_tensor("y", [CC, 128, TC], F32, kind="ExternalOutput"))

    x1_d = ap(nc.dram_tensor("x1d", [CC, 128, TC], F32, kind="Internal"))
    x2_d = ap(nc.dram_tensor("x2d", [CC, 128, TC], F32, kind="Internal"))
    wgubf_d = ap(nc.dram_tensor("wgubf", [NE, HJ, 128, 4096], BF16, kind="Internal"))
    wdbf_d = ap(nc.dram_tensor("wdbf", [4, NE, 2, 128, 2048], BF16, kind="Internal"))
    ksend_d = ap(nc.dram_tensor("ksend", [G * 128, TC], BF16, kind="Internal"))
    vsend_d = ap(nc.dram_tensor("vsend", [TC, 512], BF16, kind="Internal"))
    krecv_d = ap(nc.dram_tensor("krecv", [NCORES, G * 128, TC], BF16, kind="Internal"))
    vrecv_d = ap(nc.dram_tensor("vrecv", [NCORES, TC, 512], BF16, kind="Internal"))

    ctx = contextlib.ExitStack()
    E = ctx.enter_context

    xhat = E(nc.sbuf_tensor("xhat", [128, CC * TC], BF16))
    arena = E(nc.sbuf_tensor("arena", [128, 40960], BF16))
    AQ, AKG, AVG, AKO, AVO = 0, 16384, 24576, 32768, 36864

    scr32 = E(nc.sbuf_tensor("scr32", [128, 5120], F32))
    QRAW, T1C, T2C, COSC, SINC = 0, 512, 1024, 2048, 3072
    SPQ, RBX, TPAIR = 2048, 3072, 4096
    scrbf = E(nc.sbuf_tensor("scrbf", [128, 4096], BF16))
    o32 = E(nc.sbuf_tensor("o32", [128, 4096], F32))

    wgusb = E(nc.sbuf_tensor("wgusb", [128, 8192], BF16))
    wdsb = E(nc.sbuf_tensor("wdsb", [128, 4096], BF16))
    wqsb = E(nc.sbuf_tensor("wqsb", [128, 4096], BF16))
    gwsesb = E(nc.sbuf_tensor("gwsesb", [128, 144], BF16))

    masks_sb = E(nc.sbuf_tensor("masks_sb", [128, 2048], BF16))
    qpos_sb = E(nc.sbuf_tensor("qpos_sb", [128, TC], F32))
    grids_sb = E(nc.sbuf_tensor("grids_sb", [128, 136], F32))
    kcols_sb = E(nc.sbuf_tensor("kcols_sb", [128, 136], F32))
    ident_sb = E(nc.sbuf_tensor("ident_sb", [128, 128], F32))
    pt_sb = E(nc.sbuf_tensor("pt_sb", [128, 128], F32))
    esel_sb = E(nc.sbuf_tensor("esel_sb", [NE, NE * 128], BF16))
    onesm_sb = E(nc.sbuf_tensor("onesm_sb", [1, 128], F32))
    onescb_sb = E(nc.sbuf_tensor("onescb_sb", [128, 1], BF16))
    neg30_sb = E(nc.sbuf_tensor("neg30_sb", [128, 1], F32))
    bo_sb = E(nc.sbuf_tensor("bo_sb", [128, 16], F32))
    combfm = E(nc.sbuf_tensor("combfm", [NE, TC], BF16))
    comb8 = E(nc.sbuf_tensor("comb8", [128, 16], F32))
    gsc = E(nc.sbuf_tensor("gsc", [128, 48], F32))
    cols = E(nc.sbuf_tensor("cols", [128, 16], F32))


    P = [E(nc.psum_tensor(f"P{i}", [128, 512], F32)) for i in range(8)]

    sW = Ctr(E(nc.semaphore("sW")))
    sX = Ctr(E(nc.semaphore("sX")))
    sT = Ctr(E(nc.semaphore("sT")))
    sV = Ctr(E(nc.semaphore("sV")))
    sS = Ctr(E(nc.semaphore("sS")))
    sVV = Ctr(E(nc.semaphore("sVV")))
    sCK = Ctr(E(nc.semaphore("sCK")))
    sCV = Ctr(E(nc.semaphore("sCV")))

    block = E(nc.Block())

    GLOB = {"wbB": 0}

    def B_(name, fn):
        getattr(block, name)(fn)

    def vsync(v):
        inst = v.engine_nop()
        inst.then_inc(sVV.h, 1)
        sVV.n += 1
        v.wait_ge(sVV.h, sVV.n)

    def _consts(s):
        for dst, src in (
            (masks_sb[:, :].rearrange("p (m c) -> p m c", m=4), masks_in.rearrange("m p c -> p m c")),
            (qpos_sb[:, :], qpos_in),
            (grids_sb[:, 0:64], gridA_in),
            (grids_sb[:, 64:128], gridB_in),
            (grids_sb[:, 128:132], gridOA_in),
            (grids_sb[:, 132:136], gridOB_in),
            (ident_sb[:, :], ident_in),
            (pt_sb[:, :], pt_in),
            (esel_sb[:, :], esel_in),
            (onesm_sb[:, :], onesm_in),
            (onescb_sb[:, :], onescb_in),
            (neg30_sb[:, :], neg30_in),
            (bo_sb[:, :], bo_in),
        ):
            s.dma_start(out=dst, in_=src).then_inc(sX.h, 16)
            sX.n += 16

    B_("sync", _consts)
    consts_done = sX.n

    def _gwse(g):
        g.dma_start(out=gwsesb[:, :].rearrange("p (c n) -> p c n", n=9),
                    in_=gwse_in).then_inc(sW.h, 16)
        sW.n += 16

    B_("gpsimd", _gwse)
    gwse_done = sW.n

    def emit_ln(src, prev=None):
        ln_sv, ln_ss, ln_st, ln_sx = sV.n, sS.n, sT.n, sX.n
        src_ready = prev.get("ready") if prev else None
        xhat_free = prev.get("xhat_free", 0) if prev else 0
        scrbf_free = prev.get("scrbf_free", 0) if prev else 0
        sS_base = sS.n
        p1_sc = [sS_base + cc + 1 for cc in range(CC)]
        sT_base = sT.n
        p1_t = [sT_base + cc + 1 for cc in range(CC)]
        dma1 = []

        def _p1_sync(s):
            s.wait_ge(sS.h, ln_ss)
            s.wait_ge(sV.h, ln_sv)
            if src_ready is None:
                s.wait_ge(sT.h, ln_st)
                s.wait_ge(sX.h, ln_sx)
            for cc in range(CC):
                if src_ready is not None:
                    s.wait_ge(sX.h, src_ready[cc])
                if cc >= 2:
                    s.wait_ge(sS.h, p1_sc[cc - 2])
                s.dma_start(out=scr32[:, (cc % 2) * 1024:(cc % 2 + 1) * 1024],
                            in_=src[cc]).then_inc(sX.h, 16)
                sX.n += 16
                dma1.append(sX.n)

        B_("sync", _p1_sync)

        def _p1_scalar(sc):
            if xhat_free:
                sc.wait_ge(sT.h, xhat_free)
            if scrbf_free:
                sc.wait_ge(sT.h, scrbf_free)
            for cc in range(CC):
                par = cc % 2
                sc.wait_ge(sX.h, dma1[cc])
                if cc >= 2:
                    sc.wait_ge(sT.h, p1_t[cc - 2])
                sc.activation(xhat[:, cc * TC:(cc + 1) * TC],
                              scr32[:, par * 1024:(par + 1) * 1024], AF.Copy)
                inst = sc.activation(scrbf[:, par * 1024:(par + 1) * 1024],
                                     scr32[:, par * 1024:(par + 1) * 1024],
                                     AF.Square)
                inst.then_inc(sS.h, 1)
                sS.n += 1

        B_("scalar", _p1_scalar)
        assert sS.n == p1_sc[-1]

        def _p1_tensor(t):
            t.wait_ge(sX.h, consts_done)
            t.wait_ge(sV.h, ln_sv)
            for cc in range(CC):
                par = cc % 2
                t.wait_ge(sS.h, p1_sc[cc])
                st, sp = cc == 0, cc == CC - 1
                inst = None
                for u in range(NU):
                    t.matmul(P[u][0:1, :], onescb_sb[:, 0:1],
                             xhat[:, cc * TC + u * TU: cc * TC + (u + 1) * TU],
                             start=st, stop=sp)
                    inst = t.matmul(P[2 + u][0:1, :], onescb_sb[:, 0:1],
                                    scrbf[:, par * 1024 + u * TU: par * 1024 + (u + 1) * TU],
                                    start=st, stop=sp)
                inst.then_inc(sT.h, 1)
                sT.n += 1

        B_("tensor", _p1_tensor)
        assert sT.n == p1_t[-1]
        stats_done = sT.n

        TMPA, TMPB = 3072 - 2048, 3584 - 2048
        bcast_t = 0
        for u in range(NU):
            def _r_v1(v, u=u):
                if u == 0:
                    v.wait_ge(sT.h, stats_done)
                inst = v.tensor_scalar_mul(scr32[0:1, 2048 + u * 512:2048 + (u + 1) * 512],
                                           P[u][0:1, :], 1.0 / 2048.0)
                inst.then_inc(sV.h, 1)
                sV.n += 1

            B_("vector", _r_v1)
            mean_v = sV.n

            def _r_s1(sc, u=u, bcast_t=bcast_t, mean_v=mean_v):
                sc.wait_ge(sV.h, mean_v)
                if bcast_t:
                    sc.wait_ge(sT.h, bcast_t)
                inst = sc.activation(scr32[0:1, 2048 + TMPA:2048 + TMPA + 512],
                                     scr32[0:1, 2048 + u * 512:2048 + (u + 1) * 512], AF.Square)
                inst.then_inc(sS.h, 1)
                sS.n += 1

            B_("scalar", _r_s1)
            msq_s = sS.n

            def _r_v2(v, u=u, msq_s=msq_s):
                v.wait_ge(sS.h, msq_s)
                v.scalar_tensor_tensor(scr32[0:1, 2048 + TMPB:2048 + TMPB + 512],
                                       P[2 + u][0:1, :], 1.0 / 2048.0,
                                       scr32[0:1, 2048 + TMPA:2048 + TMPA + 512],
                                       OP.mult, OP.subtract)
                vsync(v)
                inst = v.tensor_scalar_add(scr32[0:1, 2048 + TMPB:2048 + TMPB + 512],
                                           scr32[0:1, 2048 + TMPB:2048 + TMPB + 512], 1.0e-5)
                inst.then_inc(sV.h, 1)
                sV.n += 1

            B_("vector", _r_v2)
            var_v = sV.n

            def _r_s2(sc, var_v=var_v):
                sc.wait_ge(sV.h, var_v)
                inst = sc.activation(scr32[0:1, 2048 + TMPA:2048 + TMPA + 512],
                                     scr32[0:1, 2048 + TMPB:2048 + TMPB + 512], AF.Sqrt)
                inst.then_inc(sS.h, 1)
                sS.n += 1

            B_("scalar", _r_s2)
            sd_s = sS.n

            def _r_v3(v, sd_s=sd_s):
                v.wait_ge(sS.h, sd_s)
                inst = v.reciprocal(scr32[0:1, 2048 + TMPB:2048 + TMPB + 512],
                                    scr32[0:1, 2048 + TMPA:2048 + TMPA + 512])
                inst.then_inc(sV.h, 1)
                sV.n += 1

            B_("vector", _r_v3)
            rstd_v = sV.n

            def _bc_t(t, u=u, rstd_v=rstd_v):
                t.wait_ge(sV.h, rstd_v)
                t.matmul(P[4 + u][:, :], onesm_sb[0:1, :],
                         scr32[0:1, 2048 + u * 512:2048 + (u + 1) * 512], start=True, stop=True)
                inst = t.matmul(P[6 + u][:, :], onesm_sb[0:1, :],
                                scr32[0:1, 2048 + TMPB:2048 + TMPB + 512], start=True, stop=True)
                inst.then_inc(sT.h, 1)
                sT.n += 1

            B_("tensor", _bc_t)
            bcast_t = sT.n

        sV_base = sV.n
        p2_v = [sV_base + cc + 1 for cc in range(CC)]
        dma2 = []

        def _p2_sync(s):
            for cc in range(CC):
                if src_ready is not None:
                    s.wait_ge(sX.h, src_ready[cc])
                if cc >= 2:
                    s.wait_ge(sV.h, p2_v[cc - 2])
                s.dma_start(out=scr32[:, (cc % 2) * 1024:(cc % 2 + 1) * 1024],
                            in_=src[cc]).then_inc(sX.h, 16)
                sX.n += 16
                dma2.append(sX.n)

        B_("sync", _p2_sync)

        def _p2_v(v):
            v.wait_ge(sT.h, bcast_t)
            v.wait_ge(sS.h, ln_ss)
            for cc in range(CC):
                par = cc % 2
                v.wait_ge(sX.h, dma2[cc])
                inst = None
                for u in range(NU):
                    v.tensor_tensor(scr32[:, 4096 + u * 512:4096 + (u + 1) * 512],
                                    scr32[:, par * 1024 + u * TU: par * 1024 + (u + 1) * TU],
                                    P[4 + u][:, :], OP.subtract)
                    inst = v.tensor_tensor(
                        xhat[:, cc * TC + u * TU: cc * TC + (u + 1) * TU],
                        scr32[:, 4096 + u * 512:4096 + (u + 1) * 512],
                        P[6 + u][:, :], OP.mult)
                inst.then_inc(sV.h, 1)
                sV.n += 1

        B_("vector", _p2_v)
        assert sV.n == p2_v[-1]
        return sV.n

    def emit_ffn(src, dst, first_call, prev=None):
        ffn_t0 = sT.n
        wb_gate = 0 if first_call else GLOB["wbB"]
        ln_done = emit_ln(src, prev)
        ready = [0] * CC

        prev_gate_v = 0
        for ts in range(8):
            def _g_t1(t, ts=ts, prev_gate_v=prev_gate_v):
                if ts == 0:
                    t.wait_ge(sV.h, ln_done)
                    t.wait_ge(sW.h, gwse_done)
                else:
                    t.wait_ge(sV.h, prev_gate_v)
                inst = None
                for cc in range(CC):
                    inst = t.matmul(
                        P[0][:, 0:9],
                        xhat[:, cc * TC + ts * 128: cc * TC + (ts + 1) * 128],
                        gwsesb[:, cc * 9:(cc + 1) * 9],
                        start=(cc == 0), stop=(cc == CC - 1))
                inst.then_inc(sT.h, 1)
                sT.n += 1

            B_("tensor", _g_t1)
            g_t = sT.n

            def _g_v1(v, g_t=g_t):
                v.wait_ge(sT.h, g_t)
                v.reduce_max(cols[:, 0:1], P[0][:, 0:8], axis=mybir.AxisListType.X)
                vsync(v)
                v.tensor_scalar_mul(cols[:, 1:2], cols[:, 0:1], -1.0)
                vsync(v)
                v.tensor_single_scalar(gsc[:, 0:8], P[0][:, 0:8], cols[:, 0:1],
                                       OP.is_equal)
                vsync(v)
                v.scalar_tensor_tensor(gsc[:, 8:16], gsc[:, 0:8], BIG_NEG,
                                       P[0][:, 0:8], OP.mult, OP.add)
                vsync(v)
                v.reduce_max(cols[:, 2:3], gsc[:, 8:16], axis=mybir.AxisListType.X)
                vsync(v)
                inst = v.tensor_tensor(cols[:, 3:4], cols[:, 2:3], cols[:, 0:1],
                                       OP.subtract)
                inst.then_inc(sV.h, 1)
                sV.n += 1

            B_("vector", _g_v1)
            g_v1 = sV.n

            def _g_s1(sc, g_v1=g_v1):
                sc.wait_ge(sV.h, g_v1)
                sc.activation(cols[:, 4:5], cols[:, 3:4], AF.Exp)
                sc.activation(gsc[:, 16:24], P[0][:, 0:8], AF.Exp, bias=cols[:, 1:2])
                inst = sc.activation(cols[:, 5:6], P[0][:, 8:9], AF.Exp, scale=-1.0)
                inst.then_inc(sS.h, 1)
                sS.n += 1

            B_("scalar", _g_s1)
            g_s1 = sS.n

            def _g_v2(v, g_s1=g_s1):
                v.wait_ge(sS.h, g_s1)
                v.tensor_scalar_add(cols[:, 6:7], cols[:, 4:5], 1.0)
                v.tensor_single_scalar(gsc[:, 24:32], P[0][:, 0:8], cols[:, 2:3],
                                       OP.is_ge)
                vsync(v)
                v.reciprocal(cols[:, 7:8], cols[:, 6:7])
                v.tensor_tensor(gsc[:, 32:40], gsc[:, 16:24], gsc[:, 24:32], OP.mult)
                vsync(v)
                v.tensor_scalar(comb8[:, 0:8], gsc[:, 32:40], cols[:, 7:8], 0.5,
                                OP.mult, OP.mult)
                v.tensor_scalar_add(cols[:, 8:9], cols[:, 5:6], 1.0)
                vsync(v)
                v.reciprocal(cols[:, 9:10], cols[:, 8:9])
                vsync(v)
                v.tensor_scalar_mul(comb8[:, 8:9], cols[:, 9:10], 0.5)
                inst = v.tensor_scalar_mul(comb8[:, 9:10], cols[:, 9:10], 0.5)
                inst.then_inc(sV.h, 1)
                sV.n += 1

            B_("vector", _g_v2)
            g_v2 = sV.n

            def _g_t2(t, g_v2=g_v2):
                t.wait_ge(sV.h, g_v2)
                inst = t.transpose(P[1][0:NE, 0:128], comb8[:, 0:NE], ident_sb[:, :])
                inst.then_inc(sT.h, 1)
                sT.n += 1

            B_("tensor", _g_t2)
            g_t2 = sT.n

            def _g_v3(v, g_t2=g_t2, ts=ts):
                v.wait_ge(sT.h, g_t2)
                inst = v.tensor_copy(combfm[0:NE, ts * 128:(ts + 1) * 128],
                                     P[1][0:NE, 0:128])
                inst.then_inc(sV.h, 1)
                sV.n += 1

            B_("vector", _g_v3)
            prev_gate_v = sV.n
        comb_done = sV.n
        if GLOB.get("gates_only") and (first_call != bool(GLOB.get("go2"))):
            def _dbgc(g):
                g.wait_ge(sV.h, comb_done)
                g.dma_start(out=y_out[0][0:NE, :],
                            in_=combfm[0:NE, :]).then_inc(sX.h, 16)
                sX.n += 16
                for cx in range(8):
                    g.dma_start(out=y_out[1 + cx],
                                in_=xhat[:, cx * TC:(cx + 1) * TC]).then_inc(sX.h, 16)
                    sX.n += 16
            B_("gpsimd", _dbgc)
            return sX.n

        wbA = 0
        wbB = 0
        wg_free = {0: ffn_t0, 1: ffn_t0}
        wd_free = {0: ffn_t0, 1: ffn_t0}
        xo_done = {0: 0, 1: 0}
        out_done = {0: 0, 1: 0}
        for u in range(NU):
            a_entry_sv = sV.n
            bank_free = {0: None, 1: None, 2: None, 3: None, 4: None, 5: None}
            p_prev2 = {}
            combbc_done = [0, 0]
            wg_done = {}
            s1_done = {}
            wb_done = {}
            for pe in range(NE):
                def _a_tbc(t, pe=pe, u=u, a_entry_sv=a_entry_sv):
                    if pe == 0:
                        t.wait_ge(sV.h, a_entry_sv)
                    if combbc_done[pe % 2]:
                        t.wait_ge(sV.h, combbc_done[pe % 2])
                    inst = t.matmul(P[6 + pe % 2][:, :],
                                    esel_sb[0:NE, pe * 128:(pe + 1) * 128],
                                    combfm[0:NE, u * TU:(u + 1) * TU],
                                    start=True, stop=True)
                    inst.then_inc(sT.h, 1)
                    sT.n += 1

                B_("tensor", _a_tbc)
                bc_t = sT.n

                def _a_vbc(v, bc_t=bc_t, pe=pe):
                    v.wait_ge(sT.h, bc_t)
                    inst = v.tensor_copy(
                        scr32[:, 4096 + (pe % 2) * 512: 4096 + (pe % 2 + 1) * 512],
                        P[6 + pe % 2][:, :])
                    inst.then_inc(sV.h, 1)
                    sV.n += 1

                B_("vector", _a_vbc)
                combbc_done[pe % 2] = sV.n

                for j in range(HJ):
                    wpar = j % 2

                    def _a_gw(g, pe=pe, j=j, wpar=wpar, u=u):
                        if not first_call and u == 0 and pe == 0 and j == 0:
                            g.wait_ge(sX.h, wb_gate)
                        if first_call and u == 0:
                            prev = wb_done.get((pe * HJ + j) - 2)
                            if prev:
                                g.wait_ge(sX.h, prev)
                        if wg_free[wpar]:
                            g.wait_ge(sT.h, wg_free[wpar])
                        if first_call and u == 0:
                            g.dma_start(out=wgusb[:, wpar * 4096:(wpar + 1) * 4096],
                                        in_=wgu_in[pe, j]).then_inc(sW.h, 16)
                        else:
                            if first_call and u == 1 and pe == 0 and j == 0:
                                g.wait_ge(sX.h, wbA)
                            g.dma_start(out=wgusb[:, wpar * 4096:(wpar + 1) * 4096],
                                        in_=wgubf_d[pe, j]).then_inc(sW.h, 16)
                        sW.n += 16
                        wg_done[(pe, j)] = sW.n

                    B_("gpsimd", _a_gw)

                    if first_call and u == 0:
                        def _a_wb(s, pe=pe, j=j, wpar=wpar):
                            s.wait_ge(sW.h, wg_done[(pe, j)])
                            s.dma_start(out=wgubf_d[pe, j],
                                        in_=wgusb[:, wpar * 4096:(wpar + 1) * 4096]
                                        ).then_inc(sX.h, 16)
                            sX.n += 16
                            wb_done[pe * HJ + j] = sX.n

                        B_("sync", _a_wb)

                    def _a_t(t, pe=pe, j=j, wpar=wpar, u=u):
                        jp = (pe * HJ + j) % 3
                        t.wait_ge(sW.h, wg_done[(pe, j)])
                        if bank_free[jp * 2] is not None:
                            sem, cnt = bank_free[jp * 2]
                            t.wait_ge(sem.h, cnt)
                        inst = None
                        for cc in range(CC):
                            inst = t.matmul(
                                P[jp * 2][:, :],
                                wgusb[:, wpar * 4096 + cc * 128: wpar * 4096 + (cc + 1) * 128],
                                xhat[:, cc * TC + u * TU: cc * TC + (u + 1) * TU],
                                start=(cc == 0), stop=(cc == CC - 1))
                        inst.then_inc(sT.h, 1)
                        sT.n += 1
                        if bank_free[jp * 2 + 1] is not None:
                            sem, cnt = bank_free[jp * 2 + 1]
                            t.wait_ge(sem.h, cnt)
                        for cc in range(CC):
                            inst = t.matmul(
                                P[jp * 2 + 1][:, :],
                                wgusb[:, wpar * 4096 + 2048 + cc * 128: wpar * 4096 + 2048 + (cc + 1) * 128],
                                xhat[:, cc * TC + u * TU: cc * TC + (u + 1) * TU],
                                start=(cc == 0), stop=(cc == CC - 1))
                        inst.then_inc(sT.h, 1)
                        sT.n += 1
                        wg_free[wpar] = sT.n

                    B_("tensor", _a_t)
                    h1_t = sT.n - 1
                    h3_t = sT.n

                    def _a_s(sc, h1_t=h1_t, pe=pe, j=j):
                        jp = (pe * HJ + j) % 3
                        sc.wait_ge(sT.h, h1_t)
                        pv = p_prev2.get((pe * HJ + j) % 2)
                        if pv:
                            sc.wait_ge(sV.h, pv)
                        inst = sc.activation(
                            scr32[:, 2048 + (jp % 2) * 512: 2048 + (jp % 2 + 1) * 512],
                            P[jp * 2][:, :], AF.Silu)
                        inst.then_inc(sS.h, 1)
                        sS.n += 1

                    B_("scalar", _a_s)
                    s1_done[(pe, j)] = sS.n

                    def _a_v(v, pe=pe, j=j, h3_t=h3_t):
                        jp = (pe * HJ + j) % 3
                        v.wait_ge(sT.h, h3_t)
                        v.wait_ge(sV.h, combbc_done[pe % 2])
                        v.tensor_tensor(
                            scr32[:, 3072 + (jp % 2) * 512: 3072 + (jp % 2 + 1) * 512],
                            P[jp * 2 + 1][:, :],
                            scr32[:, 4096 + (pe % 2) * 512: 4096 + (pe % 2 + 1) * 512],
                            OP.mult)
                        v.wait_ge(sS.h, s1_done[(pe, j)])
                        inst = v.tensor_tensor(
                            arena[:, pe * 4096 + j * 512: pe * 4096 + (j + 1) * 512],
                            scr32[:, 2048 + (jp % 2) * 512: 2048 + (jp % 2 + 1) * 512],
                            scr32[:, 3072 + (jp % 2) * 512: 3072 + (jp % 2 + 1) * 512],
                            OP.mult)
                        inst.then_inc(sV.h, 1)
                        sV.n += 1

                    B_("vector", _a_v)
                    bank_free[((pe * HJ + j) % 3) * 2] = (sS, s1_done[(pe, j)])
                    bank_free[((pe * HJ + j) % 3) * 2 + 1] = (sV, sV.n)
                    p_prev2[(pe * HJ + j) % 2] = sV.n
            a_done = sV.n
            if u == NU - 1:
                xhat_free_out = sT.n
            if first_call and u == 0:
                wbA = sX.n

            wd_done = {}
            wdwb_done = {}
            evac_done = {0: 0, 1: 0}
            for cg in range(4):
                cp = cg % 2

                def _b_xo(s, cg=cg, cp=cp, u=u):
                    if out_done[cp]:
                        s.wait_ge(sX.h, out_done[cp])
                    else:
                        s.wait_ge(sV.h, ln_done)
                    s.dma_start(
                        out=o32[:, cp * 2048:(cp + 1) * 2048].rearrange(
                            "p (i t) -> p i t", i=4),
                        in_=src.rearrange("c p t -> p c t")[
                            :, cg * 4:(cg + 1) * 4, u * TU:(u + 1) * TU]
                    ).then_inc(sX.h, 16)
                    sX.n += 16
                    xo_done[cp] = sX.n

                B_("sync", _b_xo)

                for pe in range(NE):
                    for j in range(HJ):
                        wpar = (pe * HJ + j) % 2

                        def _b_gw(g, cg=cg, pe=pe, j=j, u=u):
                            if j % 4 != 0:
                                return
                            jg = j // 4
                            wpar = (cg * NE * 2 + pe * 2 + jg) % 2
                            if not first_call and u == 0 and cg == 0 and pe == 0 and j == 0:
                                g.wait_ge(sX.h, wb_gate)
                            if first_call and u == 0:
                                prev = wdwb_done.get((cg * NE + pe) * 2 + jg - 2)
                                if prev:
                                    g.wait_ge(sX.h, prev)
                            if wd_free[wpar]:
                                g.wait_ge(sT.h, wd_free[wpar])
                            if first_call and u == 0:
                                g.dma_start(out=wdsb[:, wpar * 2048:(wpar + 1) * 2048],
                                            in_=wd_in[cg, pe, jg]).then_inc(sW.h, 16)
                            else:
                                if first_call and u == 1 and cg == 0 and pe == 0 and j == 0:
                                    g.wait_ge(sX.h, wbB)
                                g.dma_start(out=wdsb[:, wpar * 2048:(wpar + 1) * 2048],
                                            in_=wdbf_d[cg, pe, jg]).then_inc(sW.h, 16)
                            sW.n += 16
                            wd_done[(cg, pe, jg)] = sW.n

                        B_("gpsimd", _b_gw)

                        if first_call and u == 0:
                            def _b_wb(s, cg=cg, pe=pe, j=j):
                                if j % 4 != 0:
                                    return
                                jg = j // 4
                                wpar = (cg * NE * 2 + pe * 2 + jg) % 2
                                s.wait_ge(sW.h, wd_done[(cg, pe, jg)])
                                s.dma_start(out=wdbf_d[cg, pe, jg],
                                            in_=wdsb[:, wpar * 2048:(wpar + 1) * 2048]
                                            ).then_inc(sX.h, 16)
                                sX.n += 16
                                wdwb_done[(cg * NE + pe) * 2 + jg] = sX.n

                            B_("sync", _b_wb)

                        def _b_t(t, cg=cg, pe=pe, j=j, cp=cp):
                            jg = j // 4
                            wpar = (cg * NE * 2 + pe * 2 + jg) % 2
                            t.wait_ge(sW.h, wd_done[(cg, pe, jg)])
                            if pe == 0 and j == 0:
                                t.wait_ge(sV.h, a_done)
                                if evac_done[cp]:
                                    t.wait_ge(sV.h, evac_done[cp])
                            first = pe == 0 and j == 0
                            last = pe == NE - 1 and j == HJ - 1
                            inst = None
                            for i in range(4):
                                inst = t.matmul(
                                    P[cp * 4 + i][:, :],
                                    wdsb[:, wpar * 2048 + (j % 4) * 512 + i * 128:
                                         wpar * 2048 + (j % 4) * 512 + (i + 1) * 128],
                                    arena[:, pe * 4096 + j * 512: pe * 4096 + (j + 1) * 512],
                                    start=first, stop=last)
                            inst.then_inc(sT.h, 1)
                            sT.n += 1
                            if j % 4 == 3:
                                wd_free[wpar] = sT.n

                        B_("tensor", _b_t)
                b_t = sT.n

                def _b_v(v, cp=cp, b_t=b_t):
                    v.wait_ge(sT.h, b_t)
                    v.wait_ge(sX.h, xo_done[cp])
                    inst = None
                    for i in range(4):
                        inst = v.tensor_tensor(
                            o32[:, cp * 2048 + i * 512: cp * 2048 + (i + 1) * 512],
                            P[cp * 4 + i][:, :],
                            o32[:, cp * 2048 + i * 512: cp * 2048 + (i + 1) * 512],
                            OP.add)
                    inst.then_inc(sV.h, 1)
                    sV.n += 1
                    evac_done[cp] = sV.n

                B_("vector", _b_v)

                def _b_out(s, cg=cg, cp=cp, u=u):
                    s.wait_ge(sV.h, evac_done[cp])
                    s.dma_start(
                        out=dst.rearrange("c p t -> p c t")[
                            :, cg * 4:(cg + 1) * 4, u * TU:(u + 1) * TU],
                        in_=o32[:, cp * 2048:(cp + 1) * 2048].rearrange(
                            "p (i t) -> p i t", i=4)).then_inc(sX.h, 16)
                    sX.n += 16
                    out_done[cp] = sX.n

                B_("sync", _b_out)
                if u == NU - 1:
                    for i in range(4):
                        ready[cg * 4 + i] = sX.n
            if first_call and u == 0:
                wbB = sX.n
                GLOB["wbB"] = sX.n
        return {"last": sX.n, "ready": ready, "xhat_free": xhat_free_out,
                "scrbf_free": 0}

    def emit_attn(src, dst, prev=None):
        ln_done = emit_ln(src, prev)
        ready = [0] * CC

        cs_sv, cs_ss = sV.n, sS.n

        def _cs(s):
            s.wait_ge(sV.h, cs_sv)
            s.wait_ge(sS.h, cs_ss)
            s.dma_start(out=scr32[:, COSC:COSC + 1024], in_=cos_in).then_inc(sX.h, 16)
            sX.n += 16
            s.dma_start(out=scr32[:, SINC:SINC + 1024], in_=sin_in).then_inc(sX.h, 16)
            sX.n += 16

        B_("sync", _cs)
        cs_done = sX.n

        st = {"pq_free": {0: 0, 1: 0}, "qraw_free": {0: 0, 1: 0},
              "pr_free": {0: 0, 1: 0}, "idx": 0}

        def proj_rope(n_heads, w_in, dst_col):
            wq_done = {}
            wfree = {0: sT.n, 1: sT.n}
            for h in range(n_heads):
                wpar = h % 2

                def _pw(g, h=h, wpar=wpar):
                    if wfree[wpar]:
                        g.wait_ge(sT.h, wfree[wpar])
                    g.dma_start(out=wqsb[:, wpar * 2048:(wpar + 1) * 2048],
                                in_=w_in[h]).then_inc(sW.h, 16)
                    sW.n += 16
                    wq_done[h] = sW.n

                B_("gpsimd", _pw)

                for u in range(NU):
                    par = st["idx"] % 2
                    st["idx"] += 1

                    def _pt(t, h=h, u=u, wpar=wpar):
                        if h == 0 and u == 0:
                            t.wait_ge(sV.h, ln_done)
                        t.wait_ge(sW.h, wq_done[h])
                        if st["pq_free"][u]:
                            t.wait_ge(sS.h, st["pq_free"][u])
                        inst = None
                        for cc in range(CC):
                            inst = t.matmul(
                                P[u][:, :],
                                wqsb[:, wpar * 2048 + cc * 128: wpar * 2048 + (cc + 1) * 128],
                                xhat[:, cc * TC + u * TU: cc * TC + (u + 1) * TU],
                                start=(cc == 0), stop=(cc == CC - 1))
                        inst.then_inc(sT.h, 1)
                        sT.n += 1

                    B_("tensor", _pt)
                    mm_t = sT.n
                    if u == NU - 1:
                        wfree[wpar] = sT.n

                    def _ps(sc, mm_t=mm_t, u=u, par=par):
                        sc.wait_ge(sT.h, mm_t)
                        if st["qraw_free"][par]:
                            sc.wait_ge(sT.h, st["qraw_free"][par])
                        if st["pr_free"][par]:
                            sc.wait_ge(sV.h, st["pr_free"][par])
                        inst = sc.activation(scr32[:, par * 512:par * 512 + 512],
                                             P[u][:, :], AF.Copy)
                        inst.then_inc(sS.h, 1)
                        sS.n += 1
                        st["pq_free"][u] = sS.n

                    B_("scalar", _ps)
                    cp_s = sS.n

                    def _pt2(t, cp_s=cp_s, par=par):
                        t.wait_ge(sS.h, cp_s)
                        if st["pr_free"][par]:
                            t.wait_ge(sV.h, st["pr_free"][par])
                        inst = t.matmul(P[2 + par][:, :], pt_sb[:, :],
                                        scr32[:, par * 512:par * 512 + 512],
                                        start=True, stop=True)
                        inst.then_inc(sT.h, 1)
                        sT.n += 1
                        st["qraw_free"][par] = sT.n

                    B_("tensor", _pt2)
                    rot_t = sT.n

                    def _pv(v, h=h, u=u, rot_t=rot_t, par=par):
                        v.wait_ge(sX.h, cs_done)
                        v.wait_ge(sT.h, rot_t)
                        v.tensor_tensor(scr32[:, 1024:1536],
                                        scr32[:, par * 512:par * 512 + 512],
                                        scr32[:, COSC + u * TU:COSC + (u + 1) * TU],
                                        OP.mult)
                        v.tensor_tensor(scr32[:, 1536:2048], P[2 + par][:, :],
                                        scr32[:, SINC + u * TU:SINC + (u + 1) * TU],
                                        OP.mult)
                        vsync(v)
                        inst = v.tensor_tensor(
                            arena[:, dst_col + h * TC + u * TU:
                                  dst_col + h * TC + (u + 1) * TU],
                            scr32[:, 1024:1536], scr32[:, 1536:2048], OP.add)
                        inst.then_inc(sV.h, 1)
                        sV.n += 1
                        st["pr_free"][par] = sV.n

                    B_("vector", _pv)
            return sV.n

        q_done = proj_rope(H, wq_in, AQ)
        k_done = proj_rope(G, wk_in, AKO)
        if GLOB.get("tap_qrope"):
            def _dbgq(g):
                g.wait_ge(sV.h, q_done)
                for hx in range(16):
                    g.dma_start(out=y_out[hx],
                                in_=arena[:, AQ + hx * TC:AQ + (hx + 1) * TC]
                                ).then_inc(sX.h, 16)
                    sX.n += 16
            B_("gpsimd", _dbgq)
            return sX.n

        wv_done = {}
        wv_free = {0: sT.n, 1: sT.n, 2: sT.n, 3: sT.n}
        pv_free = {0: 0, 1: 0}
        for ts in range(8):
            for q2 in range(8):
                qp = q2 % 4

                def _vw(g, ts=ts, q2=q2, qp=qp):
                    if wv_free[qp]:
                        g.wait_ge(sT.h, wv_free[qp])
                    g.dma_start(
                        out=wqsb[:, qp * 1024:(qp + 1) * 1024].rearrange(
                            "p (c d) -> p c d", c=2),
                        in_=wv_in[q2 * 2:(q2 + 1) * 2].rearrange("c p d -> p c d")).then_inc(sW.h, 16)
                    sW.n += 16
                    wv_done[(ts, q2)] = sW.n

                B_("gpsimd", _vw)

                def _vt(t, ts=ts, q2=q2, qp=qp):
                    if ts == 0 and q2 == 0:
                        t.wait_ge(sV.h, ln_done)
                    t.wait_ge(sW.h, wv_done[(ts, q2)])
                    if q2 == 0 and pv_free[ts % 2]:
                        t.wait_ge(sV.h, pv_free[ts % 2])
                    inst = None
                    for c2 in range(2):
                        cc = q2 * 2 + c2
                        inst = t.matmul(
                            P[4 + ts % 2][:, :],
                            xhat[:, cc * TC + ts * 128: cc * TC + (ts + 1) * 128],
                            wqsb[:, qp * 1024 + c2 * 512: qp * 1024 + (c2 + 1) * 512],
                            start=(cc == 0), stop=(cc == CC - 1))
                    inst.then_inc(sT.h, 1)
                    sT.n += 1
                    wv_free[qp] = sT.n

                B_("tensor", _vt)
            mm_t = sT.n

            def _vv(v, ts=ts, mm_t=mm_t):
                v.wait_ge(sT.h, mm_t)
                inst = v.tensor_copy(arena[:, AVO + ts * 512:AVO + (ts + 1) * 512],
                                     P[4 + ts % 2][:, :])
                inst.then_inc(sV.h, 1)
                sV.n += 1
                pv_free[ts % 2] = sV.n

            B_("vector", _vv)
        v_done = sV.n
        attn_xhat_free = sT.n

        def _kvsend(s):
            s.wait_ge(sV.h, k_done)
            s.dma_start(out=ksend_d.rearrange("(g p) t -> p g t", g=G),
                        in_=arena[:, AKO:AKO + 4096].rearrange("p (g t) -> p g t", g=G)
                        ).then_inc(sX.h, 16)
            sX.n += 16
            s.wait_ge(sV.h, v_done)
            s.dma_start(out=vsend_d.rearrange("(ts p) d -> p ts d", ts=8),
                        in_=arena[:, AVO:AVO + 4096].rearrange("p (ts d) -> p ts d", ts=8)
                        ).then_inc(sX.h, 16)
            sX.n += 16

        if GLOB.get("tap_kown"):
            def _dbgko(g):
                g.wait_ge(sV.h, k_done)
                for cx in range(4):
                    g.dma_start(out=y_out[cx],
                                in_=arena[:, AKO + cx * 1024:AKO + (cx + 1) * 1024]
                                ).then_inc(sX.h, 16)
                    sX.n += 16
                g.wait_ge(sV.h, v_done)
                for cx in range(4):
                    g.dma_start(out=y_out[4 + cx],
                                in_=arena[:, AVO + cx * 1024:AVO + (cx + 1) * 1024]
                                ).then_inc(sX.h, 16)
                    sX.n += 16
                for cx in range(8):
                    g.dma_start(out=y_out[8 + cx],
                                in_=xhat[:, cx * TC:(cx + 1) * TC]
                                ).then_inc(sX.h, 16)
                    sX.n += 16
            B_("gpsimd", _dbgko)
            return sX.n

        B_("sync", _kvsend)
        kv_sent = sX.n

        def _cc(g):
            g.wait_ge(sX.h, kv_sent)
            g.collective_compute("AllGather", OP.bypass,
                                 replica_groups=[list(range(NCORES))],
                                 ins=[ksend_d.opt()], outs=[krecv_d.opt()]
                                 ).then_inc(sCK.h, 1)
            sCK.n += 1
            g.collective_compute("AllGather", OP.bypass,
                                 replica_groups=[list(range(NCORES))],
                                 ins=[vsend_d.opt()], outs=[vrecv_d.opt()]
                                 ).then_inc(sCV.h, 1)
            sCV.n += 1

        B_("gpsimd", _cc)
        ck_wait = sCK.n
        cv_wait = sCV.n

        inner_sv, inner_ss = sV.n, sS.n

        def _inner_entry(t):
            t.wait_ge(sV.h, inner_sv)
            t.wait_ge(sS.h, inner_ss)

        B_("tensor", _inner_entry)

        kg_free = 0
        sp_free = {0: 0, 1: 0}
        pt_free = {0: 0, 1: 0}
        ps_free = {0: 0, 1: 0, 2: 0, 3: 0}
        pend_se = []
        last_rb_t = [0]

        def flush_se():
            while pend_se:
                se = pend_se.pop(0)
                h_, span_, rec_v_, cb_, sb_ = se

                def _ctx_t(t, rec_v=rec_v_, sb=sb_):
                    t.wait_ge(sV.h, rec_v)
                    inst = t.matmul(P[sb][:, :], onesm_sb[0:1, :],
                                    scr32[0:1, 3584:4096], start=True, stop=True)
                    inst.then_inc(sT.h, 1)
                    sT.n += 1

                B_("tensor", _ctx_t)
                rb_t = sT.n
                last_rb_t[0] = rb_t

                def _ctx_v2(v, h=h_, span=span_, rb_t=rb_t, cb=cb_, sb=sb_):
                    v.wait_ge(sT.h, rb_t)
                    v.tensor_copy(scr32[:, RBX:RBX + 512], P[sb][:, :])
                    vsync(v)
                    inst = v.tensor_tensor(
                        arena[:, AQ + h * TC + span * TU:
                              AQ + h * TC + (span + 1) * TU],
                        P[cb][:, :], scr32[:, RBX:RBX + 512], OP.mult)
                    inst.then_inc(sV.h, 1)
                    sV.n += 1

                B_("vector", _ctx_v2)
                ctx_free[cb_] = sV.n
                sums_free[span_] = sV.n
        ctx_free = {0: 0, 1: 0}
        sums_free = {0: 0, 1: 0}
        rb_free = 0

        for g in range(G):
            def _kvload(s, g=g, kg_free=kg_free):
                s.wait_ge(sCK.h, ck_wait)
                s.wait_ge(sCV.h, cv_wait)
                if kg_free:
                    s.wait_ge(sT.h, kg_free)
                for r in range(NCORES):
                    for half in range(2):
                        t0 = 4 * r if half == 0 else 60 - 4 * r
                        s.dma_start(
                            out=arena[:, AKG + t0 * 128: AKG + (t0 + 4) * 128],
                            in_=krecv_d[r, g * 128:(g + 1) * 128,
                                        half * 512:(half + 1) * 512]
                        ).then_inc(sX.h, 16)
                        sX.n += 16
                        s.dma_start(
                            out=arena[:, AVG:AVG + 8192].rearrange(
                                "p (t d) -> p t d", d=128)[:, t0:t0 + 4, :],
                            in_=vrecv_d[r].rearrange(
                                "(lt p) (gg d) -> p lt gg d", lt=8, gg=4)[
                                :, half * 4:(half + 1) * 4, g, :]
                        ).then_inc(sX.h, 16)
                        sX.n += 16

            B_("sync", _kvload)
            kv_loaded = sX.n
            if GLOB.get("tap_kg") and g == 0:
                def _dbgkg(gg):
                    gg.wait_ge(sX.h, kv_loaded)
                    for cx in range(8):
                        gg.dma_start(out=y_out[cx],
                                     in_=arena[:, AKG + cx * 1024:AKG + (cx + 1) * 1024]
                                     ).then_inc(sX.h, 16)
                        sX.n += 16
                    for cx in range(8):
                        gg.dma_start(out=y_out[8 + cx],
                                     in_=arena[:, AVG + cx * 1024:AVG + (cx + 1) * 1024]
                                     ).then_inc(sX.h, 16)
                        sX.n += 16
                B_("gpsimd", _dbgkg)
                return sX.n

            def _kv_t(t, kv_loaded=kv_loaded):
                t.wait_ge(sX.h, kv_loaded)

            B_("tensor", _kv_t)

            for hh in range(4):
                h = 4 * g + hh

                def _hprep(v, h=h, kv_loaded=kv_loaded):
                    v.wait_ge(sX.h, kv_loaded)
                    v.tensor_scalar_mul(scr32[:, SPQ:SPQ + 1024], qpos_sb[:, :], SLOPES[h])
                    v.tensor_scalar_mul(kcols_sb[:, 0:136], grids_sb[:, 0:136],
                                        SLOPES[h])
                    vsync(v)

                B_("vector", _hprep)

                for span in range(2):
                    cap = CAPA if span == 0 else CAPB
                    units = [("m", kt) for kt in range(cap)] + \
                            [("d", jj) for jj in range(4)]
                    n_units = len(units)
                    n_groups = n_units // 4
                    ctx_bank = span
                    sums_bank = 2 + span
                    sumsc = scr32[:, 4096:4864].bitcast(BF16)

                    pend = []

                    def emit_sumc(st_):
                        grp, gb, exp_s = st_["grp"], st_["gb"], st_["exp_s"]

                        def _sumc(v, grp=grp, gb=gb, exp_s=exp_s):
                            v.wait_ge(sS.h, exp_s)
                            if grp[0][0] == "d":
                                for ui, (_, jj) in enumerate(grp):
                                    v.tensor_tensor(
                                        scrbf[:, gb * 2048 + ui * 512:
                                              gb * 2048 + (ui + 1) * 512],
                                        scrbf[:, gb * 2048 + ui * 512:
                                              gb * 2048 + (ui + 1) * 512],
                                        masks_sb[:, jj * 512:(jj + 1) * 512],
                                        OP.mult)
                            v.tensor_tensor(sumsc[:, 0:512],
                                            scrbf[:, gb * 2048:gb * 2048 + 512],
                                            scrbf[:, gb * 2048 + 512:gb * 2048 + 1024],
                                            OP.add)
                            v.tensor_tensor(sumsc[:, 512:1024],
                                            scrbf[:, gb * 2048 + 1024:gb * 2048 + 1536],
                                            scrbf[:, gb * 2048 + 1536:gb * 2048 + 2048],
                                            OP.add)
                            inst = v.tensor_tensor(sumsc[:, 1024:1536],
                                                   sumsc[:, 0:512],
                                                   sumsc[:, 512:1024], OP.add)
                            inst.then_inc(sV.h, 1)
                            sV.n += 1

                        B_("vector", _sumc)
                        st_["sumc_v"] = sV.n

                    def emit_pvt(st_):
                        grp, gb, gi = st_["grp"], st_["gb"], st_["gi"]
                        exp_s, sumc_v = st_["exp_s"], st_["sumc_v"]

                        def _pvt(t, grp=grp, gb=gb, gi=gi, g=g, span=span,
                                 exp_s=exp_s, sumc_v=sumc_v):
                            t.wait_ge(sS.h, exp_s)
                            if grp[0][0] == "d":
                                t.wait_ge(sV.h, sumc_v)
                            if gi == 0:
                                if ctx_free[ctx_bank]:
                                    t.wait_ge(sV.h, ctx_free[ctx_bank])
                                if sums_free[span]:
                                    t.wait_ge(sV.h, sums_free[span])
                            for ui, (kind, idx) in enumerate(grp):
                                if kind == "m":
                                    vt = arena[:, AVG + idx * 128:
                                               AVG + (idx + 1) * 128]
                                else:
                                    base = AVO + (span * 4 + idx) * 512 + g * 128
                                    vt = arena[:, base:base + 128]
                                t.matmul(P[ctx_bank][:, :], vt,
                                         scrbf[:, gb * 2048 + ui * 512:
                                               gb * 2048 + (ui + 1) * 512],
                                         start=(gi == 0 and ui == 0),
                                         stop=(gi == n_groups - 1 and ui == 3))
                            t.wait_ge(sV.h, sumc_v)
                            inst = t.matmul(
                                P[sums_bank][0:1, :], onescb_sb[:, 0:1],
                                sumsc[:, 1024:1536],
                                start=(gi == 0), stop=(gi == n_groups - 1))
                            inst.then_inc(sT.h, 1)
                            sT.n += 1

                        B_("tensor", _pvt)
                        pt_free[gb] = sT.n

                    for gi in range(n_groups):
                        grp = units[4 * gi: 4 * gi + 4]
                        gb = gi % 2
                        s_mm = []

                        def _st_mm(t, grp=grp, h=h, span=span, g=g, gi=gi):
                            for ui, (kind, idx) in enumerate(grp):
                                if ps_free[ui]:
                                    t.wait_ge(sV.h, ps_free[ui])
                                if kind == "m":
                                    lhs = arena[:, AKG + idx * 128:
                                                AKG + (idx + 1) * 128]
                                else:
                                    base = AKO + g * 1024 + (span * 4 + idx) * 128
                                    lhs = arena[:, base:base + 128]
                                inst = t.matmul(
                                    P[4 + ui][:, :], lhs,
                                    arena[:, AQ + h * TC + span * TU:
                                          AQ + h * TC + (span + 1) * TU],
                                    start=True, stop=True)
                                inst.then_inc(sT.h, 1)
                                sT.n += 1
                                s_mm.append(sT.n)

                        B_("tensor", _st_mm)
                        if gi == 0:
                            flush_se()
                        if len(pend) >= 2:
                            st0 = pend.pop(0)
                            emit_sumc(st0)
                            emit_pvt(st0)

                        def _st_ev(v, grp=grp, span=span, gb=gb, s_mm=s_mm):
                            for ui, (kind, idx) in enumerate(grp):
                                v.wait_ge(sT.h, s_mm[ui])
                                if ui == 0 and sp_free[gb]:
                                    v.wait_ge(sS.h, sp_free[gb])
                                if kind == "m":
                                    kcol = kcols_sb[:, span * 64 + idx:
                                                    span * 64 + idx + 1]
                                else:
                                    kcol = kcols_sb[:, 128 + span * 4 + idx:
                                                    128 + span * 4 + idx + 1]
                                inst = v.scalar_tensor_tensor(
                                    o32[:, gb * 2048 + ui * 512:
                                        gb * 2048 + (ui + 1) * 512],
                                    P[4 + ui][:, :], kcol,
                                    scr32[:, SPQ + span * TU:SPQ + (span + 1) * TU],
                                    OP.add, OP.subtract)
                                inst.then_inc(sV.h, 1)
                                sV.n += 1
                                ps_free[ui] = sV.n

                        B_("vector", _st_ev)
                        ev_v = sV.n

                        def _sc(sc, gb=gb, ev_v=ev_v):
                            sc.wait_ge(sV.h, ev_v)
                            if pt_free[gb]:
                                sc.wait_ge(sT.h, pt_free[gb])
                            sc.activation(o32[:, gb * 2048:(gb + 1) * 2048],
                                          o32[:, gb * 2048:(gb + 1) * 2048],
                                          AF.Tanh, scale=A_SCALE)
                            inst = sc.activation(
                                scrbf[:, gb * 2048:(gb + 1) * 2048],
                                o32[:, gb * 2048:(gb + 1) * 2048], AF.Exp,
                                bias=neg30_sb[:, 0:1], scale=30.0)
                            inst.then_inc(sS.h, 1)
                            sS.n += 1
                            sp_free[gb] = sS.n

                        B_("scalar", _sc)
                        pend.append({"grp": grp, "gb": gb, "gi": gi,
                                     "exp_s": sS.n})
                    while pend:
                        st0 = pend.pop(0)
                        emit_sumc(st0)
                        emit_pvt(st0)
                    span_t = sT.n

                    def _ctx_v1(v, span_t=span_t, sums_bank=sums_bank,
                                prev_rb=last_rb_t[0]):
                        v.wait_ge(sT.h, span_t)
                        if prev_rb:
                            v.wait_ge(sT.h, prev_rb)
                        inst = v.reciprocal(scr32[0:1, 3584:4096],
                                            P[sums_bank][0:1, :])
                        inst.then_inc(sV.h, 1)
                        sV.n += 1

                    B_("vector", _ctx_v1)
                    pend_se.append((h, span, sV.n, ctx_bank, sums_bank))
            kg_free = sT.n
        flush_se()
        attn_done = sV.n
        attn_scrbf_free = sT.n
        if GLOB.get("tap_ctx"):
            def _dbgc2(g):
                g.wait_ge(sV.h, attn_done)
                for hx in range(16):
                    g.dma_start(out=y_out[hx],
                                in_=arena[:, AQ + hx * TC:AQ + (hx + 1) * TC]
                                ).then_inc(sX.h, 16)
                    sX.n += 16
            B_("gpsimd", _dbgc2)
            return sX.n

        wo_done = {}
        wo_free = {0: sT.n, 1: sT.n}
        wo_evac = [sV.n]
        oout_done = [0]
        for ocg in range(4):
            def _w_xo(s, ocg=ocg):
                if oout_done[0]:
                    s.wait_ge(sX.h, oout_done[0])
                else:
                    s.wait_ge(sV.h, attn_done)
                s.dma_start(
                    out=o32[:, :].rearrange("p (i t) -> p i t", i=4),
                    in_=src.rearrange("c p t -> p c t")[:, ocg * 4:(ocg + 1) * 4, :]
                ).then_inc(sX.h, 16)
                sX.n += 16

            B_("sync", _w_xo)
            oxo_done = sX.n

            wo_guard = wo_evac[0]
            for hc in range(CC):
                wpar = hc % 2

                def _w_gw(g, ocg=ocg, hc=hc, wpar=wpar):
                    if wo_free[wpar]:
                        g.wait_ge(sT.h, wo_free[wpar])
                    g.dma_start(out=wqsb[:, wpar * 2048:wpar * 2048 + 512],
                                in_=wo_in[hc, ocg]).then_inc(sW.h, 16)
                    sW.n += 16
                    wo_done[(ocg, hc)] = sW.n

                B_("gpsimd", _w_gw)

                def _w_t(t, ocg=ocg, hc=hc, wpar=wpar, wo_guard=wo_guard):
                    t.wait_ge(sW.h, wo_done[(ocg, hc)])
                    if hc == 0:
                        t.wait_ge(sV.h, wo_guard)
                    inst = None
                    for oc4 in range(4):
                        for u in range(NU):
                            inst = t.matmul(
                                P[oc4 * 2 + u][:, :],
                                wqsb[:, wpar * 2048 + oc4 * 128:
                                     wpar * 2048 + (oc4 + 1) * 128],
                                arena[:, AQ + hc * TC + u * TU:
                                      AQ + hc * TC + (u + 1) * TU],
                                start=(hc == 0), stop=(hc == CC - 1))
                    inst.then_inc(sT.h, 1)
                    sT.n += 1
                    wo_free[wpar] = sT.n

                B_("tensor", _w_t)
            w_t = sT.n

            def _w_v(v, ocg=ocg, w_t=w_t, oxo_done=oxo_done):
                v.wait_ge(sT.h, w_t)
                v.wait_ge(sX.h, oxo_done)
                inst = None
                for oc4 in range(4):
                    for u in range(NU):
                        inst = v.scalar_tensor_tensor(
                            o32[:, oc4 * 1024 + u * 512: oc4 * 1024 + (u + 1) * 512],
                            P[oc4 * 2 + u][:, :],
                            bo_sb[:, ocg * 4 + oc4: ocg * 4 + oc4 + 1],
                            o32[:, oc4 * 1024 + u * 512: oc4 * 1024 + (u + 1) * 512],
                            OP.add, OP.add)
                inst.then_inc(sV.h, 1)
                sV.n += 1
                wo_evac[0] = sV.n

            B_("vector", _w_v)

            def _w_out(s, ocg=ocg):
                s.wait_ge(sV.h, wo_evac[0])
                s.dma_start(
                    out=dst.rearrange("c p t -> p c t")[:, ocg * 4:(ocg + 1) * 4, :],
                    in_=o32[:, :].rearrange("p (i t) -> p i t", i=4)
                ).then_inc(sX.h, 16)
                sX.n += 16
                oout_done[0] = sX.n

                for i in range(4):
                    ready[ocg * 4 + i] = sX.n

            B_("sync", _w_out)
        return {"last": sX.n, "ready": ready, "xhat_free": attn_xhat_free,
                "scrbf_free": attn_scrbf_free}

    if tap == "comb":
        GLOB["gates_only"] = True
        emit_ffn(x0_in, y_out, first_call=True)
    elif tap == "comb2":
        GLOB["gates_only"] = True
        GLOB["go2"] = True
        r1 = emit_ffn(x0_in, x1_d, first_call=True)
        r2 = emit_attn(x1_d, x2_d, prev=r1)
        emit_ffn(x2_d, y_out, first_call=False, prev=r2)
    elif tap in ("qrope", "ctx", "kg", "kown"):
        GLOB["tap_qrope"] = tap == "qrope"
        GLOB["tap_ctx"] = tap == "ctx"
        GLOB["tap_kg"] = tap == "kg"
        GLOB["tap_kown"] = tap == "kown"
        r1 = emit_ffn(x0_in, x1_d, first_call=True)
        emit_attn(x1_d, y_out, prev=r1)
    elif tap == "ln1":
        ln_done = emit_ln(x0_in)
        def _dbg(g):
            g.wait_ge(sV.h, sV.n)
            for cc in range(CC):
                g.dma_start(out=y_out[cc],
                            in_=xhat[:, cc * TC:(cc + 1) * TC]).then_inc(sX.h, 16)
                sX.n += 16
        B_("gpsimd", _dbg)
    elif tap == "x1":
        emit_ffn(x0_in, y_out, first_call=True)
    elif tap == "x2":
        r1 = emit_ffn(x0_in, x1_d, first_call=True)
        emit_attn(x1_d, y_out, prev=r1)
    else:
        r1 = emit_ffn(x0_in, x1_d, first_call=True)
        r2 = emit_attn(x1_d, x2_d, prev=r1)
        emit_ffn(x2_d, y_out, first_call=False, prev=r2)

    final = sX.n

    def _fin(s):
        s.wait_ge(sX.h, final)

    B_("sync", _fin)
    ctx.close()
    return nc



def _prep_shared(I):
    f32 = np.float32
    Wq = np.asarray(I["L0_attn_Wq"], f32)
    Wk = np.asarray(I["L0_attn_Wk"], f32)
    Wv = np.asarray(I["L0_attn_Wv"], f32)
    Wo = np.asarray(I["L0_attn_Wo"], f32)
    bo = np.asarray(I["L0_attn_bo"], f32)
    gate_w = np.asarray(I["L0_ffn_gate_w"], f32)
    se_w = np.asarray(I["L0_ffn_se_gate_w"], f32)
    w_gate = np.asarray(I["L0_ffn_w_gate"], f32)
    w_up = np.asarray(I["L0_ffn_w_up"], f32)
    w_down = np.asarray(I["L0_ffn_w_down"], f32)
    sh_gate = np.asarray(I["L0_ffn_sh_gate"], f32)
    sh_up = np.asarray(I["L0_ffn_sh_up"], f32)
    sh_down = np.asarray(I["L0_ffn_sh_down"], f32)

    out = {}
    out["wq"] = np.ascontiguousarray(
        Wq.reshape(16, 128, 16, 128).transpose(2, 1, 0, 3)).reshape(16, 128, 2048)
    out["wk"] = np.ascontiguousarray(
        Wk.reshape(16, 128, 4, 128).transpose(2, 1, 0, 3)).reshape(4, 128, 2048)
    out["wv"] = np.ascontiguousarray(Wv.reshape(16, 128, 512))
    out["wo"] = np.ascontiguousarray(Wo.reshape(16, 128, 4, 512).transpose(0, 2, 1, 3))
    out["bo"] = np.ascontiguousarray(bo.reshape(16, 128).T)
    gws = np.concatenate([gate_w, se_w], 0)
    out["gwse"] = np.ascontiguousarray(gws.T.reshape(16, 128, 9).transpose(1, 0, 2))
    wg_all = np.concatenate([w_gate, sh_gate.reshape(2, 1024, 2048)], 0)
    wu_all = np.concatenate([w_up, sh_up.reshape(2, 1024, 2048)], 0)
    wg_p = wg_all.reshape(10, 8, 128, 16, 128).transpose(0, 1, 4, 3, 2).reshape(
        10, 8, 128, 2048)
    wu_p = wu_all.reshape(10, 8, 128, 16, 128).transpose(0, 1, 4, 3, 2).reshape(
        10, 8, 128, 2048)
    out["wgu"] = np.ascontiguousarray(np.concatenate([wg_p, wu_p], axis=3))
    wd_all = np.concatenate(
        [w_down, np.stack([sh_down[:, :1024], sh_down[:, 1024:]], 0)], 0)
    wd_p = wd_all.reshape(10, 4, 512, 8, 128).transpose(1, 0, 3, 4, 2)
    out["wd"] = np.ascontiguousarray(
        wd_p.reshape(4, 10, 2, 4, 128, 512).transpose(0, 1, 2, 4, 3, 5).reshape(
            4, 10, 2, 128, 2048))
    masks = []
    for jj in range(4):
        parts = []
        for m in range(4):
            if m > jj:
                parts.append(np.ones((128, 128), f32))
            elif m == jj:
                parts.append(np.triu(np.ones((128, 128), f32)))
            else:
                parts.append(np.zeros((128, 128), f32))
        masks.append(np.concatenate(parts, axis=1))
    out["masks"] = np.stack(masks, 0).astype(ml_dtypes.bfloat16)
    out["ident"] = np.eye(128, dtype=f32)
    pt = np.zeros((128, 128), f32)
    for i in range(64):
        pt[2 * i + 1, 2 * i] = -1.0
        pt[2 * i, 2 * i + 1] = 1.0
    out["ptmat"] = pt
    esel = np.zeros((10, 1280), f32)
    for pe in range(10):
        esel[pe, pe * 128:(pe + 1) * 128] = 1.0
    out["esel"] = esel.astype(ml_dtypes.bfloat16)
    out["onesm"] = np.ones((1, 128), f32)
    out["onescb"] = np.ones((128, 1), f32).astype(ml_dtypes.bfloat16)
    out["neg30"] = np.full((128, 1), -30.0, f32)
    return out


def _prep_core(I, i):
    f32 = np.float32
    tiles = core_tiles(i)
    hsT = np.asarray(I["L0_attn_hidden_states"], f32)[0].T
    x0 = np.concatenate([hsT[:, 128 * g:128 * (g + 1)] for g in tiles], axis=1)
    out = {"x0": np.ascontiguousarray(x0).reshape(16, 128, 1024)}
    pos = np.concatenate([np.arange(128 * g, 128 * (g + 1), dtype=f32) for g in tiles])
    out["qpos"] = np.ascontiguousarray(np.broadcast_to(pos[None, :], (128, 1024)))
    p = np.arange(128, dtype=f32)[:, None]
    kt = np.arange(64, dtype=f32)[None, :]
    gA = np.where(kt < 4 * i, kt * 128 + p, BIG_NEG)
    out["gridA"] = np.ascontiguousarray(gA.astype(f32))
    gB = np.where(kt < 60 - 4 * i, kt * 128 + p, BIG_NEG)
    out["gridB"] = np.ascontiguousarray(gB.astype(f32))
    out["gridOA"] = np.ascontiguousarray(
        (np.array(tiles[:4], f32)[None, :] * 128 + p).astype(f32))
    out["gridOB"] = np.ascontiguousarray(
        (np.array(tiles[4:], f32)[None, :] * 128 + p).astype(f32))
    inv = 1.0 / 10000.0 ** (np.arange(0, 128, 2, dtype=np.float64) / 128)
    invr = np.repeat(inv, 2)
    fr = invr[:, None] * pos[None, :].astype(np.float64)
    out["cosl"] = np.ascontiguousarray(np.cos(fr).astype(f32))
    out["sinl"] = np.ascontiguousarray(np.sin(fr).astype(f32))
    return out


def kernel(**inputs):
    shared = _prep_shared(inputs)
    in_maps = []
    for i in range(NCORES):
        m = dict(shared)
        m.update(_prep_core(inputs, i))
        in_maps.append(m)
    nc = build_nc(TAP)
    res = run_bass_kernel_spmd(nc, in_maps, core_ids=list(range(NCORES)))
    out = np.zeros((8192, 2048), np.float32)
    for i in range(NCORES):
        y = np.asarray(res.results[i]["y"], np.float32).reshape(2048, 1024)
        for j, g in enumerate(core_tiles(i)):
            out[128 * g:128 * (g + 1), :] = y[:, 128 * j:128 * (j + 1)].T
    return out[None, :, :]


